# version 03
# baseline (speedup 1.0000x reference)
import math
from collections import defaultdict
from contextlib import ExitStack

import numpy as np
import ml_dtypes
import concourse.bass as bass
import concourse.mybir as mybir
from concourse.bass import IndirectOffsetOnAxis
from concourse.bass_utils import run_bass_kernel_spmd

F32 = mybir.dt.float32
BF16 = mybir.dt.bfloat16
I32 = mybir.dt.int32
AF = mybir.ActivationFunctionType
OP = mybir.AluOpType
BF = ml_dtypes.bfloat16

D = 2048
NCH = D // 128
T = 1024
S = 4096
H = 16
DH = 128
INTER = 8192
EPS_LN = 1e-5
EPS_RMS = 1e-6
ISQ = 1.0 / math.sqrt(DH)
TOPK = 1024
NEG = -30000.0


def to_fm(x_tm):
    t = x_tm.shape[0]
    return np.ascontiguousarray(
        x_tm.reshape(t, NCH, 128).transpose(2, 1, 0).reshape(128, NCH * t))


def from_fm(x_fm, t):
    return np.ascontiguousarray(
        x_fm.reshape(128, NCH, t).transpose(2, 1, 0).reshape(t, D))


def host_consts():
    c = {}
    c["ones_bf"] = np.ones((128, 128), dtype=BF)
    c["ones_f"] = np.ones((128, 128), dtype=np.float32)
    c["eye_bf"] = np.eye(128, dtype=BF)
    c["eye_f"] = np.eye(128, dtype=np.float32)
    p64 = np.zeros((128, 128), dtype=BF)
    for m in range(128):
        p64[(m + 64) % 128, m] = 1
    c["p64_bf"] = p64
    ind = np.zeros((128, 16 * 16), dtype=BF)
    for h in range(16):
        ind[:, h * 16 + h] = 1
    c["zind_bf"] = ind
    bsel = np.zeros((16, 16 * 128), dtype=BF)
    for h in range(16):
        bsel[h, h * 128:(h + 1) * 128] = 1
    c["bsel_bf"] = bsel
    io = np.zeros((128, 8), dtype=np.float32)
    for pp in range(128):
        for jj in range(8):
            io[pp, jj] = jj * 128 + pp
    c["iotasel_f"] = io
    sg = np.ones((128, 1), dtype=np.float32)
    sg[0:64] = -1.0
    c["sgn_f"] = sg
    dm = np.zeros((128, 4 * 512), dtype=BF)
    for tt in range(4):
        for pp in range(128):
            s0 = pp + 128 * tt
            if s0 < 512:
                dm[pp, 512 * tt + s0: 512 * (tt + 1)] = 1
    c["diag_bf"] = dm
    return c


CONSTS = host_consts()


class Prog:
    def __init__(self, name):
        self.name = name
        self.nc = bass.Bass("TRN2", target_bir_lowering=False, debug=False)
        self.st = ExitStack()
        self.semh = {}
        self.cnt = defaultdict(int)
        self.block = None
        self.consts = {}
        self.const_names = []

    def sem(self, name):
        if name not in self.semh:
            self.semh[name] = self.st.enter_context(self.nc.semaphore(name))
        return self.semh[name]

    def i(self, instr, name, n=1):
        self.cnt[name] += n
        instr.then_inc(self.sem(name), n)
        return self.cnt[name]

    def c(self, name):
        return self.cnt[name]

    def sb(self, name, shape, dtype):
        return self.st.enter_context(self.nc.sbuf_tensor(name, list(shape), dtype))

    def psum(self, name, shape, dtype):
        return self.st.enter_context(self.nc.psum_tensor(name, list(shape), dtype))

    def din(self, name, shape, dtype):
        return self.nc.dram_tensor(name, list(shape), dtype, kind="ExternalInput").ap()

    def dout(self, name, shape, dtype):
        return self.nc.dram_tensor(name, list(shape), dtype, kind="ExternalOutput").ap()

    def dint(self, name, shape, dtype):
        return self.nc.dram_tensor(name, list(shape), dtype).ap()

    def begin(self):
        self.block = self.st.enter_context(self.nc.Block())
        self.psf = self.psum("psf", [128, 8 * 512], F32)
        self.bounce = self.sb("bounce", [128, 1024], F32)
        self.wt = self.sb("wt", [128, 2048], BF16)
        return self.block

    def use_consts(self, names):
        for n in names:
            arr = CONSTS[n]
            dt = {np.dtype(np.float32): F32, np.dtype(BF): BF16}[arr.dtype]
            self.consts[n] = (self.din("c_" + n, arr.shape, dt),
                              self.sb("cs_" + n, arr.shape, dt))
        self.const_names = list(names)

    def use_fconsts(self, values):
        self.fconst_vals = list(values)
        arr = np.tile(np.array(values, np.float32), (128, 1))
        CONSTS["fc_" + self.name] = arr
        dt = F32
        self.consts["fc_" + self.name] = (
            self.din("c_fc_" + self.name, arr.shape, dt),
            self.sb("cs_fc_" + self.name, arr.shape, dt))
        self.const_names.append("fc_" + self.name)
        sbuf = self.consts["fc_" + self.name][1]
        for j, v in enumerate(values):
            self.nc.const_aps.register(F32, float(v), sbuf[:, j:j + 1])

    def load_consts(self):
        p = self

        @p.block.sync
        def _(e):
            for n in p.const_names:
                dram, sbuf = p.consts[n]
                p.i(e.dma_start(out=sbuf[:, :], in_=dram[:, :]), "cl", 16)

    def cs(self, name):
        return self.consts[name][1]

    def input_const_maps(self):
        return {("c_" + n): CONSTS[n] for n in self.const_names}

    def finish(self):
        self.st.close()
        return self.nc


def bk(p, b, n=512, off=0):
    return p.psf[:, 512 * b + off: 512 * b + off + n]


def const_maps(names):
    return {("c_" + n): CONSTS[n] for n in names}


def emit_ln_fm(p, x_dram, xb, lnx, sq, stat, eps=EPS_LN, tok=T, stream=True):
    ones = p.cs("ones_bf")
    tc2 = tok // 2
    assert tc2 <= 512
    base_vv = p.c("vv")
    base_sc = p.c("sc")
    base_xl = [p.c("xl0"), p.c("xl1")]
    base_mm = p.c("mm")

    if stream:
        @p.block.sync
        def _(e):
            e.wait_ge(p.sem("cl"), p.c("cl"))
            for k in range(32):
                tc, ch = divmod(k, NCH)
                if k >= 2:
                    e.wait_ge(p.sem("vv"), base_vv + k - 1)
                p.i(e.dma_start(
                    out=p.bounce[:, (k % 2) * 512:(k % 2) * 512 + tc2],
                    in_=x_dram[:, ch * tok + tc * tc2: ch * tok + (tc + 1) * tc2]),
                    "xl%d" % (k % 2), 16)

        @p.block.vector
        def _(e):
            for k in range(32):
                tc, ch = divmod(k, NCH)
                e.wait_ge(p.sem("xl%d" % (k % 2)), base_xl[k % 2] + 16 * (k // 2 + 1))
                p.i(e.tensor_copy(
                    xb[:, ch * tok + tc * tc2: ch * tok + (tc + 1) * tc2],
                    p.bounce[:, (k % 2) * 512:(k % 2) * 512 + tc2]), "vv", 1)
    else:
        @p.block.vector
        def _(e):
            for k in range(32):
                tc, ch = divmod(k, NCH)
                p.i(e.tensor_copy(
                    xb[:, ch * tok + tc * tc2: ch * tok + (tc + 1) * tc2],
                    x_dram[:, ch * tok + tc * tc2: ch * tok + (tc + 1) * tc2]),
                    "vv", 1)
    vv_cast0 = base_vv

    @p.block.scalar
    def _(e):
        for k in range(32):
            tc, ch = divmod(k, NCH)
            if k >= 2:
                e.wait_ge(p.sem("mm"), base_mm + k - 1)
            e.wait_ge(p.sem("vv"), vv_cast0 + k + 1)
            p.i(e.activation(sq[:, (k % 2) * 512:(k % 2) * 512 + tc2],
                             xb[:, ch * tok + tc * tc2: ch * tok + (tc + 1) * tc2],
                             AF.Square), "sc", 1)

    @p.block.tensor
    def _(e):
        for k in range(32):
            tc, ch = divmod(k, NCH)
            e.wait_ge(p.sem("sc"), base_sc + k + 1)
            e.matmul(bk(p, 0 + tc, tc2), ones[:, :],
                     xb[:, ch * tok + tc * tc2: ch * tok + (tc + 1) * tc2],
                     start=(ch == 0), stop=(ch == NCH - 1))
            mm = e.matmul(bk(p, 2 + tc, tc2), ones[:, :],
                          sq[:, (k % 2) * 512:(k % 2) * 512 + tc2],
                          start=(ch == 0), stop=(ch == NCH - 1))
            p.i(mm, "mm", 1)
    mm_done = p.c("mm")

    @p.block.scalar
    def _(e):
        e.wait_ge(p.sem("mm"), mm_done)
        for tc in range(2):
            p.i(e.activation(stat[:, tc * 512:tc * 512 + tc2], bk(p, tc, tc2),
                             AF.Copy, scale=1.0 / D), "sc", 1)
        e.wait_ge(p.sem("sc"), p.c("sc"))
        for tc in range(2):
            p.i(e.activation(p.bounce[:, tc * 512:tc * 512 + tc2],
                             stat[:, tc * 512:tc * 512 + tc2],
                             AF.Square), "sc", 1)
    sc_msq = p.c("sc")

    @p.block.vector
    def _(e):
        e.wait_ge(p.sem("sc"), sc_msq)
        for tc in range(2):
            p.i(e.scalar_tensor_tensor(
                stat[:, 1024 + tc * 512:1024 + tc * 512 + tc2],
                bk(p, 2 + tc, tc2), 1.0 / D, p.bounce[:, tc * 512:tc * 512 + tc2],
                OP.mult, OP.subtract), "vv", 1)
    vv_var = p.c("vv")

    @p.block.scalar
    def _(e):
        e.wait_ge(p.sem("vv"), vv_var)
        for tc in range(2):
            p.i(e.activation(stat[:, 1024 + tc * 512:1024 + tc * 512 + tc2],
                             stat[:, 1024 + tc * 512:1024 + tc * 512 + tc2],
                             AF.Sqrt, bias=eps), "sc", 1)
    sc_sqrt = p.c("sc")

    @p.block.vector
    def _(e):
        e.wait_ge(p.sem("sc"), sc_sqrt)
        for tc in range(2):
            p.i(e.reciprocal(stat[:, 1024 + tc * 512:1024 + tc * 512 + tc2],
                             stat[:, 1024 + tc * 512:1024 + tc * 512 + tc2]),
                "vv", 1)
        e.wait_ge(p.sem("vv"), p.c("vv"))
        for tc in range(2):
            for ch in range(NCH):
                xs = xb[:, ch * tok + tc * tc2: ch * tok + (tc + 1) * tc2]
                dst = lnx[:, ch * tok + tc * tc2: ch * tok + (tc + 1) * tc2]
                p.i(e.tensor_sub(bk(p, 6, tc2), xs,
                                 stat[:, tc * 512:tc * 512 + tc2]), "vv", 1)
                e.wait_ge(p.sem("vv"), p.c("vv"))
                p.i(e.tensor_mul(dst, bk(p, 6, tc2),
                                 stat[:, 1024 + tc * 512:1024 + tc * 512 + tc2]),
                    "vv", 1)
                e.wait_ge(p.sem("vv"), p.c("vv"))
    return p.c("vv")



def emit_proj_chstream(p, n_og, nch, tile_load, tile_mms, evacs, psum_lag=2,
                       rdy=(), queue="sync"):
    base_wl = [p.c("wl0"), p.c("wl1")]
    base_mp = p.c("mp")
    base_ev = p.c("ev")

    def emit_loads(e):
        for og in range(n_og):
            for ch in range(nch):
                k = og * nch + ch
                e.wait_ge(p.sem("mp"), base_mp + max(0, k - 1))
                p.i(tile_load(e, og, ch, k % 2), "wl%d" % (k % 2), 16)

    if queue == "sync":
        @p.block.sync
        def _(e):
            emit_loads(e)
    else:
        @p.block.gpsimd
        def _(e):
            emit_loads(e)

    @p.block.tensor
    def _(e):
        for og in range(n_og):
            for ch in range(nch):
                k = og * nch + ch
                e.wait_ge(p.sem("wl%d" % (k % 2)),
                          base_wl[k % 2] + 16 * (k // 2 + 1))
                if k == 0:
                    for (sn, sv) in rdy:
                        e.wait_ge(p.sem(sn), sv)
                if ch == 0:
                    e.wait_ge(p.sem("ev"), base_ev + max(0, og - psum_lag + 1))
                last = tile_mms(e, og, ch, k % 2, ch == 0, ch == nch - 1)
                p.i(last, "mp", 1)

    for og in range(n_og):
        evacs[og](base_mp + (og + 1) * nch)
    return p.c("ev")



def emit_ffn(p, x_in, x1_out, wgu_bf, wdn_bf, XB, LNB, HH, SQ, STT, GB, XIO,
             vv_lnx):
    def h_ap(mt, lo, ln):
        if mt < 16:
            return XB[:, mt * T + lo: mt * T + lo + ln]
        return HH[:, (mt - 16) * T + lo: (mt - 16) * T + lo + ln]

    def gu_load(e, og, ch, par):
        e.wait_ge(p.sem("wcgu"), 512 * (og // 16 + 1))
        return e.dma_start(
            out=p.wt[:, par * 1024: par * 1024 + 256],
            in_=wgu_bf[og * 256:(og + 1) * 256, ch * 128:(ch + 1) * 128],
            transpose=True)

    def gu_mms(e, og, ch, par, first, last):
        b = (og % 2) * 4
        gt = p.wt[:, par * 1024: par * 1024 + 128]
        ut = p.wt[:, par * 1024 + 128: par * 1024 + 256]
        lnc = LNB[:, ch * T: (ch + 1) * T]
        for tc in range(2):
            e.matmul(bk(p, b + tc), gt, lnc[:, tc * 512:(tc + 1) * 512],
                     start=first, stop=last)
            mm = e.matmul(bk(p, b + 2 + tc), ut, lnc[:, tc * 512:(tc + 1) * 512],
                          start=first, stop=last)
        return mm

    def gu_evac(og):
        def run(mp_target):
            b = (og % 2) * 4

            @p.block.scalar
            def _(e):
                e.wait_ge(p.sem("mp"), mp_target)
                e.wait_ge(p.sem("ev"), p.c("ev"))
                p.i(e.activation(GB[:, :], p.psf[:, b * 512:(b + 2) * 512],
                                 AF.Silu), "sc", 1)
            sc_t = p.c("sc")

            @p.block.vector
            def _(e):
                e.wait_ge(p.sem("sc"), sc_t)
                p.i(e.tensor_mul(h_ap(og, 0, T), GB[:, :],
                                 p.psf[:, (b + 2) * 512:(b + 4) * 512]), "ev", 1)
        return run

    emit_proj_chstream(p, 64, NCH, gu_load, gu_mms, [gu_evac(g) for g in range(64)],
                       psum_lag=2, rdy=[("vv", vv_lnx)])
    ev_gu = p.c("ev")

    def dn_load(e, og, ch, par):
        e.wait_ge(p.sem("wcdn"), 16 * min(2, (og * 256 + 256 + 1023) // 1024))
        return e.dma_start(
            out=p.wt[:, par * 1024: par * 1024 + 256],
            in_=wdn_bf[og * 256:(og + 1) * 256, ch * 128:(ch + 1) * 128],
            transpose=True)

    def dn_mms(e, og, ch, par, first, last):
        b = (og % 2) * 4
        for dt in range(2):
            wti = p.wt[:, par * 1024 + dt * 128: par * 1024 + dt * 128 + 128]
            for tc in range(2):
                mm = e.matmul(bk(p, b + dt * 2 + tc), wti,
                              h_ap(ch, tc * 512, 512),
                              start=first, stop=last)
        return mm

    def dn_evac(og):
        def run(mp_target):
            for dt in range(2):
                d_idx = og * 2 + dt
                b = (og % 2) * 4 + dt * 2
                xl0 = p.c("xl")

                @p.block.gpsimd
                def _(e):
                    e.wait_ge(p.sem("vv"), p.c("vv"))
                    p.i(e.dma_start(
                        out=XIO[:, (d_idx % 2) * 1024:(d_idx % 2) * 1024 + 1024],
                        in_=x_in[:, d_idx * T:(d_idx + 1) * T]), "xg%d" % (d_idx % 2), 16)

                @p.block.vector
                def _(e):
                    e.wait_ge(p.sem("xg%d" % (d_idx % 2)), p.c("xg%d" % (d_idx % 2)))
                    e.wait_ge(p.sem("mp"), mp_target)
                    e.wait_ge(p.sem("st%d" % (d_idx % 2)), p.c("st%d" % (d_idx % 2)))
                    p.i(e.scalar_tensor_tensor(
                        XIO[:, 2048 + (d_idx % 2) * 1024: 2048 + (d_idx % 2) * 1024 + 1024],
                        p.psf[:, b * 512:(b + 2) * 512], 0.5,
                        XIO[:, (d_idx % 2) * 1024:(d_idx % 2) * 1024 + 1024],
                        OP.mult, OP.add), "vv", 1)
                    if dt == 1:
                        p.i(e.engine_nop(), "ev", 1)
                vv_t = p.c("vv")

                @p.block.gpsimd
                def _(e):
                    e.wait_ge(p.sem("vv"), vv_t)
                    p.i(e.dma_start(
                        out=x1_out[:, d_idx * T:(d_idx + 1) * T],
                        in_=XIO[:, 2048 + (d_idx % 2) * 1024: 2048 + (d_idx % 2) * 1024 + 1024]),
                        "st%d" % (d_idx % 2), 16)
        return run

    emit_proj_chstream(p, 8, 64, dn_load, dn_mms, [dn_evac(g) for g in range(8)],
                       psum_lag=2, rdy=[("ev", ev_gu)])
    return (p.c("st0"), p.c("st1"))


def build_L1():
    p = Prog("L1")
    x_in = p.din("x_fm", [128, NCH * T], F32)
    wgu = p.din("wgu", [2 * INTER, D], F32)
    wdn = p.din("wdn", [D, INTER], F32)
    w_qkv = {n: p.din(n, [D, D], F32) for n in ("wk", "wq", "wv")}

    x1_out = p.dout("x1_fm", [128, NCH * T], F32)
    q_out = p.dout("q_fm", [128, H * T], BF16)
    k_out = p.dout("k_fm", [128, H * T], BF16)
    v_out = p.dout("v_tm", [T, D], BF16)
    nrm_out = p.dout("qknorm", [1, 32], F32)

    wgu_bf = p.dint("wgu_bf", [2 * INTER, D], BF16)
    wdn_bf = p.dint("wdn_bf", [D, INTER], BF16)
    w_bf = {n: p.dint(n + "_bf", [D, D], BF16) for n in ("wk", "wq", "wv")}

    XB = p.sb("XB", [128, 16384], BF16)
    LNB = p.sb("LNB", [128, 16384], BF16)
    HH = p.sb("HH", [128, 49152], BF16)
    SQ = p.sb("SQ", [128, 1024], BF16)
    STT = p.sb("STT", [128, 2048], F32)
    GB = p.sb("GB", [128, 1024], BF16)
    XIO = p.sb("XIO", [128, 4096], F32)
    NRM = p.sb("NRM", [128, 16], F32)
    NO = p.sb("NO", [1, 32], F32)

    p.use_consts(["ones_bf"])
    p.use_fconsts([EPS_LN])
    p.begin()
    p.load_consts()

    @p.block.gpsimd
    def _(e):
        for blk in range(4):
            if blk:
                e.wait_ge(p.sem("wcgu"), p.c("wcgu"))
            for g in range(blk * 16, blk * 16 + 16):
                p.i(e.dma_start(out=wgu_bf[g * 256: g * 256 + 128, :],
                                in_=wgu[g * 128: (g + 1) * 128, :]), "wcgu", 16)
                p.i(e.dma_start(out=wgu_bf[g * 256 + 128: g * 256 + 256, :],
                                in_=wgu[INTER + g * 128: INTER + (g + 1) * 128, :]),
                    "wcgu", 16)
        e.wait_ge(p.sem("wcgu"), p.c("wcgu"))
        for b in range(2):
            if b:
                e.wait_ge(p.sem("wcdn"), p.c("wcdn"))
            p.i(e.dma_start(out=wdn_bf[b * 1024:(b + 1) * 1024, :],
                            in_=wdn[b * 1024:(b + 1) * 1024, :]), "wcdn", 16)
        e.wait_ge(p.sem("wcdn"), p.c("wcdn"))
        for n in ("wk", "wq", "wv"):
            p.i(e.dma_start(out=w_bf[n][:, :], in_=w_qkv[n][:, :]), "wc" + n, 16)

    vv_ln = emit_ln_fm(p, x_in, XB, LNB, SQ, STT)

    st_x1 = emit_ffn(p, x_in, x1_out, wgu_bf, wdn_bf, XB, LNB, HH, SQ, STT,
                     GB, XIO, vv_ln)

    @p.block.sync
    def _(e):
        e.wait_ge(p.sem("st0"), st_x1[0])
        e.wait_ge(p.sem("st1"), st_x1[1])
    vv_ln1 = emit_ln_fm(p, x1_out, XB, LNB, SQ, STT)

    KB = HH[:, 0:16384]
    QB = HH[:, 16384:32768]
    VB = HH[:, 32768:49152]

    def qk_spec(wbf, dstbuf):
        def load(e, og, ch, par):
            e.wait_ge(p.sem("wc" + wbf_name[id(wbf)]), 16)
            return e.dma_start(
                out=p.wt[:, par * 1024: par * 1024 + 256],
                in_=wbf[og * 256:(og + 1) * 256, ch * 128:(ch + 1) * 128],
                transpose=True)

        def mms(e, og, ch, par, first, last):
            for hd in range(2):
                wti = p.wt[:, par * 1024 + hd * 128: par * 1024 + hd * 128 + 128]
                for tc in range(2):
                    mm = e.matmul(bk(p, (og % 2) * 4 + hd * 2 + tc), wti,
                                  LNB[:, ch * T + tc * 512: ch * T + tc * 512 + 512],
                                  start=first, stop=last)
            return mm

        def evac(og):
            def run(mp_target):
                @p.block.vector
                def _(e):
                    e.wait_ge(p.sem("mp"), mp_target)
                    b = (og % 2) * 4
                    for hd in range(2):
                        h = og * 2 + hd
                        p.i(e.tensor_copy(
                            dstbuf[:, h * T:(h + 1) * T],
                            p.psf[:, (b + hd * 2) * 512:(b + hd * 2 + 2) * 512]),
                            "ev", 1 if hd == 1 else 0)
            return run
        return load, mms, [evac(g) for g in range(8)]

    wbf_name = {id(w_bf["wk"]): "wk", id(w_bf["wq"]): "wq", id(w_bf["wv"]): "wv"}
    lk, mk, ek = qk_spec(w_bf["wk"], KB)
    emit_proj_chstream(p, 8, NCH, lk, mk, ek, psum_lag=2, rdy=[("vv", vv_ln1)])
    lq, mq, eq = qk_spec(w_bf["wq"], QB)
    emit_proj_chstream(p, 8, NCH, lq, mq, eq, psum_lag=2)

    def v_load(e, og, ch, par):
        e.wait_ge(p.sem("wcwv"), 16)
        return e.dma_start(
            out=p.wt[:, par * 1024: par * 1024 + 256],
            in_=w_bf["wv"][og * 256:(og + 1) * 256, ch * 128:(ch + 1) * 128],
            transpose=True)

    def v_mms(e, og, ch, par, first, last):
        wti = p.wt[:, par * 1024: par * 1024 + 256]
        for tt in range(8):
            mm = e.matmul(p.psf[:, tt * 512: tt * 512 + 256],
                          LNB[:, ch * T + tt * 128: ch * T + tt * 128 + 128], wti,
                          start=first, stop=last)
        return mm

    def v_evac(og):
        def run(mp_target):
            @p.block.vector
            def _(e):
                e.wait_ge(p.sem("mp"), mp_target)
                for tt in range(8):
                    p.i(e.tensor_copy(
                        VB[:, tt * 2048 + og * 256: tt * 2048 + og * 256 + 256],
                        p.psf[:, tt * 512: tt * 512 + 256]),
                        "ev", 1 if tt == 7 else 0)
        return run

    emit_proj_chstream(p, 8, NCH, v_load, v_mms, [v_evac(g) for g in range(8)],
                       psum_lag=1)
    ev_qkv = p.c("ev")

    ones = p.cs("ones_bf")
    NRM2 = p.sb("NRM2", [128, 2], F32)
    sc_base = p.c("sc")
    mm_base = p.c("mm")
    vv_base = p.c("vv")

    @p.block.scalar
    def _(e):
        e.wait_ge(p.sem("ev"), ev_qkv)
        k = 0
        for row, buf in ((0, QB), (1, KB)):
            for h in range(H):
                k += 1
                if k >= 2:
                    e.wait_ge(p.sem("mm"), mm_base + k - 1)
                p.i(e.activation(SQ[:, :], buf[:, h * T:(h + 1) * T],
                                 AF.Square), "sc", 1)

    @p.block.tensor
    def _(e):
        k = 0
        for row in range(2):
            for h in range(H):
                k += 1
                e.wait_ge(p.sem("sc"), sc_base + k)
                if k >= 2:
                    e.wait_ge(p.sem("vv"), vv_base + 4 * (k - 1))
                e.matmul(bk(p, 5), ones[:, :], SQ[:, 0:512], start=True, stop=True)
                mm = e.matmul(bk(p, 6), ones[:, :], SQ[:, 512:1024],
                              start=True, stop=True)
                p.i(mm, "mm", 1)

    @p.block.vector
    def _(e):
        k = 0
        for row in range(2):
            for h in range(H):
                k += 1
                e.wait_ge(p.sem("mm"), mm_base + k)
                p.i(e.reduce_max(NRM2[:, 0:1], bk(p, 5),
                                 axis=mybir.AxisListType.X), "vv", 1)
                p.i(e.reduce_max(NRM2[:, 1:2], bk(p, 6),
                                 axis=mybir.AxisListType.X), "vv", 1)
                e.wait_ge(p.sem("vv"), p.c("vv"))
                p.i(e.tensor_max(NRM[:, h:h + 1],
                                 NRM2[:, 0:1], NRM2[:, 1:2]), "vv", 1)
                e.wait_ge(p.sem("vv"), p.c("vv"))
                p.i(e.tensor_copy(NO[0:1, row * 16 + h: row * 16 + h + 1],
                                  NRM[0:1, h:h + 1]), "vv", 1)
    vv_nrm = p.c("vv")

    @p.block.sync
    def _(e):
        e.wait_ge(p.sem("ev"), ev_qkv)
        p.i(e.dma_start(out=k_out[:, :], in_=KB), "st", 16)
        p.i(e.dma_start(out=q_out[:, :], in_=QB), "st", 16)
        for tt in range(8):
            p.i(e.dma_start(out=v_out[tt * 128:(tt + 1) * 128, :],
                            in_=VB[:, tt * 2048:(tt + 1) * 2048]), "st", 16)
        e.wait_ge(p.sem("vv"), vv_nrm)
        p.i(e.dma_start(out=nrm_out[:, :], in_=NO[:, :]), "st", 16)
        e.wait_ge(p.sem("st"), p.c("st"))

    p.finish()
    return p


def ref_partial_l1(x, wgu, wdn, wk, wq, wv):
    import numpy as _np
    xx = x.astype(_np.float64)
    m = xx.mean(-1, keepdims=True)
    v = xx.var(-1, keepdims=True)
    ln = (xx - m) / _np.sqrt(v + EPS_LN)
    gu = ln @ wgu.astype(_np.float64).T
    g, u = gu[:, :INTER], gu[:, INTER:]
    hh = g / (1 + _np.exp(-g)) * u
    x1 = xx + 0.5 * (hh @ wdn.astype(_np.float64).T)
    m1 = x1.mean(-1, keepdims=True)
    v1 = x1.var(-1, keepdims=True)
    ln1 = (x1 - m1) / _np.sqrt(v1 + EPS_LN)
    k = ln1 @ wk.astype(_np.float64).T
    q = ln1 @ wq.astype(_np.float64).T
    vv = ln1 @ wv.astype(_np.float64).T
    return x1, ln1, q, k, vv


def host_rotate_k(k_fm_full, r):
    rot = np.roll(k_fm_full, -r, axis=2)
    return np.ascontiguousarray(rot.reshape(128, 16 * S))


def host_rotate_v(v_tm_full, r):
    rot = np.roll(v_tm_full, -r, axis=0)
    return np.ascontiguousarray(
        rot.reshape(S, 4, 512).transpose(1, 0, 2))


def host_cbias(r):
    out = np.zeros((64,), np.float32)
    for kt in range(32):
        for qc in range(2):
            if qc * 4 <= kt < qc * 4 + 4:
                v = 0.0
            else:
                ka = (r + kt * 128) % S
                q0 = r + qc * 512
                if ka + 127 <= q0:
                    v = 0.0
                else:
                    v = NEG
            out[kt * 2 + qc] = v
    return np.tile(out[None, :], (128, 1)).astype(np.float32)


def ref_attention(q, k, v, wo, causal=True):
    nq = q.shape[0]
    nk = k.shape[0]
    qh = q.reshape(nq, H, DH).astype(np.float64)
    kh = k.reshape(nk, H, DH).astype(np.float64)
    vh = v.reshape(nk, H, DH).astype(np.float64)
    out = np.zeros((nq, H, DH))
    return qh, kh, vh, out


def emit_selection(p, logits_in, selrows, LR, LSB, ST):
    ones_f = p.cs("ones_f")

    @p.block.sync
    def _(e):
        p.i(e.dma_start(out=LR[:, :], in_=logits_in[:, :]), "se", 16)
        p.i(e.dma_start(out=LSB[:, :],
                        in_=logits_in.rearrange("a (q j) -> (a q) j", q=128)),
            "se", 16)
    se0 = p.c("se")

    @p.block.vector
    def _(e):
        e.wait_ge(p.sem("se"), se0)
        e.wait_ge(p.sem("cl"), p.c("cl"))
        p.i(e.reduce_max(ST[0:1, 1:2], LR[:, :], axis=mybir.AxisListType.X),
            "vv", 1)
        p.i(e.tensor_reduce(ST[0:1, 0:1], LR[:, :], mybir.AxisListType.X,
                            OP.min), "vv", 1)
        e.wait_ge(p.sem("vv"), p.c("vv"))
        p.i(e.tensor_copy(ST[0:1, 6:7], ST[0:1, 1:2]), "vv", 1)
        p.i(e.tensor_scalar_add(ST[0:1, 0:1], ST[0:1, 0:1], -1.0), "vv", 1)
        e.wait_ge(p.sem("vv"), p.c("vv"))

    for it in range(50):
        @p.block.vector
        def _(e):
            p.i(e.tensor_add(ST[0:1, 2:3], ST[0:1, 0:1], ST[0:1, 1:2]), "vv", 1)
            e.wait_ge(p.sem("vv"), p.c("vv"))
            p.i(e.tensor_scalar_mul(ST[0:1, 2:3], ST[0:1, 2:3], 0.5), "vv", 1)
        vv_mid = p.c("vv")

        @p.block.tensor
        def _(e):
            e.wait_ge(p.sem("vv"), vv_mid)
            if it > 0:
                pass
            p.i(e.matmul(bk(p, 0, 1), ones_f[0:1, :], ST[0:1, 2:3],
                         start=True, stop=True), "mm", 1)
        mm_mid = p.c("mm")

        @p.block.vector
        def _(e):
            e.wait_ge(p.sem("mm"), mm_mid)
            p.i(e.tensor_scalar(p.bounce[:, 0:32],
                                LSB[:, :], bk(p, 0, 1), None, OP.is_gt,
                                op1=OP.add, accum_out=p.bounce[:, 32:33]),
                "vv", 1)
        vv_cnt = p.c("vv")

        @p.block.tensor
        def _(e):
            e.wait_ge(p.sem("vv"), vv_cnt)
            p.i(e.matmul(p.psf[0:1, 4:5], ones_f[:, 0:1],
                         p.bounce[:, 32:33], start=True, stop=True), "mm", 1)
        mm_tot = p.c("mm")

        @p.block.vector
        def _(e):
            e.wait_ge(p.sem("mm"), mm_tot)
            p.i(e.tensor_scalar(ST[0:1, 3:4], p.psf[0:1, 4:5], 1024.5,
                                None, OP.is_gt), "vv", 1)
            p.i(e.tensor_sub(ST[0:1, 4:5], ST[0:1, 2:3], ST[0:1, 0:1]), "vv", 1)
            e.wait_ge(p.sem("vv"), p.c("vv"))
            p.i(e.tensor_mul(ST[0:1, 4:5], ST[0:1, 4:5], ST[0:1, 3:4]), "vv", 1)
            e.wait_ge(p.sem("vv"), p.c("vv"))
            p.i(e.tensor_add(ST[0:1, 0:1], ST[0:1, 0:1], ST[0:1, 4:5]), "vv", 1)
            e.wait_ge(p.sem("vv"), p.c("vv"))
            p.i(e.tensor_sub(ST[0:1, 4:5], ST[0:1, 2:3], ST[0:1, 1:2]), "vv", 1)
            p.i(e.tensor_scalar(ST[0:1, 5:6], ST[0:1, 3:4], 1.0, -1.0,
                                OP.subtract, OP.mult), "vv", 1)
            e.wait_ge(p.sem("vv"), p.c("vv"))
            p.i(e.tensor_mul(ST[0:1, 4:5], ST[0:1, 4:5], ST[0:1, 5:6]), "vv", 1)
            e.wait_ge(p.sem("vv"), p.c("vv"))
            p.i(e.tensor_add(ST[0:1, 1:2], ST[0:1, 1:2], ST[0:1, 4:5]), "vv", 1)
            e.wait_ge(p.sem("vv"), p.c("vv"))

    @p.block.vector
    def _(e):
        p.i(e.tensor_scalar(selrowbuf(p, 1), LR[:, :], ST[0:1, 1:2], None,
                            OP.is_gt), "vv", 1)
        e.wait_ge(p.sem("vv"), p.c("vv"))
        p.i(e.tensor_tensor_scan(selrowbuf(p, 0), selrowbuf(p, 1),
                                 selrowbuf(p, 1), 0.0, OP.add, OP.bypass),
            "vv", 1)
        p.i(e.tensor_scalar_mul(ST[0:1, 7:8], ST[0:1, 6:7], -1.0), "vv", 1)
        e.wait_ge(p.sem("vv"), p.c("vv"))
    vv_c = p.c("vv")

    @p.block.scalar
    def _(e):
        e.wait_ge(p.sem("vv"), vv_c)
        p.i(e.activation(selrowbuf(p, 2), LR[:, :], AF.Exp,
                         bias=ST[0:1, 7:8]), "sc", 1)
    sc_e = p.c("sc")

    @p.block.vector
    def _(e):
        e.wait_ge(p.sem("sc"), sc_e)
        p.i(e.tensor_mul(selrowbuf(p, 2), selrowbuf(p, 2), selrowbuf(p, 1)),
            "vv", 1)
        e.wait_ge(p.sem("vv"), p.c("vv"))
        p.i(e.reduce_sum(ST[0:1, 8:9], selrowbuf(p, 2),
                         axis=mybir.AxisListType.X), "vv", 1)
        e.wait_ge(p.sem("vv"), p.c("vv"))
        p.i(e.reciprocal(ST[0:1, 9:10], ST[0:1, 8:9]), "vv", 1)
        e.wait_ge(p.sem("vv"), p.c("vv"))
        p.i(e.tensor_scalar(selrowbuf(p, 2), selrowbuf(p, 2), ST[0:1, 9:10],
                            None, OP.mult), "vv", 1)
    vv_sel = p.c("vv")

    @p.block.sync
    def _(e):
        e.wait_ge(p.sem("vv"), vv_sel)
        for rr in range(3):
            p.i(e.dma_start(out=selrows[rr][:, :], in_=selrowbuf(p, rr)),
                "se", 16)
    return p.c("se")


def selrowbuf(p, rr):
    return p.selrow_aps[rr]


def build_L3():
    p = Prog("L3")
    lg_in = p.din("logits", [1, S], F32)
    lnx_tm = p.din("lnx_tm", [S, D], BF16)
    g1_in = p.din("g1_fm", [128, 16], F32)
    wqs_in = p.din("wqs", [D, 512], F32)
    wks_in = p.din("wks", [D, 512], F32)
    wvs_in = p.din("wvs", [D, 512], F32)
    wos_in = p.din("wos", [512, D], F32)
    maskT_in = p.din("maskT", [TOPK, TOPK], F32)
    cosr_in = p.din("cosr", [128, TOPK], F32)
    sinr_in = p.din("sinr", [128, TOPK], F32)

    part_out = p.dout("partial", [128, NCH * T], F32)
    filt_out = p.dout("filt_fm", [128, NCH * T], BF16)

    wbf = {n: p.dint(n + "_bf", [D, 512], BF16) for n in ("wqs", "wks", "wvs")}
    wos_bf = p.dint("wos_bf", [512, D], BF16)
    selrows = [p.dint("selC", [32, 128], F32), p.dint("selM", [32, 128], F32),
               p.dint("selR", [32, 128], F32)]
    cbd = p.dint("cbd", [128, S], F32)

    FB = p.sb("FB", [128, 16384], BF16)
    H1 = p.sb("H1", [128, 16384], BF16)
    QKV3 = p.sb("QKV3", [128, 12288], BF16)
    OO = p.sb("OO", [128, 8192], BF16)
    CBF = p.sb("CBF", [128, 4096], F32)
    RT = p.sb("RT", [128, 2048], F32)
    GT = p.sb("GT", [128, 4096], BF16)
    MTB = p.sb("MTB", [128, 1024], F32)
    TMP = p.sb("TMP", [128, 1024], F32)
    EBB = p.sb("EBB", [128, 2048], BF16)
    LSB = p.sb("LSB", [128, 32], F32)
    ST = p.sb("ST", [1, 16], F32)
    SELF_ = p.sb("SELF", [128, 8], F32)
    SELI = p.sb("SELI", [128, 8], I32)
    SQ = p.sb("SQ", [128, 1024], BF16)
    STT = p.sb("STT", [128, 2048], F32)
    GM = p.sb("GM", [128, 16], F32)
    NB = p.sb("NB", [1, 32], F32)
    CBB = p.sb("CBB", [128, 4], F32)
    MM1 = p.sb("MM1", [128, 1], F32)
    H1F = H1.bitcast(F32)
    p.selrow_aps = [H1F[0:1, 0:4096], H1F[0:1, 4096:8192],
                    QKV3.bitcast(F32)[0:1, 0:4096]]
    LR = CBF[0:1, 0:4096]

    p.use_consts(["ones_bf", "ones_f", "eye_bf", "p64_bf", "zind_bf",
                  "bsel_bf", "sgn_f", "iotasel_f"])
    p.use_fconsts([EPS_RMS, -40.0])
    p.begin()
    p.load_consts()

    ones_bf = p.cs("ones_bf")
    ones_f = p.cs("ones_f")
    eye = p.cs("eye_bf")
    p64 = p.cs("p64_bf")
    zind = p.cs("zind_bf")
    bsel = p.cs("bsel_bf")

    ZS2 = None
    XIO = p.sb("XIO", [128, 4096], F32)
    ZS2 = XIO[0:4, 0:1024]
    ZSB2 = XIO[0:4, 1024:1536].bitcast(BF16)

    @p.block.gpsimd
    def _(e):
        for n, src_ in (("wqs", wqs_in), ("wks", wks_in), ("wvs", wvs_in)):
            p.i(e.dma_start(out=wbf[n][:, :], in_=src_[:, :]), "wc" + n, 16)
        p.i(e.dma_start(out=wos_bf[:, :], in_=wos_in[:, :]), "wcwos", 16)

    @p.block.sync
    def _(e):
        p.i(e.dma_start(out=RT[:, 0:1024], in_=cosr_in[:, :]), "rt", 16)
        p.i(e.dma_start(out=RT[:, 1024:2048], in_=sinr_in[:, :]), "rt", 16)
        p.i(e.dma_start(out=GM[:, :], in_=g1_in[:, :]), "rt", 16)
    rt0 = p.c("rt")

    @p.block.vector
    def _(e):
        e.wait_ge(p.sem("rt"), rt0)
        e.wait_ge(p.sem("cl"), p.c("cl"))
        p.i(e.tensor_scalar(RT[:, 1024:2048], RT[:, 1024:2048],
                            p.cs("sgn_f")[:, 0:1], None, OP.mult), "vv", 1)
    vv_rt = p.c("vv")

    emit_selection(p, lg_in, selrows, LR, LSB, ST)
    se_done = p.c("se")

    @p.block.sync
    def _(e):
        e.wait_ge(p.sem("se"), se_done)
        p.i(e.dma_start(out=cbd[0:1, :], in_=selrows[0][:, :]), "cb", 16)
        rows = 1
        while rows < 128:
            e.wait_ge(p.sem("cb"), p.c("cb"))
            p.i(e.dma_start(out=cbd[rows:2 * rows, :], in_=cbd[0:rows, :]),
                "cb", 16)
            rows *= 2
        e.wait_ge(p.sem("cb"), p.c("cb"))
        p.i(e.dma_start(out=CBF[:, :], in_=cbd[:, :]), "cb", 16)
    cb_done = p.c("cb")

    @p.block.vector
    def _(e):
        e.wait_ge(p.sem("cb"), cb_done)
        for jj in range(8):
            p.i(e.tensor_scalar(FB.bitcast(F32)[:, 0:4096],
                                CBF[:, :], p.cs("iotasel_f")[:, jj:jj + 1], None,
                                OP.is_le, op1=OP.add,
                                accum_out=SELF_[:, jj:jj + 1]), "vv", 1)
            e.wait_ge(p.sem("vv"), p.c("vv"))
        e.wait_ge(p.sem("vv"), p.c("vv"))
        p.i(e.tensor_copy(SELI[:, :], SELF_[:, :]), "vv", 1)
    vv_seli = p.c("vv")

    mt_base = p.c("mt")
    vv_fb_base = p.c("vv")
    gt_base = [p.c("gt0"), p.c("gt1")]

    for jj in range(8):
        @p.block.gpsimd
        def _(e):
            if jj == 0:
                e.wait_ge(p.sem("vv"), vv_seli)
            if jj >= 2:
                e.wait_ge(p.sem("mt"), mt_base + (jj - 1) * NCH)
            p.i(e.indirect_dma_start(
                GT[:, (jj % 2) * 2048:(jj % 2) * 2048 + 2048],
                None, lnx_tm[:, :], IndirectOffsetOnAxis(SELI[:, jj:jj + 1], 0),
                bounds_check=S - 1, oob_is_err=False), "gt%d" % (jj % 2), 16)

        @p.block.tensor
        def _(e):
            e.wait_ge(p.sem("gt%d" % (jj % 2)),
                      gt_base[jj % 2] + 16 * (jj // 2 + 1))
            for ch in range(NCH):
                tix = jj * NCH + ch
                if tix >= 2:
                    e.wait_ge(p.sem("vv"), vv_fb_base + tix - 1)
                p.i(e.transpose(
                    p.psf[:, (tix % 2) * 512:(tix % 2) * 512 + 64].bitcast(BF16),
                    GT[:, (jj % 2) * 2048 + ch * 128:(jj % 2) * 2048 + (ch + 1) * 128],
                    eye[:, :]), "mt", 1)

        @p.block.vector
        def _(e):
            for ch in range(NCH):
                tix = jj * NCH + ch
                e.wait_ge(p.sem("mt"), mt_base + tix + 1)
                p.i(e.tensor_copy(
                    FB[:, ch * T + jj * 128: ch * T + (jj + 1) * 128],
                    p.psf[:, (tix % 2) * 512:(tix % 2) * 512 + 64].bitcast(BF16)),
                    "vv", 1)
    ev_fb = p.c("vv")

    sq_base = p.c("sc")
    mmr_base = p.c("mm")

    @p.block.scalar
    def _(e):
        e.wait_ge(p.sem("vv"), ev_fb)
        for k in range(32):
            tc, ch = divmod(k, NCH)
            if k >= 2:
                e.wait_ge(p.sem("mm"), mmr_base + k - 1)
            p.i(e.activation(SQ[:, (k % 2) * 512:(k % 2) * 512 + 512],
                             FB[:, ch * T + tc * 512: ch * T + tc * 512 + 512],
                             AF.Square), "sc", 1)

    @p.block.tensor
    def _(e):
        for k in range(32):
            tc, ch = divmod(k, NCH)
            e.wait_ge(p.sem("sc"), sq_base + k + 1)
            p.i(e.matmul(bk(p, tc), ones_bf[:, :],
                         SQ[:, (k % 2) * 512:(k % 2) * 512 + 512],
                         start=(ch == 0), stop=(ch == NCH - 1)), "mm", 1)
    mm_rs = p.c("mm")

    @p.block.scalar
    def _(e):
        e.wait_ge(p.sem("mm"), mm_rs)
        for tc in range(2):
            p.i(e.activation(STT[:, tc * 512:(tc + 1) * 512], bk(p, tc),
                             AF.Sqrt, scale=1.0 / D, bias=EPS_RMS), "sc", 1)
    sc_rs = p.c("sc")

    @p.block.vector
    def _(e):
        e.wait_ge(p.sem("sc"), sc_rs)
        for tc in range(2):
            p.i(e.reciprocal(STT[:, tc * 512:(tc + 1) * 512],
                             STT[:, tc * 512:(tc + 1) * 512]), "vv", 1)
        e.wait_ge(p.sem("vv"), p.c("vv"))
        e.wait_ge(p.sem("rt"), rt0)
        for tc in range(2):
            for ch in range(NCH):
                sl = slice(ch * T + tc * 512, ch * T + tc * 512 + 512)
                p.i(e.scalar_tensor_tensor(
                    H1[:, sl], FB[:, sl], GM[:, ch:ch + 1],
                    STT[:, tc * 512:(tc + 1) * 512], OP.mult, OP.mult), "vv", 1)
    vv_h1 = p.c("vv")

    QH = QKV3[:, 0:4096]
    KH = QKV3[:, 4096:8192]
    VC = QKV3[:, 8192:12288]

    def mk_qk(wname, dst):
        def load(e, og, ch, par):
            e.wait_ge(p.sem("wc" + wname), 16)
            return e.dma_start(
                out=p.wt[:, par * 1024: par * 1024 + 256],
                in_=wbf[wname][ch * 128:(ch + 1) * 128, og * 256:(og + 1) * 256])

        def mms(e, og, ch, par, first, last):
            b = (og % 2) * 4
            for hd in range(2):
                wti = p.wt[:, par * 1024 + hd * 128: par * 1024 + hd * 128 + 128]
                for tc in range(2):
                    mm = e.matmul(bk(p, b + hd * 2 + tc), wti,
                                  H1[:, ch * T + tc * 512: ch * T + tc * 512 + 512],
                                  start=first, stop=last)
            return mm

        def evac(og):
            def run(mp_target):
                @p.block.vector
                def _(e):
                    e.wait_ge(p.sem("mp"), mp_target)
                    b = (og % 2) * 4
                    for hd in range(2):
                        h = og * 2 + hd
                        p.i(e.tensor_copy(
                            dst[:, h * T:(h + 1) * T],
                            p.psf[:, (b + hd * 2) * 512:(b + hd * 2 + 2) * 512]),
                            "ev", 1 if hd == 1 else 0)
            return run
        return load, mms, [evac(g) for g in range(2)]

    lq, mq, eq = mk_qk("wqs", QH)
    emit_proj_chstream(p, 2, NCH, lq, mq, eq, psum_lag=2, rdy=[("vv", vv_h1)])
    lk, mk2, ek = mk_qk("wks", KH)
    emit_proj_chstream(p, 2, NCH, lk, mk2, ek, psum_lag=2)

    def v_load(e, og, ch, par):
        e.wait_ge(p.sem("wcwvs"), 16)
        return e.dma_start(
            out=p.wt[:, par * 1024: par * 1024 + 256],
            in_=wbf["wvs"][ch * 128:(ch + 1) * 128, og * 256:(og + 1) * 256])

    def v_mms(e, og, ch, par, first, last):
        wti = p.wt[:, par * 1024: par * 1024 + 256]
        for tt in range(8):
            mm = e.matmul(p.psf[:, tt * 512: tt * 512 + 256],
                          H1[:, ch * T + tt * 128: ch * T + tt * 128 + 128], wti,
                          start=first, stop=last)
        return mm

    def v_evac(og):
        def run(mp_target):
            @p.block.vector
            def _(e):
                e.wait_ge(p.sem("mp"), mp_target)
                for tt in range(8):
                    p.i(e.tensor_copy(
                        VC[:, tt * 512 + og * 256: tt * 512 + og * 256 + 256],
                        p.psf[:, tt * 512: tt * 512 + 256]),
                        "ev", 1 if tt == 7 else 0)
        return run

    emit_proj_chstream(p, 2, NCH, v_load, v_mms, [v_evac(g) for g in range(2)],
                       psum_lag=1)
    ev_qkv3 = p.c("ev")

    mmp_base = p.c("mm")
    vvr_base = p.c("vv")

    @p.block.tensor
    def _(e):
        e.wait_ge(p.sem("ev"), ev_qkv3)
        for i8 in range(8):
            if i8 >= 2:
                e.wait_ge(p.sem("vv"), vvr_base + (i8 - 1) * 3)
            buf = QH if i8 < 4 else KH
            h = i8 % 4
            for tc in range(2):
                p.i(e.matmul(bk(p, 4 + (i8 % 2) * 2 + tc), p64[:, :],
                             buf[:, h * T + tc * 512: h * T + tc * 512 + 512],
                             start=True, stop=True), "mm", 1)

    @p.block.vector
    def _(e):
        for i8 in range(8):
            buf = QH if i8 < 4 else KH
            h = i8 % 4
            e.wait_ge(p.sem("mm"), mmp_base + (i8 + 1) * 2)
            sl = slice(h * T, (h + 1) * T)
            p.i(e.tensor_mul(MTB[:, 0:1024],
                             p.psf[:, (4 + (i8 % 2) * 2) * 512:
                                   (4 + (i8 % 2) * 2 + 2) * 512],
                             RT[:, 1024:2048]), "vv", 1)
            p.i(e.tensor_mul(GT.bitcast(F32)[:, 0:1024], buf[:, sl],
                             RT[:, 0:1024]), "vv", 1)
            e.wait_ge(p.sem("vv"), p.c("vv"))
            p.i(e.tensor_add(buf[:, sl], GT.bitcast(F32)[:, 0:1024],
                             MTB[:, 0:1024]), "vv", 1)
            e.wait_ge(p.sem("vv"), p.c("vv"))
    vv_rope = p.c("vv")

    nb_base = p.c("mm")
    scn_base = p.c("sc")
    vvn_base = p.c("vv")

    @p.block.scalar
    def _(e):
        e.wait_ge(p.sem("vv"), vv_rope)
        for i8 in range(8):
            if i8 >= 1:
                e.wait_ge(p.sem("mm"), nb_base + i8)
            buf = QH if i8 < 4 else KH
            h = i8 % 4
            p.i(e.activation(SQ[:, :], buf[:, h * T:(h + 1) * T], AF.Square),
                "sc", 1)

    @p.block.tensor
    def _(e):
        for i8 in range(8):
            e.wait_ge(p.sem("sc"), scn_base + i8 + 1)
            if i8 >= 1:
                e.wait_ge(p.sem("vv"), vvn_base + 3 * i8)
            e.matmul(bk(p, 0), ones_bf[:, :], SQ[:, 0:512], start=True, stop=True)
            p.i(e.matmul(bk(p, 1), ones_bf[:, :], SQ[:, 512:1024],
                         start=True, stop=True), "mm", 1)

    @p.block.vector
    def _(e):
        for i8 in range(8):
            e.wait_ge(p.sem("mm"), nb_base + i8 + 1)
            p.i(e.reduce_max(NB[0:1, 16:17], bk(p, 0)[0:1, :],
                             axis=mybir.AxisListType.X), "vv", 1)
            p.i(e.reduce_max(NB[0:1, 17:18], bk(p, 1)[0:1, :],
                             axis=mybir.AxisListType.X), "vv", 1)
            e.wait_ge(p.sem("vv"), p.c("vv"))
            p.i(e.tensor_max(NB[0:1, i8:i8 + 1], NB[0:1, 16:17],
                             NB[0:1, 17:18]), "vv", 1)
            e.wait_ge(p.sem("vv"), p.c("vv"))
    vv_nb = p.c("vv")

    mrow = p.dint("mrow", [1, 128], F32)
    mm_base2 = p.c("mk")

    vvm_base = p.c("vm")

    @p.block.sync
    def _(e):
        for mkt in range(8):
            if mkt == 0:
                e.wait_ge(p.sem("vv"), vv_rope)
            else:
                e.wait_ge(p.sem("vm"), vvm_base + mkt)
            p.i(e.dma_start(out=MTB[:, :],
                            in_=maskT_in[mkt * 128:(mkt + 1) * 128, :]),
                "mk", 16)

    @p.block.vector
    def _(e):
        for mkt in range(8):
            e.wait_ge(p.sem("mk"), mm_base2 + 16 * (mkt + 1))
            p.i(e.reduce_max(MM1[:, 0:1] if mkt == 0 else CBB[:, 0:1],
                             MTB[:, :], axis=mybir.AxisListType.X), "vm", 1)
            e.wait_ge(p.sem("vm"), p.c("vm"))
            if mkt > 0:
                p.i(e.tensor_max(MM1[:, 0:1], MM1[:, 0:1], CBB[:, 0:1]), "vv", 1)
                e.wait_ge(p.sem("vv"), p.c("vv"))
    vv_mm1 = p.c("vv")

    @p.block.sync
    def _(e):
        e.wait_ge(p.sem("vv"), vv_mm1)
        p.i(e.dma_start(out=mrow[:, :], in_=MM1[:, 0:1]), "mk", 16)
        e.wait_ge(p.sem("mk"), p.c("mk"))
        p.i(e.dma_start(out=NB[0:1, 18:19] if False else MM1[0:1, 0:1],
                        in_=mrow[0:1, 0:1]) if False else
            e.dma_start(out=CBF[0:1, 0:128], in_=mrow[:, :]), "mk", 16)
    mk_row = p.c("mk")

    @p.block.vector
    def _(e):
        e.wait_ge(p.sem("mk"), mk_row)
        p.i(e.reduce_max(NB[0:1, 18:19], CBF[0:1, 0:128],
                         axis=mybir.AxisListType.X), "vv", 1)
        e.wait_ge(p.sem("vv"), p.c("vv"))
        p.i(e.tensor_scalar(NB[0:1, 19:20], NB[0:1, 18:19], 0.0, -40.0,
                            OP.max, OP.add), "vv", 1)
        p.i(e.tensor_mul(NB[0:1, 20:24], NB[0:1, 0:4], NB[0:1, 4:8]), "vv", 1)
        e.wait_ge(p.sem("vv"), p.c("vv"))
    vv_qk = p.c("vv")

    @p.block.scalar
    def _(e):
        e.wait_ge(p.sem("vv"), vv_qk)
        p.i(e.activation(NB[0:1, 20:24], NB[0:1, 20:24], AF.Sqrt), "sc", 1)
        e.wait_ge(p.sem("sc"), p.c("sc"))
        p.i(e.activation(NB[0:1, 24:28], NB[0:1, 20:24], AF.Relu, scale=ISQ,
                         bias=NB[0:1, 19:20]), "sc", 1)
    sc_c = p.c("sc")

    @p.block.vector
    def _(e):
        e.wait_ge(p.sem("sc"), sc_c)
        p.i(e.tensor_scalar_mul(NB[0:1, 24:28], NB[0:1, 24:28], -1.0), "vv", 1)
        e.wait_ge(p.sem("vv"), p.c("vv"))
    vv_negc = p.c("vv")

    @p.block.tensor
    def _(e):
        e.wait_ge(p.sem("vv"), vv_negc)
        for h in range(4):
            mm = e.matmul(bk(p, 7, 1, off=h), ones_f[0:1, :],
                          NB[0:1, 24 + h:25 + h], start=True, stop=True)
        p.i(mm, "mm", 1)
    mm_cb = p.c("mm")

    @p.block.vector
    def _(e):
        e.wait_ge(p.sem("mm"), mm_cb)
        p.i(e.tensor_copy(CBB[:, 0:4], bk(p, 7, 4)), "vv", 1)
        e.wait_ge(p.sem("vv"), p.c("vv"))
    vv_cbb = p.c("vv")

    OA2 = OO[:, 0:4096]
    OB2 = OO[:, 4096:8192]
    ms3 = p.c("ms")
    va3 = p.c("va")
    sc3 = p.c("sc")
    ma3 = p.c("ma")
    ev3 = p.c("ev")
    mk3 = [p.c("mk0"), p.c("mk1")]

    @p.block.sync
    def _(e):
        for m in range(16):
            qc, kt = divmod(m, 8)
            if m >= 2:
                e.wait_ge(p.sem("va"), va3 + (m - 1) * 4)
            p.i(e.dma_start(
                out=MTB[:, (m % 2) * 512:(m % 2) * 512 + 512],
                in_=maskT_in[kt * 128:(kt + 1) * 128, qc * 512:(qc + 1) * 512]),
                "mk%d" % (m % 2), 16)

    @p.block.tensor
    def _(e):
        j = 0
        for qc in range(2):
            for kt in range(8):
                m = qc * 8 + kt
                e.wait_ge(p.sem("mk%d" % (m % 2)), mk3[m % 2] + 16 * (m // 2 + 1))
                if m == 0:
                    e.wait_ge(p.sem("vv"), vv_cbb)
                if kt == 0 and qc == 1:
                    e.wait_ge(p.sem("ev"), ev3 + 5)
                for h in range(4):
                    if j >= 2:
                        e.wait_ge(p.sem("va"), va3 + j - 1)
                    p.i(e.matmul(
                        bk(p, 5 + (j % 2)),
                        KH[:, h * T + kt * 128: h * T + kt * 128 + 128],
                        QH[:, h * T + qc * 512: h * T + qc * 512 + 512],
                        start=True, stop=True), "ms", 1)
                    e.wait_ge(p.sem("sc"), sc3 + j + 1)
                    eb = EBB[:, (j % 4) * 512:(j % 4) * 512 + 512]
                    e.matmul(bk(p, h),
                             VC[:, kt * 512 + h * 128: kt * 512 + h * 128 + 128],
                             eb, start=(kt == 0), stop=(kt == 7))
                    p.i(e.matmul(p.psf[0:16, 4 * 512:4 * 512 + 512],
                                 zind[:, h * 16:h * 16 + 16], eb,
                                 start=(j % 32 == 0), stop=(j % 32 == 31)),
                        "ma", 1)
                    j += 1

    @p.block.vector
    def _(e):
        j = 0
        for qc in range(2):
            for kt in range(8):
                for h in range(4):
                    e.wait_ge(p.sem("ms"), ms3 + j + 1)
                    if j >= 2:
                        e.wait_ge(p.sem("sc"), sc3 + j - 1)
                    p.i(e.scalar_tensor_tensor(
                        TMP[:, (j % 2) * 512:(j % 2) * 512 + 512],
                        bk(p, 5 + (j % 2)), ISQ,
                        MTB[:, ((qc * 8 + kt) % 2) * 512:
                            ((qc * 8 + kt) % 2) * 512 + 512],
                        OP.mult, OP.add), "va", 1)
                    j += 1
            e.wait_ge(p.sem("ma"), ma3 + (qc + 1) * 32)
            for h in range(4):
                p.i(e.tensor_copy(
                    OA2[:, h * T + qc * 512: h * T + qc * 512 + 512],
                    bk(p, h)), "ev", 1)
            p.i(e.tensor_copy(ZS2[:, qc * 512:(qc + 1) * 512],
                              p.psf[0:4, 4 * 512:4 * 512 + 512]), "ev", 1)

    @p.block.scalar
    def _(e):
        j = 0
        for qc in range(2):
            for kt in range(8):
                for h in range(4):
                    e.wait_ge(p.sem("va"), va3 + j + 1)
                    if j >= 4:
                        e.wait_ge(p.sem("ma"), ma3 + j - 3)
                    p.i(e.activation(EBB[:, (j % 4) * 512:(j % 4) * 512 + 512],
                                     TMP[:, (j % 2) * 512:(j % 2) * 512 + 512],
                                     AF.Exp, bias=CBB[:, h:h + 1]), "sc", 1)
                    j += 1
    ev_att3 = p.c("ev")

    @p.block.vector
    def _(e):
        e.wait_ge(p.sem("ev"), ev_att3)
        p.i(e.reciprocal(ZS2[:, :], ZS2[:, :]), "vv", 1)
        e.wait_ge(p.sem("vv"), p.c("vv"))
        p.i(e.tensor_copy(ZSB2[:, :], ZS2[:, :]), "vv", 1)
        e.wait_ge(p.sem("vv"), p.c("vv"))
    vv_rz3 = p.c("vv")

    @p.block.tensor
    def _(e):
        e.wait_ge(p.sem("vv"), vv_rz3)
        for h in range(4):
            for qc in range(2):
                idx = h * 2 + qc
                if idx >= 2:
                    e.wait_ge(p.sem("vv"), vv_rz3 + idx - 1)
                p.i(e.matmul(bk(p, 5 + (idx % 2)), bsel[0:4, h * 128:(h + 1) * 128],
                             ZSB2[:, qc * 512:(qc + 1) * 512],
                             start=True, stop=True), "mz", 1)
    mz3 = p.c("mz") - 8

    @p.block.vector
    def _(e):
        for h in range(4):
            for qc in range(2):
                idx = h * 2 + qc
                e.wait_ge(p.sem("mz"), mz3 + idx + 1)
                p.i(e.tensor_mul(
                    OB2[:, h * T + qc * 512: h * T + qc * 512 + 512],
                    OA2[:, h * T + qc * 512: h * T + qc * 512 + 512],
                    bk(p, 5 + (idx % 2))), "vv", 1)
    vv_ob2 = p.c("vv")

    def wos_load(e, og, ch, par):
        e.wait_ge(p.sem("wcwos"), 16)
        return e.dma_start(
            out=p.wt[:, par * 1024: par * 1024 + 256],
            in_=wos_bf[ch * 128:(ch + 1) * 128, og * 256:(og + 1) * 256])

    def wos_mms(e, og, ch, par, first, last):
        b = (og % 2) * 4
        for dt in range(2):
            wti = p.wt[:, par * 1024 + dt * 128: par * 1024 + dt * 128 + 128]
            for tc in range(2):
                mm = e.matmul(bk(p, b + dt * 2 + tc), wti,
                              OB2[:, ch * T + tc * 512: ch * T + tc * 512 + 512],
                              start=first, stop=last)
        return mm

    def wos_evac(og):
        def run(mp_target):
            for dt in range(2):
                d_idx = og * 2 + dt
                b = (og % 2) * 4 + dt * 2

                @p.block.vector
                def _(e):
                    e.wait_ge(p.sem("mp"), mp_target)
                    e.wait_ge(p.sem("st%d" % (d_idx % 2)), p.c("st%d" % (d_idx % 2)))
                    p.i(e.tensor_copy(
                        XIO[:, 2048 + (d_idx % 2) * 1024:
                            2048 + (d_idx % 2) * 1024 + 1024],
                        p.psf[:, b * 512:(b + 2) * 512]), "vv", 1)
                    if dt == 1:
                        p.i(e.engine_nop(), "ev", 1)
                vv_t = p.c("vv")

                @p.block.gpsimd
                def _(e):
                    e.wait_ge(p.sem("vv"), vv_t)
                    p.i(e.dma_start(
                        out=part_out[:, d_idx * T:(d_idx + 1) * T],
                        in_=XIO[:, 2048 + (d_idx % 2) * 1024:
                                2048 + (d_idx % 2) * 1024 + 1024]),
                        "st%d" % (d_idx % 2), 16)
        return run

    emit_proj_chstream(p, 8, 4, wos_load, wos_mms,
                       [wos_evac(g) for g in range(8)], psum_lag=2,
                       rdy=[("vv", vv_ob2)])

    @p.block.sync
    def _(e):
        e.wait_ge(p.sem("vv"), vv_h1)
        p.i(e.dma_start(out=filt_out[:, :], in_=FB[:, :]), "so", 16)
        e.wait_ge(p.sem("so"), p.c("so"))
        e.wait_ge(p.sem("st0"), p.c("st0"))
        e.wait_ge(p.sem("st1"), p.c("st1"))

    p.finish()
    return p


def build_L4():
    p = Prog("L4")
    filt_in = p.din("filt_fm", [128, NCH * T], BF16)
    pa_in = [p.din("pa%d" % i, [128, NCH * T], F32) for i in range(4)]
    g2_in = p.din("g2_fm", [128, 16], F32)
    w1s_in = p.din("w1s", [D, D], F32)
    w3s_in = p.din("w3s", [D, D], F32)
    w2s_in = p.din("w2s", [D, D], F32)
    pffn_out = p.dout("pffn", [128, NCH * T], F32)
    xaa_out = p.dout("xaa", [128, NCH * T], F32)

    wbf = {}
    for n in ("w1s", "w3s", "w2s"):
        wbf[n] = p.dint(n + "_bf", [D, D], BF16)

    FB = p.sb("FB", [128, 16384], BF16)
    XAA = p.sb("XAA", [128, 16384], F32)
    H2 = p.sb("H2", [128, 16384], BF16)
    HC = p.sb("HC", [128, 16384], BF16)
    XIO = p.sb("XIO", [128, 4096], F32)
    SQ = p.sb("SQ", [128, 1024], BF16)
    STT = p.sb("STT", [128, 2048], F32)
    GB = p.sb("GB", [128, 1024], BF16)
    GM = p.sb("GM", [128, 16], F32)
    PIO = HC.bitcast(F32)

    p.use_consts(["ones_bf"])
    p.use_fconsts([EPS_RMS])
    p.begin()
    p.load_consts()
    ones_bf = p.cs("ones_bf")

    @p.block.gpsimd
    def _(e):
        for n, src_ in (("w1s", w1s_in), ("w3s", w3s_in), ("w2s", w2s_in)):
            p.i(e.dma_start(out=wbf[n][:, :], in_=src_[:, :]), "wc" + n, 16)

    @p.block.sync
    def _(e):
        p.i(e.dma_start(out=FB[:, :], in_=filt_in[:, :]), "cf", 16)
        p.i(e.dma_start(out=GM[:, :], in_=g2_in[:, :]), "cf", 16)
    cf0 = p.c("cf")

    vva_base = p.c("vv")

    @p.block.sync
    def _(e):
        for ci in range(NCH):
            par = ci % 2
            if ci >= 2:
                e.wait_ge(p.sem("vv"), vva_base + (ci - 1) * 4)
            for k in range(4):
                p.i(e.dma_start(
                    out=PIO[:, par * 4096 + k * 1024: par * 4096 + (k + 1) * 1024],
                    in_=pa_in[k][:, ci * T:(ci + 1) * T]), "pl%d" % par, 16)

    @p.block.vector
    def _(e):
        for ci in range(NCH):
            par = ci % 2
            e.wait_ge(p.sem("pl%d" % par), 64 * (ci // 2 + 1))
            if ci == 0:
                e.wait_ge(p.sem("cf"), cf0)
            dst = XAA[:, ci * T:(ci + 1) * T]
            p.i(e.tensor_add(dst, FB[:, ci * T:(ci + 1) * T],
                             PIO[:, par * 4096: par * 4096 + 1024]), "vv", 1)
            e.wait_ge(p.sem("vv"), p.c("vv"))
            for k in range(1, 4):
                p.i(e.tensor_add(dst, dst,
                                 PIO[:, par * 4096 + k * 1024:
                                     par * 4096 + (k + 1) * 1024]), "vv", 1)
                e.wait_ge(p.sem("vv"), p.c("vv"))
    vv_xaa = p.c("vv")

    @p.block.sync
    def _(e):
        e.wait_ge(p.sem("vv"), vv_xaa)
        p.i(e.dma_start(out=xaa_out[:, :], in_=XAA[:, :]), "so", 16)

    @p.block.vector
    def _(e):
        e.wait_ge(p.sem("vv"), vv_xaa)
        p.i(e.tensor_copy(FB[:, :], XAA[:, :]), "vv", 1)
    vv_xb2 = p.c("vv")

    sq_base = p.c("sc")
    mmr_base = p.c("mm")

    @p.block.scalar
    def _(e):
        e.wait_ge(p.sem("vv"), vv_xb2)
        for k in range(32):
            tc, ch = divmod(k, NCH)
            if k >= 2:
                e.wait_ge(p.sem("mm"), mmr_base + k - 1)
            p.i(e.activation(SQ[:, (k % 2) * 512:(k % 2) * 512 + 512],
                             FB[:, ch * T + tc * 512: ch * T + tc * 512 + 512],
                             AF.Square), "sc", 1)

    @p.block.tensor
    def _(e):
        e.wait_ge(p.sem("cl"), p.c("cl"))
        for k in range(32):
            tc, ch = divmod(k, NCH)
            e.wait_ge(p.sem("sc"), sq_base + k + 1)
            p.i(e.matmul(bk(p, tc), ones_bf[:, :],
                         SQ[:, (k % 2) * 512:(k % 2) * 512 + 512],
                         start=(ch == 0), stop=(ch == NCH - 1)), "mm", 1)
    mm_rs = p.c("mm")

    @p.block.scalar
    def _(e):
        e.wait_ge(p.sem("mm"), mm_rs)
        for tc in range(2):
            p.i(e.activation(STT[:, tc * 512:(tc + 1) * 512], bk(p, tc),
                             AF.Sqrt, scale=1.0 / D, bias=EPS_RMS), "sc", 1)
    sc_rs = p.c("sc")

    @p.block.vector
    def _(e):
        e.wait_ge(p.sem("sc"), sc_rs)
        for tc in range(2):
            p.i(e.reciprocal(STT[:, tc * 512:(tc + 1) * 512],
                             STT[:, tc * 512:(tc + 1) * 512]), "vv", 1)
        e.wait_ge(p.sem("vv"), p.c("vv"))
        for tc in range(2):
            for ch in range(NCH):
                sl = slice(ch * T + tc * 512, ch * T + tc * 512 + 512)
                p.i(e.scalar_tensor_tensor(
                    H2[:, sl], FB[:, sl], GM[:, ch:ch + 1],
                    STT[:, tc * 512:(tc + 1) * 512], OP.mult, OP.mult), "vv", 1)
    vv_h2 = p.c("vv")

    def gu_load(e, og, ch, par):
        e.wait_ge(p.sem("wcw1s"), 16)
        e.wait_ge(p.sem("wcw3s"), 16)
        e.dma_start(out=p.wt[:, par * 1024: par * 1024 + 128],
                    in_=wbf["w1s"][ch * 128:(ch + 1) * 128,
                                   og * 128:(og + 1) * 128]).then_inc(
                                       p.sem("wl%d" % par), 16)
        p.cnt["wl%d" % par] += 16
        return e.dma_start(out=p.wt[:, par * 1024 + 128: par * 1024 + 256],
                           in_=wbf["w3s"][ch * 128:(ch + 1) * 128,
                                          og * 128:(og + 1) * 128])

    def gu_mms(e, og, ch, par, first, last):
        b = (og % 2) * 4
        gt = p.wt[:, par * 1024: par * 1024 + 128]
        ut = p.wt[:, par * 1024 + 128: par * 1024 + 256]
        for tc in range(2):
            e.matmul(bk(p, b + tc), gt,
                     H2[:, ch * T + tc * 512: ch * T + tc * 512 + 512],
                     start=first, stop=last)
            mm = e.matmul(bk(p, b + 2 + tc), ut,
                          H2[:, ch * T + tc * 512: ch * T + tc * 512 + 512],
                          start=first, stop=last)
        return mm

    def gu_evac(og):
        def run(mp_target):
            b = (og % 2) * 4

            @p.block.scalar
            def _(e):
                e.wait_ge(p.sem("mp"), mp_target)
                e.wait_ge(p.sem("ev"), p.c("ev"))
                p.i(e.activation(GB[:, :], p.psf[:, b * 512:(b + 2) * 512],
                                 AF.Silu), "sc", 1)
            sc_t = p.c("sc")

            @p.block.vector
            def _(e):
                e.wait_ge(p.sem("sc"), sc_t)
                p.i(e.tensor_mul(HC[:, og * T:(og + 1) * T], GB[:, :],
                                 p.psf[:, (b + 2) * 512:(b + 4) * 512]), "ev", 1)
        return run

    base_wl = [p.c("wl0"), p.c("wl1")]
    base_mp = p.c("mp")
    base_ev = p.c("ev")
    n_og, nch = 16, NCH
    evacs = [gu_evac(g) for g in range(n_og)]

    @p.block.sync
    def _(e):
        for og in range(n_og):
            for ch in range(nch):
                k = og * nch + ch
                e.wait_ge(p.sem("mp"), base_mp + max(0, k - 1))
                p.i(gu_load(e, og, ch, k % 2), "wl%d" % (k % 2), 16)

    @p.block.tensor
    def _(e):
        for og in range(n_og):
            for ch in range(nch):
                k = og * nch + ch
                e.wait_ge(p.sem("wl%d" % (k % 2)),
                          base_wl[k % 2] + 32 * (k // 2 + 1))
                if k == 0:
                    e.wait_ge(p.sem("vv"), vv_h2)
                if ch == 0:
                    e.wait_ge(p.sem("ev"), base_ev + max(0, og - 1))
                last = gu_mms(e, og, ch, k % 2, ch == 0, ch == nch - 1)
                p.i(last, "mp", 1)

    for og in range(n_og):
        evacs[og](base_mp + (og + 1) * nch)
    ev_hc = p.c("ev")

    def dn_load(e, og, ch, par):
        e.wait_ge(p.sem("wcw2s"), 16)
        return e.dma_start(
            out=p.wt[:, par * 1024: par * 1024 + 256],
            in_=wbf["w2s"][ch * 128:(ch + 1) * 128, og * 256:(og + 1) * 256])

    def dn_mms(e, og, ch, par, first, last):
        b = (og % 2) * 4
        for dt in range(2):
            wti = p.wt[:, par * 1024 + dt * 128: par * 1024 + dt * 128 + 128]
            for tc in range(2):
                mm = e.matmul(bk(p, b + dt * 2 + tc), wti,
                              HC[:, ch * T + tc * 512: ch * T + tc * 512 + 512],
                              start=first, stop=last)
        return mm

    def dn_evac(og):
        def run(mp_target):
            for dt in range(2):
                d_idx = og * 2 + dt
                b = (og % 2) * 4 + dt * 2

                @p.block.vector
                def _(e):
                    e.wait_ge(p.sem("mp"), mp_target)
                    e.wait_ge(p.sem("st%d" % (d_idx % 2)), p.c("st%d" % (d_idx % 2)))
                    p.i(e.tensor_copy(
                        XIO[:, 2048 + (d_idx % 2) * 1024:
                            2048 + (d_idx % 2) * 1024 + 1024],
                        p.psf[:, b * 512:(b + 2) * 512]), "vv", 1)
                    if dt == 1:
                        p.i(e.engine_nop(), "ev", 1)
                vv_t = p.c("vv")

                @p.block.gpsimd
                def _(e):
                    e.wait_ge(p.sem("vv"), vv_t)
                    p.i(e.dma_start(
                        out=pffn_out[:, d_idx * T:(d_idx + 1) * T],
                        in_=XIO[:, 2048 + (d_idx % 2) * 1024:
                                2048 + (d_idx % 2) * 1024 + 1024]),
                        "st%d" % (d_idx % 2), 16)
        return run

    emit_proj_chstream(p, 8, NCH, dn_load, dn_mms, [dn_evac(g) for g in range(8)],
                       psum_lag=2, rdy=[("ev", ev_hc)])

    @p.block.sync
    def _(e):
        e.wait_ge(p.sem("so"), p.c("so"))
        e.wait_ge(p.sem("st0"), p.c("st0"))
        e.wait_ge(p.sem("st1"), p.c("st1"))

    p.finish()
    return p


def build_L5(qkv=True):
    p = Prog("L5" if qkv else "L9")
    x3_in = p.din("x3_fm", [128, NCH * T], F32)
    lnx3_in = p.din("lnx3_fm", [128, NCH * T], BF16)
    lg_in = p.din("logits", [1, S], F32)
    xaa_in = p.din("xaa", [128, NCH * T], F32)
    pf_in = [p.din("pf%d" % i, [128, NCH * T], F32) for i in range(4)]
    myoff_in = p.din("myoff", [8, 1], I32)
    if qkv:
        wqkv_in = p.din("wqkv", [D, 3 * D], F32)
        gq_in = p.din("gq", [128, 1], F32)
        gqr_in = p.din("gqr", [128, 1], F32)
        gk_in = p.din("gk", [128, 1], F32)
        gkr_in = p.din("gkr", [128, 1], F32)
        cosa_in = p.din("cosa", [128, T], F32)
        sina_in = p.din("sina", [128, T], F32)

    x4_out = p.dout("x4_fm", [128, NCH * T], F32)
    if qkv:
        q_out = p.dout("q_fm", [128, H * T], BF16)
        k_out = p.dout("k_fm", [128, H * T], BF16)
        v_out = p.dout("v_tm", [T, D], BF16)
        nrm_out = p.dout("qknorm", [1, 32], F32)
        wqkv_bf = p.dint("wqkv_bf", [D, 3 * D], BF16)
        qraw = p.dint("qraw", [128, H * T], BF16)
        kraw = p.dint("kraw", [128, H * T], BF16)

    selrows = [p.dint("selC", [32, 128], F32), p.dint("selM", [32, 128], F32),
               p.dint("selR", [32, 128], F32)]
    xout_tm = p.dint("xout_tm", [S // 4, D], F32)

    X4 = p.sb("X4", [128, 16384], F32)
    BIGB = p.sb("BIGB", [128, 16384], BF16)
    BIGC = p.sb("BIGC", [128, 16384], BF16)
    LNB = p.sb("LNB", [128, 16384], BF16)
    SQ = p.sb("SQ", [128, 1024], BF16)
    STT = p.sb("STT", [128, 2048], F32)
    LSB = p.sb("LSB", [128, 32], F32)
    ST = p.sb("ST", [1, 16], F32)
    CMR = p.sb("CMR", [8, 384], F32)
    ROFS = p.sb("ROFS", [128, 16], F32)
    ROFI = p.sb("ROFI", [128, 8], I32)
    HT = p.sb("HT", [128, 2048], BF16)
    HT2 = p.sb("HT2", [128, 2048], BF16)
    RSD = p.sb("RSD", [128, 1024], F32)
    RT2 = p.sb("RT2", [128, 2048], F32)
    GMS = p.sb("GMS", [128, 4], F32)
    NB = p.sb("NB", [1, 36], F32)
    BF32 = BIGB.bitcast(F32)
    PIO = BF32[:, 0:8192]
    GX = BF32[:, 0:2048]
    TMPA = BF32[:, 2048:3072]
    TMPB = BF32[:, 3072:4096]
    TMB = [BF32[:, 4096:6144], BF32[:, 6144:8192]]

    p.use_consts(["ones_bf", "ones_f", "eye_f", "p64_bf"])
    p.use_fconsts([EPS_RMS, EPS_LN])
    p.begin()
    p.load_consts()
    ones_bf = p.cs("ones_bf")
    ones_f = p.cs("ones_f")
    eye_f = p.cs("eye_f")
    p64 = p.cs("p64_bf")
    H1F = LNB.bitcast(F32)
    p.selrow_aps = [H1F[0:1, 0:4096], H1F[0:1, 4096:8192],
                    BIGC.bitcast(F32)[0:1, 0:4096]]
    LR = BF32[0:1, 0:4096]

    emit_selection(p, lg_in, selrows, LR, LSB, ST)
    se_done = p.c("se")

    @p.block.gpsimd
    def _(e):
        e.wait_ge(p.sem("se"), se_done)
        p.i(e.dma_start(out=ROFI[0:8, 0:1], in_=myoff_in[:, :]), "my", 16)
        e.wait_ge(p.sem("my"), p.c("my"))
        for rr in range(3):
            p.i(e.indirect_dma_start(
                CMR[:, rr * 128:(rr + 1) * 128], None,
                selrows[rr][:, :],
                IndirectOffsetOnAxis(ROFI[0:8, 0:1], 0),
                bounds_check=31, oob_is_err=False), "my", 16)
    my_done = p.c("my")

    @p.block.tensor
    def _(e):
        e.wait_ge(p.sem("my"), my_done)
        e.wait_ge(p.sem("cl"), p.c("cl"))
        for rr in range(2):
            p.i(e.transpose(bk(p, 5 + rr, 8), CMR[:, rr * 128:(rr + 1) * 128],
                            eye_f[0:8, 0:8]), "mt", 1)
    mt_cmr = p.c("mt")

    @p.block.vector
    def _(e):
        e.wait_ge(p.sem("mt"), mt_cmr)
        p.i(e.tensor_scalar_add(ROFS[:, 0:8], bk(p, 5, 8), -1.0), "vv", 1)
        p.i(e.tensor_scalar(ROFS[:, 8:16], bk(p, 6, 8), -1.0, -1000000.0,
                            OP.add, OP.mult), "vv", 1)
        e.wait_ge(p.sem("vv"), p.c("vv"))
        p.i(e.tensor_add(ROFS[:, 0:8], ROFS[:, 0:8], ROFS[:, 8:16]), "vv", 1)
        e.wait_ge(p.sem("vv"), p.c("vv"))
        p.i(e.tensor_copy(ROFI[:, :], ROFS[:, 0:8]), "vv", 1)
    vv_rofi = p.c("vv")

    @p.block.tensor
    def _(e):
        e.wait_ge(p.sem("vv"), vv_rofi)
        p.i(e.transpose(bk(p, 5, 8), CMR[:, 2 * 128:3 * 128], eye_f[0:8, 0:8]),
            "mt", 1)
    mt_rw = p.c("mt")

    @p.block.vector
    def _(e):
        e.wait_ge(p.sem("mt"), mt_rw)
        p.i(e.tensor_scalar_mul(ROFS[:, 8:16], bk(p, 5, 8), 0.5), "vv", 1)
    vv_rwh = p.c("vv")

    vva_base = p.c("vv")

    @p.block.sync
    def _(e):
        for ci in range(NCH):
            par = ci % 2
            if ci >= 2:
                e.wait_ge(p.sem("vv"), vva_base + (ci - 1) * 4)
            p.i(e.dma_start(out=PIO[:, par * 4096 + 3072: par * 4096 + 4096],
                            in_=xaa_in[:, ci * T:(ci + 1) * T]), "pl%d" % par, 16)
            for k in range(3):
                p.i(e.dma_start(
                    out=PIO[:, par * 4096 + k * 1024: par * 4096 + (k + 1) * 1024],
                    in_=pf_in[k][:, ci * T:(ci + 1) * T]), "pl%d" % par, 16)

    @p.block.gpsimd
    def _(e):
        for ci in range(NCH):
            par = ci % 2
            if ci >= 2:
                e.wait_ge(p.sem("vv"), vva_base + (ci - 1) * 4)
            p.i(e.dma_start(out=STT.bitcast(F32)[:, par * 1024:(par + 1) * 1024]
                            if False else
                            STT[:, par * 1024:(par + 1) * 1024],
                            in_=pf_in[3][:, ci * T:(ci + 1) * T]),
                "pg%d" % par, 16)

    @p.block.vector
    def _(e):
        for ci in range(NCH):
            par = ci % 2
            e.wait_ge(p.sem("pl%d" % par), 64 * (ci // 2 + 1))
            e.wait_ge(p.sem("pg%d" % par), 16 * (ci // 2 + 1))
            if ci == 0:
                e.wait_ge(p.sem("vv"), vv_rwh)
            dst = X4[:, ci * T:(ci + 1) * T]
            p.i(e.tensor_add(dst, PIO[:, par * 4096 + 3072: par * 4096 + 4096],
                             PIO[:, par * 4096: par * 4096 + 1024]), "vv", 1)
            e.wait_ge(p.sem("vv"), p.c("vv"))
            for k in range(1, 3):
                p.i(e.tensor_add(dst, dst,
                                 PIO[:, par * 4096 + k * 1024:
                                     par * 4096 + (k + 1) * 1024]), "vv", 1)
                e.wait_ge(p.sem("vv"), p.c("vv"))
            p.i(e.tensor_add(dst, dst, STT[:, par * 1024:(par + 1) * 1024]),
                "vv", 1)
            e.wait_ge(p.sem("vv"), p.c("vv"))
    vv_xout = p.c("vv")

    mtx_base = p.c("mt")
    vvx_base = p.c("vv")
    xs_base = [p.c("xs0"), p.c("xs1")]

    @p.block.tensor
    def _(e):
        e.wait_ge(p.sem("vv"), vv_xout)
        for tt in range(8):
            for ci in range(NCH):
                tix = tt * NCH + ci
                if tix >= 2:
                    e.wait_ge(p.sem("vv"), vvx_base + tix - 1)
                p.i(e.transpose(bk(p, tix % 2, 128),
                                X4[:, ci * T + tt * 128: ci * T + (tt + 1) * 128],
                                eye_f[:, :]), "mt", 1)

    @p.block.vector
    def _(e):
        for tt in range(8):
            for ci in range(NCH):
                tix = tt * NCH + ci
                e.wait_ge(p.sem("mt"), mtx_base + tix + 1)
                if ci == 0 and tt >= 2:
                    e.wait_ge(p.sem("xs%d" % (tt % 2)),
                              xs_base[tt % 2] + 16 * (tt // 2))
                p.i(e.tensor_copy(TMB[tt % 2][:, ci * 128:(ci + 1) * 128],
                                  bk(p, tix % 2, 128)), "vv", 1)
            if True:
                e.wait_ge(p.sem("vv"), p.c("vv"))
                p.i(e.engine_nop(), "xt", 1)

    @p.block.gpsimd
    def _(e):
        for tt in range(8):
            e.wait_ge(p.sem("xt"), tt + 1)
            p.i(e.dma_start(out=xout_tm[tt * 128:(tt + 1) * 128, :],
                            in_=TMB[tt % 2][:, :]), "xs%d" % (tt % 2), 16)
    xs_done = (p.c("xs0"), p.c("xs1"))

    vvb_base = p.c("vv")

    @p.block.sync
    def _(e):
        e.wait_ge(p.sem("xs0"), xs_done[0])
        e.wait_ge(p.sem("xs1"), xs_done[1])
        for ci in range(NCH):
            par = ci % 2
            if ci >= 2:
                e.wait_ge(p.sem("vv"), vvb_base + ci - 1)
            p.i(e.dma_start(out=PIO[:, par * 4096: par * 4096 + 1024],
                            in_=x3_in[:, ci * T:(ci + 1) * T]), "xb%d" % par, 16)
            p.i(e.dma_start(
                out=PIO[:, par * 4096 + 1024: par * 4096 + 2048].bitcast(BF16)
                [:, 0:1024],
                in_=lnx3_in[:, ci * T:(ci + 1) * T]), "xb%d" % par, 16)

    @p.block.vector
    def _(e):
        for ci in range(NCH):
            par = ci % 2
            e.wait_ge(p.sem("xb%d" % par), 32 * (ci // 2 + 1))
            p.i(e.scalar_tensor_tensor(
                X4[:, ci * T:(ci + 1) * T],
                PIO[:, par * 4096 + 1024: par * 4096 + 2048].bitcast(BF16)[:, 0:1024],
                0.5, PIO[:, par * 4096: par * 4096 + 1024],
                OP.mult, OP.add), "vv", 1)
            e.wait_ge(p.sem("vv"), p.c("vv"))
    vv_x4b = p.c("vv")

    vvS_base = p.c("vv")
    for jj in range(8):
        @p.block.gpsimd
        def _(e):
            if jj == 0:
                e.wait_ge(p.sem("vv"), vv_x4b)
            else:
                e.wait_ge(p.sem("vv"), p.c("vv"))
            p.i(e.memset(GX[:, :], 0.0), "gm", 1)
            e.wait_ge(p.sem("gm"), p.c("gm"))
            p.i(e.indirect_dma_start(
                GX[:, :], None, xout_tm[:, :],
                IndirectOffsetOnAxis(ROFI[:, jj:jj + 1], 0),
                bounds_check=TOPK - 1, oob_is_err=False,
                compute_op=OP.add), "gx", 16)
        gx_done = p.c("gx")

        @p.block.vector
        def _(e):
            e.wait_ge(p.sem("gx"), gx_done)
            p.i(e.tensor_scalar(GX[:, :], GX[:, :], ROFS[:, 8 + jj:9 + jj], None,
                                OP.mult), "vv", 1)
        vv_gx = p.c("vv")

        @p.block.tensor
        def _(e):
            e.wait_ge(p.sem("vv"), vv_gx)
            for ci in range(NCH):
                tix = jj * NCH + ci
                if tix >= 2:
                    k = tix - 2
                    e.wait_ge(p.sem("vv"), vvS_base + k + k // NCH + 2)
                p.i(e.transpose(bk(p, 2 + tix % 2, 128), GX[:, ci * 128:(ci + 1) * 128],
                                eye_f[:, :]), "mg", 1)

        mg_c = p.c("mg")

        @p.block.vector
        def _(e):
            for ci in range(NCH):
                tix = jj * NCH + ci
                e.wait_ge(p.sem("mg"), mg_c - NCH + ci + 1)
                p.i(e.tensor_add(
                    X4[:, ci * T + jj * 128: ci * T + (jj + 1) * 128],
                    X4[:, ci * T + jj * 128: ci * T + (jj + 1) * 128],
                    bk(p, 2 + tix % 2, 128)), "vv", 1)
    vv_x4 = p.c("vv")

    @p.block.sync
    def _(e):
        e.wait_ge(p.sem("vv"), vv_x4)
        p.i(e.dma_start(out=x4_out[:, :], in_=X4[:, :]), "so", 16)
    so_x4 = p.c("so")

    if not qkv:
        @p.block.sync
        def _(e):
            e.wait_ge(p.sem("so"), p.c("so"))
        p.finish()
        return p

    @p.block.sync
    def _(e):
        e.wait_ge(p.sem("so"), so_x4)
    vv_ln4 = emit_ln_fm(p, x4_out, BIGC, LNB, SQ, STT)

    @p.block.gpsimd
    def _(e):
        for b_ in range(3):
            if b_:
                e.wait_ge(p.sem("wcq"), p.c("wcq"))
            p.i(e.dma_start(out=wqkv_bf[:, b_ * D:(b_ + 1) * D],
                            in_=wqkv_in[:, b_ * D:(b_ + 1) * D]), "wcq", 16)
        p.i(e.dma_start(out=RT2[:, 0:1024], in_=cosa_in[:, :]), "rt", 16)
        p.i(e.dma_start(out=RT2[:, 1024:2048], in_=sina_in[:, :]), "rt", 16)
        p.i(e.dma_start(out=GMS[:, 0:1], in_=gq_in[:, :]), "rt", 16)
        p.i(e.dma_start(out=GMS[:, 1:2], in_=gqr_in[:, :]), "rt", 16)
        p.i(e.dma_start(out=GMS[:, 2:3], in_=gk_in[:, :]), "rt", 16)
        p.i(e.dma_start(out=GMS[:, 3:4], in_=gkr_in[:, :]), "rt", 16)
    rt_done = p.c("rt")

    def mk_qkraw(col0, dst_dram):
        def load(e, og, ch, par):
            e.wait_ge(p.sem("wcq"), 16 * ((col0 + (og + 1) * 256 + D - 1) // D))
            return e.dma_start(
                out=p.wt[:, par * 1024: par * 1024 + 256],
                in_=wqkv_bf[ch * 128:(ch + 1) * 128,
                            col0 + og * 256: col0 + (og + 1) * 256])

        def mms(e, og, ch, par, first, last):
            b = (og % 2) * 4
            for hd in range(2):
                wti = p.wt[:, par * 1024 + hd * 128: par * 1024 + hd * 128 + 128]
                for tc in range(2):
                    mm = e.matmul(bk(p, b + hd * 2 + tc), wti,
                                  LNB[:, ch * T + tc * 512: ch * T + tc * 512 + 512],
                                  start=first, stop=last)
            return mm

        def evac(og):
            def run(mp_target):
                @p.block.vector
                def _(e):
                    e.wait_ge(p.sem("mp"), mp_target)
                    e.wait_ge(p.sem("qs"), p.c("qs"))
                    b = (og % 2) * 4
                    for hd in range(2):
                        p.i(e.tensor_copy(
                            HT[:, hd * 1024:(hd + 1) * 1024],
                            p.psf[:, (b + hd * 2) * 512:(b + hd * 2 + 2) * 512]),
                            "ev", 1 if hd == 1 else 0)
                vv_t = p.c("ev")

                @p.block.gpsimd
                def _(e):
                    e.wait_ge(p.sem("ev"), vv_t)
                    p.i(e.dma_start(out=dst_dram[:, og * 2048:(og + 1) * 2048],
                                    in_=HT[:, :]), "qs", 16)
            return run
        return load, mms, [evac(g) for g in range(8)]

    lq, mq, eq = mk_qkraw(0, qraw)
    emit_proj_chstream(p, 8, NCH, lq, mq, eq, psum_lag=2, rdy=[("vv", vv_ln4)])
    lk, mk2, ek = mk_qkraw(D, kraw)
    emit_proj_chstream(p, 8, NCH, lk, mk2, ek, psum_lag=2)

    VT = BIGC

    def v_load(e, og, ch, par):
        e.wait_ge(p.sem("wcq"), 48)
        return e.dma_start(
            out=p.wt[:, par * 1024: par * 1024 + 256],
            in_=wqkv_bf[ch * 128:(ch + 1) * 128,
                        2 * D + og * 256: 2 * D + (og + 1) * 256])

    def v_mms(e, og, ch, par, first, last):
        wti = p.wt[:, par * 1024: par * 1024 + 256]
        for tt in range(8):
            mm = e.matmul(p.psf[:, tt * 512: tt * 512 + 256],
                          LNB[:, ch * T + tt * 128: ch * T + tt * 128 + 128], wti,
                          start=first, stop=last)
        return mm

    def v_evac(og):
        def run(mp_target):
            @p.block.vector
            def _(e):
                e.wait_ge(p.sem("mp"), mp_target)
                for tt in range(8):
                    p.i(e.tensor_copy(
                        VT[:, tt * 2048 + og * 256: tt * 2048 + og * 256 + 256],
                        p.psf[:, tt * 512: tt * 512 + 256]),
                        "ev", 1 if tt == 7 else 0)
        return run

    emit_proj_chstream(p, 8, NCH, v_load, v_mms, [v_evac(g) for g in range(8)],
                       psum_lag=1)
    ev_v = p.c("ev")

    @p.block.sync
    def _(e):
        e.wait_ge(p.sem("ev"), ev_v)
        for tt in range(8):
            p.i(e.dma_start(out=v_out[tt * 128:(tt + 1) * 128, :],
                            in_=VT[:, tt * 2048:(tt + 1) * 2048]), "so", 16)

    nbase = dict(mm=p.c("mm"), sc=p.c("sc"), vv=p.c("vv"), hl0=p.c("hl0"),
                 hl1=p.c("hl1"), qf0=p.c("qf0"), qf1=p.c("qf1"))
    qs_all = p.c("qs")

    @p.block.sync
    def _(e):
        e.wait_ge(p.sem("qs"), qs_all)
        for i32 in range(32):
            par = i32 % 2
            if i32 >= 2:
                e.wait_ge(p.sem("vv"), nbase["vv"] + (i32 - 1) * 8)
            src_ = qraw if i32 < 16 else kraw
            h = i32 % 16
            p.i(e.dma_start(out=HT[:, par * 1024:(par + 1) * 1024],
                            in_=src_[:, h * 1024:(h + 1) * 1024]),
                "hl%d" % par, 16)

    for i32 in range(32):
        par = i32 % 2
        h = i32 % 16
        isq = i32 < 16
        gcol = 0 if isq else 2
        dst_dram = q_out if isq else k_out
        nbcol = i32

        @p.block.scalar
        def _(e):
            e.wait_ge(p.sem("hl%d" % par), 16 * (i32 // 2 + 1))
            if i32 >= 1:
                e.wait_ge(p.sem("mm"), nbase["mm"] + 6 * i32)
            p.i(e.activation(SQ[:, :], HT[:, par * 1024:(par + 1) * 1024],
                             AF.Square), "sc", 1)
        sc1 = p.c("sc")

        @p.block.tensor
        def _(e):
            e.wait_ge(p.sem("sc"), sc1)
            if i32 >= 1:
                e.wait_ge(p.sem("vv"), nbase["vv"] + (i32 - 1) * 8 + 7)
            for tc in range(2):
                p.i(e.matmul(bk(p, 4 + tc), ones_bf[:, :],
                             SQ[:, tc * 512:(tc + 1) * 512],
                             start=True, stop=True), "mm", 1)
        mm1 = p.c("mm")

        @p.block.scalar
        def _(e):
            e.wait_ge(p.sem("mm"), mm1)
            p.i(e.activation(RSD[:, :], p.psf[:, 4 * 512:4 * 512 + 1024],
                             AF.Sqrt, scale=1.0 / DH, bias=EPS_RMS), "sc", 1)
        sc2 = p.c("sc")

        @p.block.tensor
        def _(e):
            e.wait_ge(p.sem("sc"), sc1)
            if i32 >= 1:
                e.wait_ge(p.sem("vv"), nbase["vv"] + (i32 - 1) * 8 + 3)
            for tc in range(2):
                p.i(e.matmul(bk(p, 6 + tc), p64[:, :],
                             HT[:, par * 1024 + tc * 512:
                                par * 1024 + (tc + 1) * 512],
                             start=True, stop=True), "mm", 1)
        mm2 = p.c("mm")

        @p.block.vector
        def _(e):
            e.wait_ge(p.sem("sc"), sc2)
            p.i(e.reciprocal(RSD[:, :], RSD[:, :]), "vv", 1)
            e.wait_ge(p.sem("vv"), p.c("vv"))
            e.wait_ge(p.sem("rt"), rt_done)
            p.i(e.scalar_tensor_tensor(TMPA[:, :],
                                       HT[:, par * 1024:(par + 1) * 1024],
                                       GMS[:, gcol:gcol + 1], RT2[:, 0:1024],
                                       OP.mult, OP.mult), "vv", 1)
            e.wait_ge(p.sem("mm"), mm2)
            p.i(e.scalar_tensor_tensor(TMPB[:, :],
                                       p.psf[:, 6 * 512:6 * 512 + 1024],
                                       GMS[:, gcol + 1:gcol + 2], RT2[:, 1024:2048],
                                       OP.mult, OP.mult), "vv", 1)
            e.wait_ge(p.sem("vv"), p.c("vv"))
            p.i(e.tensor_add(TMPA[:, :], TMPA[:, :], TMPB[:, :]), "vv", 1)
            e.wait_ge(p.sem("vv"), p.c("vv"))
            if i32 >= 2:
                e.wait_ge(p.sem("qf%d" % par),
                          nbase["qf%d" % par] + 16 * (i32 // 2))
            p.i(e.tensor_mul(HT2[:, par * 1024:(par + 1) * 1024], TMPA[:, :],
                             RSD[:, :]), "vv", 1)
            e.wait_ge(p.sem("vv"), p.c("vv"))
        sc3_b = p.c("sc")

        @p.block.scalar
        def _(e):
            e.wait_ge(p.sem("vv"), p.c("vv"))
            p.i(e.activation(SQ[:, :], HT2[:, par * 1024:(par + 1) * 1024],
                             AF.Square), "sc", 1)
        sc3 = p.c("sc")

        @p.block.tensor
        def _(e):
            e.wait_ge(p.sem("sc"), sc3)
            for tc in range(2):
                p.i(e.matmul(bk(p, 4 + tc), ones_bf[:, :],
                             SQ[:, tc * 512:(tc + 1) * 512],
                             start=True, stop=True), "mm", 1)
        mm3 = p.c("mm")

        @p.block.vector
        def _(e):
            e.wait_ge(p.sem("mm"), mm3)
            p.i(e.reduce_max(NB[0:1, 32:33], bk(p, 4)[0:1, :],
                             axis=mybir.AxisListType.X), "vv", 1)
            p.i(e.reduce_max(NB[0:1, 33:34], bk(p, 5)[0:1, :],
                             axis=mybir.AxisListType.X), "vv", 1)
            e.wait_ge(p.sem("vv"), p.c("vv"))
            p.i(e.tensor_max(NB[0:1, nbcol:nbcol + 1], NB[0:1, 32:33],
                             NB[0:1, 33:34]), "vv", 1)
            e.wait_ge(p.sem("vv"), p.c("vv"))
        vv_h = p.c("vv")

        @p.block.gpsimd
        def _(e):
            e.wait_ge(p.sem("vv"), vv_h)
            p.i(e.dma_start(out=dst_dram[:, h * 1024:(h + 1) * 1024],
                            in_=HT2[:, par * 1024:(par + 1) * 1024]),
                "qf%d" % par, 16)
    vv_nb_all = p.c("vv")

    @p.block.sync
    def _(e):
        e.wait_ge(p.sem("vv"), vv_nb_all)
        p.i(e.dma_start(out=nrm_out[:, :], in_=NB[0:1, 0:32]), "so", 16)
        e.wait_ge(p.sem("so"), p.c("so"))
        e.wait_ge(p.sem("qf0"), p.c("qf0"))
        e.wait_ge(p.sem("qf1"), p.c("qf1"))

    p.finish()
    return p


_PROG_CACHE = {}


def get_prog(name, builder, *a, **k):
    if name not in _PROG_CACHE:
        _PROG_CACHE[name] = builder(*a, **k)
    return _PROG_CACHE[name]


def _launch(p, in_maps):
    res = run_bass_kernel_spmd(p.nc, in_maps, core_ids=list(range(8)))
    return res.results


def _rope_tables(r):
    pos = np.arange(r, r + T, dtype=np.float64)
    inv = 1.0 / (10000.0 ** (np.arange(0, DH, 2, dtype=np.float64) / DH))
    emb = pos[:, None] * inv[None, :]
    cosa = np.empty((128, T), np.float32)
    sina = np.empty((128, T), np.float32)
    for pp in range(128):
        cosa[pp] = np.cos(emb[:, pp % 64])
        sina[pp] = np.sin(emb[:, pp % 64]) * (-1.0 if pp < 64 else 1.0)
    return cosa, sina


def kernel(L0_attn_x, L0_attn_W_K, L0_attn_W_Q, L0_attn_W_V, L0_attn_W_O,
           L0_ffn_w_gate_up, L0_ffn_w_down, L1_attn_w_qkv, L1_attn_w_out,
           L1_attn_q_gamma, L1_attn_k_gamma, L1_ffn_mask, L1_ffn_freqs_cis,
           L1_ffn_w_router, L1_ffn_g1, L1_ffn_wq, L1_ffn_wk, L1_ffn_wv,
           L1_ffn_wo, L1_ffn_g2, L1_ffn_w1, L1_ffn_w3, L1_ffn_w2):
    x = np.ascontiguousarray(np.asarray(L0_attn_x, dtype=np.float32))
    wk0 = np.ascontiguousarray(np.asarray(L0_attn_W_K).reshape(D, D))
    wq0 = np.ascontiguousarray(np.asarray(L0_attn_W_Q).reshape(D, D))
    wv0 = np.ascontiguousarray(np.asarray(L0_attn_W_V).reshape(D, D))
    wo0 = np.ascontiguousarray(np.asarray(L0_attn_W_O))
    wgu = np.ascontiguousarray(np.asarray(L0_ffn_w_gate_up))
    wdn = np.ascontiguousarray(np.asarray(L0_ffn_w_down))
    wqkv1 = np.ascontiguousarray(np.asarray(L1_attn_w_qkv))
    wo1 = np.ascontiguousarray(np.asarray(L1_attn_w_out).T)
    gq = np.asarray(L1_attn_q_gamma, dtype=np.float32)
    gk = np.asarray(L1_attn_k_gamma, dtype=np.float32)
    maskT = np.ascontiguousarray(np.asarray(L1_ffn_mask).T)
    freqs = np.asarray(L1_ffn_freqs_cis, dtype=np.float32)
    wr_fm = np.ascontiguousarray(
        to_fm(np.asarray(L1_ffn_w_router, np.float32)[None, :]).reshape(128, 16))
    g1_fm = np.ascontiguousarray(
        to_fm(np.asarray(L1_ffn_g1, np.float32)[None, :]).reshape(128, 16))
    g2_fm = np.ascontiguousarray(
        to_fm(np.asarray(L1_ffn_g2, np.float32)[None, :]).reshape(128, 16))
    wq1 = np.ascontiguousarray(np.asarray(L1_ffn_wq))
    wk1 = np.ascontiguousarray(np.asarray(L1_ffn_wk))
    wv1 = np.ascontiguousarray(np.asarray(L1_ffn_wv))
    wo1f = np.ascontiguousarray(np.asarray(L1_ffn_wo))
    w1 = np.ascontiguousarray(np.asarray(L1_ffn_w1))
    w3 = np.ascontiguousarray(np.asarray(L1_ffn_w3))
    w2 = np.ascontiguousarray(np.asarray(L1_ffn_w2))

    cosr = np.empty((128, TOPK), np.float32)
    sinr = np.empty((128, TOPK), np.float32)
    for pp in range(128):
        cosr[pp] = freqs[:, pp % 64, 0]
        sinr[pp] = freqs[:, pp % 64, 1]

    def qi_of(c):
        return c % 4

    def b_of(c):
        return c // 4

    def ffn_attn_layer(x_cur, wk_, wq_, wv_, wo_, l1_style, wqkv_=None):
        pass

    pL1 = get_prog("L1", build_L1)
    cm1 = pL1.input_const_maps()
    ims = []
    for c in range(8):
        im = dict(cm1)
        im["x_fm"] = to_fm(x[b_of(c), qi_of(c) * T:(qi_of(c) + 1) * T])
        im["wgu"] = wgu
        im["wdn"] = wdn
        im["wk"] = wk0
        im["wq"] = wq0
        im["wv"] = wv0
        ims.append(im)
    r1 = _launch(pL1, ims)

    def attn_launch(r_prev, wo_, with_ffn, prog_name):
        pA = get_prog(prog_name, build_L2, with_ffn=with_ffn)
        cmA = pA.input_const_maps()
        kfull = {}
        vfull = {}
        qkn = {}
        for b in range(2):
            ks = [r_prev[4 * b + q]["k_fm"].reshape(128, H, T) for q in range(4)]
            kfull[b] = np.concatenate(ks, axis=2)
            vs = [r_prev[4 * b + q]["v_tm"] for q in range(4)]
            vfull[b] = np.concatenate(vs, axis=0)
            qkn[b] = np.concatenate(
                [r_prev[4 * b + q]["qknorm"] for q in range(4)], axis=1)
        ims = []
        for c in range(8):
            b, qi = b_of(c), qi_of(c)
            r = 1024 * qi
            im = dict(cmA)
            im["c_diag_bf"] = CONSTS["diag_bf"]
            im["x1_fm"] = r_prev[c]["x1_fm"]
            im["q_fm"] = r_prev[c]["q_fm"]
            im["kin"] = host_rotate_k(kfull[b], r)
            im["vin"] = host_rotate_v(vfull[b], r)
            im["cbias"] = host_cbias(r)
            im["qkn"] = qkn[b]
            im["wo"] = wo_
            im["wr_fm"] = wr_fm
            if with_ffn:
                im["wgu"] = wgu
                im["wdn"] = wdn
            ims.append(im)
        return _launch(pA, ims)

    r2 = attn_launch(r1, wo0, True, "L2")

    def ffn1_step(r_att, key):
        lgfull = {}
        lnxtm = {}
        for b in range(2):
            lgfull[b] = np.concatenate(
                [r_att[4 * b + q]["logits"] for q in range(4)], axis=1)
            lnxtm[b] = np.concatenate(
                [from_fm(r_att[4 * b + q]["lnx3_fm"], T) for q in range(4)],
                axis=0)
        pL3 = get_prog("L3", build_L3)
        cm3 = pL3.input_const_maps()
        ims = []
        for c in range(8):
            b, qi = b_of(c), qi_of(c)
            im = dict(cm3)
            im["logits"] = lgfull[b]
            im["lnx_tm"] = lnxtm[b]
            im["g1_fm"] = g1_fm
            im["wqs"] = np.ascontiguousarray(wq1[:, qi * 512:(qi + 1) * 512])
            im["wks"] = np.ascontiguousarray(wk1[:, qi * 512:(qi + 1) * 512])
            im["wvs"] = np.ascontiguousarray(wv1[:, qi * 512:(qi + 1) * 512])
            im["wos"] = np.ascontiguousarray(wo1f[qi * 512:(qi + 1) * 512, :])
            im["maskT"] = maskT
            im["cosr"] = cosr
            im["sinr"] = sinr
            ims.append(im)
        r3 = _launch(pL3, ims)

        pL4 = get_prog("L4", build_L4)
        cm4 = pL4.input_const_maps()
        ims = []
        for c in range(8):
            b, qi = b_of(c), qi_of(c)
            im = dict(cm4)
            im["filt_fm"] = r3[4 * b]["filt_fm"]
            for i in range(4):
                im["pa%d" % i] = r3[4 * b + i]["partial"]
            im["g2_fm"] = g2_fm
            im["w1s"] = np.ascontiguousarray(w1[:, qi * D:(qi + 1) * D])
            im["w3s"] = np.ascontiguousarray(w3[:, qi * D:(qi + 1) * D])
            im["w2s"] = np.ascontiguousarray(w2[qi * D:(qi + 1) * D, :])
            ims.append(im)
        r4 = _launch(pL4, ims)
        return r3, r4

    r3, r4 = ffn1_step(r2, "f1")

    pL5 = get_prog("L5", build_L5, qkv=True)
    cm5 = pL5.input_const_maps()
    ims = []
    for c in range(8):
        b, qi = b_of(c), qi_of(c)
        r = 1024 * qi
        cosa, sina = _rope_tables(r)
        im = dict(cm5)
        im["x3_fm"] = r2[c]["x3_fm"]
        im["lnx3_fm"] = r2[c]["lnx3_fm"]
        im["logits"] = np.concatenate(
            [r2[4 * b + q]["logits"] for q in range(4)], axis=1)
        im["xaa"] = r4[4 * b]["xaa"]
        for i in range(4):
            im["pf%d" % i] = r4[4 * b + i]["pffn"]
        im["myoff"] = (np.arange(8, dtype=np.int32)[:, None] + r // 128)
        im["wqkv"] = wqkv1
        im["gq"] = gq[:, None]
        im["gqr"] = np.ascontiguousarray(np.roll(gq, -64)[:, None])
        im["gk"] = gk[:, None]
        im["gkr"] = np.ascontiguousarray(np.roll(gk, -64)[:, None])
        im["cosa"] = cosa
        im["sina"] = sina
        ims.append(im)
    r5 = _launch(pL5, ims)

    r5n = [dict(x1_fm=r5[c]["x4_fm"], q_fm=r5[c]["q_fm"], k_fm=r5[c]["k_fm"],
                v_tm=r5[c]["v_tm"], qknorm=r5[c]["qknorm"]) for c in range(8)]
    r6 = attn_launch(r5n, wo1, False, "L6")

    r7, r8 = ffn1_step(r6, "f2")

    pL9 = get_prog("L9", build_L5, qkv=False)
    cm9 = pL9.input_const_maps()
    ims = []
    for c in range(8):
        b, qi = b_of(c), qi_of(c)
        r = 1024 * qi
        im = dict(cm9)
        im["x3_fm"] = r6[c]["x3_fm"]
        im["lnx3_fm"] = r6[c]["lnx3_fm"]
        im["logits"] = np.concatenate(
            [r6[4 * b + q]["logits"] for q in range(4)], axis=1)
        im["xaa"] = r8[4 * b]["xaa"]
        for i in range(4):
            im["pf%d" % i] = r8[4 * b + i]["pffn"]
        im["myoff"] = (np.arange(8, dtype=np.int32)[:, None] + r // 128)
        ims.append(im)
    r9 = _launch(pL9, ims)

    out = np.empty((2, S, D), dtype=np.float32)
    for c in range(8):
        b, qi = b_of(c), qi_of(c)
        out[b, qi * T:(qi + 1) * T] = from_fm(r9[c]["x4_fm"], T)
    return out


def emit_attention(p, QB, kin, vin, cbias_in, qkn_in, OB, KS, VS, EB, CBH, CBC,
                   ZS, ZSB, OA, diag, diag_qc):
    ones_f = p.cs("ones_f")
    zind = p.cs("zind_bf")
    bsel = p.cs("bsel_bf")
    uq = str(p.cnt["uniq"]); p.cnt["uniq"] += 1
    QKN = p.sb("QKN" + uq, [1, 128], F32)
    CBIAS = p.sb("CBIAS" + uq, [128, 64], F32)

    @p.block.sync
    def _(e):
        p.i(e.dma_start(out=QKN[:, :], in_=qkn_in[:, :]), "ca", 16)
        p.i(e.dma_start(out=CBIAS[:, :], in_=cbias_in[:, :]), "ca", 16)
    cl_att = p.c("ca")

    @p.block.vector
    def _(e):
        e.wait_ge(p.sem("ca"), cl_att)
        for q in range(1, 4):
            p.i(e.tensor_max(QKN[0:1, 0:32], QKN[0:1, 0:32],
                             QKN[0:1, q * 32:(q + 1) * 32]), "vv", 1)
            e.wait_ge(p.sem("vv"), p.c("vv"))
        p.i(e.tensor_mul(QKN[0:1, 32:48], QKN[0:1, 0:16], QKN[0:1, 16:32]),
            "vv", 1)
    vv_b = p.c("vv")

    @p.block.scalar
    def _(e):
        e.wait_ge(p.sem("vv"), vv_b)
        e.wait_ge(p.sem("cl"), p.c("cl"))
        p.i(e.activation(QKN[0:1, 32:48], QKN[0:1, 32:48], AF.Sqrt), "sc", 1)
        e.wait_ge(p.sem("sc"), p.c("sc"))
        p.i(e.activation(QKN[0:1, 48:64], QKN[0:1, 32:48], AF.Relu,
                         scale=ISQ, bias=-40.0), "sc", 1)
    sc_b = p.c("sc")

    @p.block.tensor
    def _(e):
        e.wait_ge(p.sem("sc"), sc_b)
        for h in range(16):
            mm = e.matmul(bk(p, 7, 1, off=h), ones_f[0:1, :],
                          QKN[0:1, 48 + h:49 + h], start=True, stop=True)
        p.i(mm, "mm", 1)
    mm_b = p.c("mm")

    @p.block.vector
    def _(e):
        e.wait_ge(p.sem("mm"), mm_b)
        p.i(e.tensor_copy(CBC[:, :], bk(p, 7, 16)), "vv", 1)
        e.wait_ge(p.sem("vv"), p.c("vv"))
        for h in range(16):
            p.i(e.tensor_scalar(CBH[:, h * 64:(h + 1) * 64], CBIAS[:, :],
                                CBC[:, h:h + 1], None, OP.subtract), "vv", 1)
    vv_cbh = p.c("vv")

    steps = []
    groups = []
    j = 0
    for qc in range(2):
        for hg in range(4):
            for ktg in range(8):
                groups.append(dict(qc=qc, hg=hg, ktg=ktg, first_j=j))
                for kt in range(4):
                    ktglob = ktg * 4 + kt
                    for hh in range(4):
                        steps.append(dict(
                            j=j, qc=qc, hg=hg, ktg=ktg, kt=kt, hh=hh,
                            h=hg * 4 + hh, ktglob=ktglob, g=len(groups) - 1,
                            di=diag_qc(qc, ktglob)))
                        j += 1
    nj = len(steps)
    ndiag_upto = [0] * (nj + 1)
    for s in steps:
        ndiag_upto[s["j"] + 1] = ndiag_upto[s["j"]] + (s["di"] is not None)

    def ev_after_grp(grp):
        base = (grp + 1) * 4
        if grp >= 3:
            base += 1
        if grp >= 7:
            base += 1
        return base

    base_ka = [p.c("ka0"), p.c("ka1")]
    base_ms = p.c("ms")
    base_ma = p.c("ma")
    base_sc2 = p.c("sc")
    base_ev2 = p.c("ev")
    base_vd = p.c("vd")

    @p.block.sync
    def _(e):
        for g, gr in enumerate(groups):
            par = g % 2
            if g >= 2:
                e.wait_ge(p.sem("ma"), base_ma + (g - 1) * 16)
            qc, hg, ktg = gr["qc"], gr["hg"], gr["ktg"]
            for hh in range(4):
                h = hg * 4 + hh
                p.i(e.dma_start(
                    out=KS[:, par * 2048 + hh * 512: par * 2048 + (hh + 1) * 512],
                    in_=kin[:, h * 4096 + ktg * 512: h * 4096 + (ktg + 1) * 512]),
                    "ka%d" % par, 16)
            for kt in range(4):
                p.i(e.dma_start(
                    out=VS[:, par * 2048 + kt * 512: par * 2048 + (kt + 1) * 512],
                    in_=vin[hg, ktg * 512 + kt * 128: ktg * 512 + (kt + 1) * 128, :]),
                    "ka%d" % par, 16)

    @p.block.tensor
    def _(e):
        for s in steps:
            j_, g = s["j"], s["g"]
            par = g % 2
            if s["kt"] == 0 and s["hh"] == 0:
                e.wait_ge(p.sem("ka%d" % par), base_ka[par] + (g // 2 + 1) * 128)
                if j_ == 0:
                    e.wait_ge(p.sem("vv"), vv_cbh)
                if s["ktg"] == 0:
                    grp = s["qc"] * 4 + s["hg"]
                    if grp >= 1:
                        e.wait_ge(p.sem("ev"), base_ev2 + ev_after_grp(grp - 1))
            sb_ = 5 + (j_ % 2)
            if j_ >= 2:
                e.wait_ge(p.sem("sc"), base_sc2 + j_ - 1)
            p.i(e.matmul(
                bk(p, sb_),
                KS[:, par * 2048 + s["hh"] * 512 + s["kt"] * 128:
                   par * 2048 + s["hh"] * 512 + s["kt"] * 128 + 128],
                QB[:, s["h"] * 1024 + s["qc"] * 512: s["h"] * 1024 + s["qc"] * 512 + 512],
                start=True, stop=True), "ms", 1)
            e.wait_ge(p.sem("sc"), base_sc2 + j_ + 1)
            if s["di"] is not None:
                e.wait_ge(p.sem("vd"), base_vd + ndiag_upto[j_ + 1])
            eb = EB[:, (j_ % 4) * 512:(j_ % 4) * 512 + 512]
            e.matmul(bk(p, s["hh"]),
                     VS[:, par * 2048 + s["kt"] * 512 + s["hh"] * 128:
                        par * 2048 + s["kt"] * 512 + s["hh"] * 128 + 128],
                     eb, start=(s["ktg"] == 0 and s["kt"] == 0),
                     stop=(s["ktg"] == 7 and s["kt"] == 3))
            p.i(e.matmul(p.psf[0:16, 4 * 512: 4 * 512 + 512],
                         zind[:, s["h"] * 16: s["h"] * 16 + 16], eb,
                         start=(j_ % 512 == 0), stop=(j_ % 512 == 511)),
                "ma", 1)

    @p.block.scalar
    def _(e):
        for s in steps:
            j_ = s["j"]
            e.wait_ge(p.sem("ms"), base_ms + j_ + 1)
            if j_ >= 4:
                e.wait_ge(p.sem("ma"), base_ma + j_ - 3)
                if ndiag_upto[j_ - 3] > 0:
                    e.wait_ge(p.sem("vd"), base_vd + ndiag_upto[j_ - 3])
            if j_ == 0:
                e.wait_ge(p.sem("vv"), vv_cbh)
            bias = CBH[:, s["h"] * 64 + s["ktglob"] * 2 + s["qc"]:
                       s["h"] * 64 + s["ktglob"] * 2 + s["qc"] + 1]
            p.i(e.activation(EB[:, (j_ % 4) * 512:(j_ % 4) * 512 + 512],
                             bk(p, 5 + (j_ % 2)), AF.Exp, bias=bias, scale=ISQ),
                "sc", 1)

    @p.block.vector
    def _(e):
        for s in steps:
            j_ = s["j"]
            if s["di"] is not None:
                e.wait_ge(p.sem("sc"), base_sc2 + j_ + 1)
                p.i(e.tensor_mul(EB[:, (j_ % 4) * 512:(j_ % 4) * 512 + 512],
                                 EB[:, (j_ % 4) * 512:(j_ % 4) * 512 + 512],
                                 diag[:, s["di"] * 512:(s["di"] + 1) * 512]),
                    "vd", 1)
            if s["ktg"] == 7 and s["kt"] == 3 and s["hh"] == 3:
                e.wait_ge(p.sem("ma"), base_ma + j_ + 1)
                for hh in range(4):
                    h = s["hg"] * 4 + hh
                    p.i(e.tensor_copy(
                        OA[:, h * 1024 + s["qc"] * 512: h * 1024 + s["qc"] * 512 + 512],
                        bk(p, hh)), "ev", 1)
                if s["hg"] == 3:
                    p.i(e.tensor_copy(ZS[:, s["qc"] * 512:(s["qc"] + 1) * 512],
                                      p.psf[0:16, 4 * 512: 4 * 512 + 512]),
                        "ev", 1)
    ev_att = p.c("ev")

    @p.block.vector
    def _(e):
        e.wait_ge(p.sem("ev"), ev_att)
        p.i(e.reciprocal(ZS[:, :], ZS[:, :]), "vv", 1)
        e.wait_ge(p.sem("vv"), p.c("vv"))
        p.i(e.tensor_copy(ZSB[:, :], ZS[:, :]), "vv", 1)
        e.wait_ge(p.sem("vv"), p.c("vv"))
    vv_rz = p.c("vv")

    @p.block.tensor
    def _(e):
        e.wait_ge(p.sem("vv"), vv_rz)
        for h in range(16):
            for qc in range(2):
                idx = h * 2 + qc
                if idx >= 2:
                    e.wait_ge(p.sem("vv"), vv_rz + idx - 1)
                p.i(e.matmul(bk(p, 5 + (idx % 2)), bsel[:, h * 128:(h + 1) * 128],
                             ZSB[:, qc * 512:(qc + 1) * 512],
                             start=True, stop=True), "mz", 1)
    base_mz = p.c("mz") - 32

    @p.block.vector
    def _(e):
        for h in range(16):
            for qc in range(2):
                idx = h * 2 + qc
                e.wait_ge(p.sem("mz"), base_mz + idx + 1)
                p.i(e.tensor_mul(
                    OB[:, h * 1024 + qc * 512: h * 1024 + qc * 512 + 512],
                    OA[:, h * 1024 + qc * 512: h * 1024 + qc * 512 + 512],
                    bk(p, 5 + (idx % 2))), "vv", 1)
    return p.c("vv")


def emit_resid_evacs(p, n_og, x_src, x_dst, XIO, scale):
    def mk(og):
        def run(mp_target):
            for dt in range(2):
                d_idx = og * 2 + dt
                b = (og % 2) * 4 + dt * 2

                @p.block.gpsimd
                def _(e):
                    e.wait_ge(p.sem("vv"), p.c("vv"))
                    p.i(e.dma_start(
                        out=XIO[:, (d_idx % 2) * 1024:(d_idx % 2) * 1024 + 1024],
                        in_=x_src[:, d_idx * T:(d_idx + 1) * T]),
                        "xg%d" % (d_idx % 2), 16)

                @p.block.vector
                def _(e):
                    e.wait_ge(p.sem("xg%d" % (d_idx % 2)), p.c("xg%d" % (d_idx % 2)))
                    e.wait_ge(p.sem("mp"), mp_target)
                    e.wait_ge(p.sem("st%d" % (d_idx % 2)), p.c("st%d" % (d_idx % 2)))
                    p.i(e.scalar_tensor_tensor(
                        XIO[:, 2048 + (d_idx % 2) * 1024: 2048 + (d_idx % 2) * 1024 + 1024],
                        p.psf[:, b * 512:(b + 2) * 512], scale,
                        XIO[:, (d_idx % 2) * 1024:(d_idx % 2) * 1024 + 1024],
                        OP.mult, OP.add), "vv", 1)
                    if dt == 1:
                        p.i(e.engine_nop(), "ev", 1)
                vv_t = p.c("vv")

                @p.block.gpsimd
                def _(e):
                    e.wait_ge(p.sem("vv"), vv_t)
                    p.i(e.dma_start(
                        out=x_dst[:, d_idx * T:(d_idx + 1) * T],
                        in_=XIO[:, 2048 + (d_idx % 2) * 1024: 2048 + (d_idx % 2) * 1024 + 1024]),
                        "st%d" % (d_idx % 2), 16)
        return run
    return [mk(g) for g in range(n_og)]


def build_L2(with_ffn=True):
    p = Prog("L2" if with_ffn else "L6")
    x1_in = p.din("x1_fm", [128, NCH * T], F32)
    q_in = p.din("q_fm", [128, H * T], BF16)
    kin = p.din("kin", [128, H * S], BF16)
    vin = p.din("vin", [4, S, 512], BF16)
    cbias_in = p.din("cbias", [128, 64], F32)
    qkn_in = p.din("qkn", [1, 128], F32)
    wo_in = p.din("wo", [D, D], F32)
    wr_in = p.din("wr_fm", [128, 16], F32)
    if with_ffn:
        wgu = p.din("wgu", [2 * INTER, D], F32)
        wdn = p.din("wdn", [D, INTER], F32)
        wgu_bf = p.dint("wgu_bf", [2 * INTER, D], BF16)
        wdn_bf = p.dint("wdn_bf", [D, INTER], BF16)

    x3_out = p.dout("x3_fm", [128, NCH * T], F32)
    lnx3_out = p.dout("lnx3_fm", [128, NCH * T], BF16)
    lg_out = p.dout("logits", [1, T], F32)

    wo_bf = p.dint("wo_bf", [D, D], BF16)
    x2_dram = x3_out if not with_ffn else p.dint("x2_fm", [128, NCH * T], F32)

    XB = p.sb("XB", [128, 16384], BF16)
    LNB = p.sb("LNB", [128, 16384], BF16)
    HH = p.sb("HH", [128, 49152], BF16)
    SQ = p.sb("SQ", [128, 1024], BF16)
    STT = p.sb("STT", [128, 2048], F32)
    GB = p.sb("GB", [128, 1024], BF16)
    XIO = p.sb("XIO", [128, 4096], F32)
    WR = p.sb("WR", [128, 16], BF16)

    p.use_consts(["ones_bf", "ones_f", "zind_bf", "bsel_bf"])
    diag_in = p.din("c_diag_bf", [128, 2048], BF16)
    p.use_fconsts([EPS_LN, -40.0])
    p.begin()
    p.load_consts()

    QB = HH[:, 0:16384]
    OA = HH[:, 16384:32768]
    OB = HH[:, 32768:49152]
    KS = XB[:, 0:4096]
    VS = XB[:, 4096:8192]
    EB = XB[:, 8192:10240]
    CBH = STT[:, 0:1024]
    CBC = STT[:, 1024:1040]
    DIAG = XB[:, 10240:12288]
    ZS = XIO[0:16, 0:1024]
    ZSB = XIO[0:16, 1024:1536].bitcast(BF16)
    LG = STT[0:1, 0:1024]

    @p.block.gpsimd
    def _(e):
        p.i(e.dma_start(out=wo_bf[:, :], in_=wo_in[:, :]), "wcwo", 16)
        p.i(e.dma_start(out=WR[:, :], in_=wr_in[:, :]), "wcwr", 16)
        if with_ffn:
            e.wait_ge(p.sem("wcwo"), 16)
            for blk in range(4):
                if blk:
                    e.wait_ge(p.sem("wcgu"), p.c("wcgu"))
                for g in range(blk * 16, blk * 16 + 16):
                    p.i(e.dma_start(out=wgu_bf[g * 256: g * 256 + 128, :],
                                    in_=wgu[g * 128: (g + 1) * 128, :]), "wcgu", 16)
                    p.i(e.dma_start(out=wgu_bf[g * 256 + 128: g * 256 + 256, :],
                                    in_=wgu[INTER + g * 128: INTER + (g + 1) * 128, :]),
                        "wcgu", 16)
            e.wait_ge(p.sem("wcgu"), p.c("wcgu"))
            for b in range(2):
                if b:
                    e.wait_ge(p.sem("wcdn"), p.c("wcdn"))
                p.i(e.dma_start(out=wdn_bf[b * 1024:(b + 1) * 1024, :],
                                in_=wdn[b * 1024:(b + 1) * 1024, :]), "wcdn", 16)

    @p.block.sync
    def _(e):
        p.i(e.dma_start(out=QB, in_=q_in[:, :]), "cq", 16)
        p.i(e.dma_start(out=DIAG, in_=diag_in[:, :]), "cq", 16)
    cl_q = p.c("cq")

    @p.block.tensor
    def _(e):
        e.wait_ge(p.sem("cq"), cl_q)

    @p.block.vector
    def _(e):
        e.wait_ge(p.sem("cq"), cl_q)

    def diag_qc(qc, ktglob):
        lo = qc * 4
        return ktglob - lo if lo <= ktglob < lo + 4 else None

    emit_attention(p, QB, kin, vin, cbias_in, qkn_in, OB, KS, VS, EB, CBH, CBC,
                   ZS, ZSB, OA, DIAG, diag_qc)
    vv_att = p.c("vv")

    def wo_load(e, og, ch, par):
        e.wait_ge(p.sem("wcwo"), 16)
        return e.dma_start(
            out=p.wt[:, par * 1024: par * 1024 + 256],
            in_=wo_bf[og * 256:(og + 1) * 256, ch * 128:(ch + 1) * 128],
            transpose=True)

    def wo_mms(e, og, ch, par, first, last):
        b = (og % 2) * 4
        for dt in range(2):
            wti = p.wt[:, par * 1024 + dt * 128: par * 1024 + dt * 128 + 128]
            for tc in range(2):
                mm = e.matmul(bk(p, b + dt * 2 + tc), wti,
                              OB[:, ch * T + tc * 512: ch * T + tc * 512 + 512],
                              start=first, stop=last)
        return mm

    emit_proj_chstream(p, 8, NCH, wo_load, wo_mms,
                       emit_resid_evacs(p, 8, x1_in, x2_dram, XIO, 1.0),
                       psum_lag=2, rdy=[("vv", vv_att)])
    st_x2 = (p.c("st0"), p.c("st1"))

    @p.block.sync
    def _(e):
        e.wait_ge(p.sem("st0"), st_x2[0])
        e.wait_ge(p.sem("st1"), st_x2[1])

    if with_ffn:
        vv_ln2 = emit_ln_fm(p, x2_dram, XB, LNB, SQ, STT)
        st_x3 = emit_ffn(p, x2_dram, x3_out, wgu_bf, wdn_bf, XB, LNB, HH, SQ,
                         STT, GB, XIO, vv_ln2)

        @p.block.sync
        def _(e):
            e.wait_ge(p.sem("st0"), st_x3[0])
            e.wait_ge(p.sem("st1"), st_x3[1])
        vv_ln3 = emit_ln_fm(p, x3_out, XB, LNB, SQ, STT)
    else:
        vv_ln3 = emit_ln_fm(p, x2_dram, XB, LNB, SQ, STT)

    cl_wr = p.c("wcwr")
    mr_base = p.c("mr")

    @p.block.tensor
    def _(e):
        e.wait_ge(p.sem("wcwr"), cl_wr)
        e.wait_ge(p.sem("vv"), vv_ln3)
        e.wait_ge(p.sem("ev"), p.c("ev"))
        for tc in range(2):
            for ch in range(NCH):
                mm = e.matmul(p.psf[0:1, tc * 512:(tc + 1) * 512],
                              WR[:, ch:ch + 1],
                              LNB[:, ch * T + tc * 512: ch * T + tc * 512 + 512],
                              start=(ch == 0), stop=(ch == NCH - 1))
            p.i(mm, "mr", 1)

    @p.block.vector
    def _(e):
        e.wait_ge(p.sem("mr"), mr_base + 2)
        p.i(e.tensor_copy(LG[:, :], p.psf[0:1, 0:1024]), "vv", 1)
    vv_lg = p.c("vv")

    @p.block.sync
    def _(e):
        e.wait_ge(p.sem("vv"), vv_ln3)
        p.i(e.dma_start(out=lnx3_out[:, :], in_=LNB[:, :]), "so", 16)
        e.wait_ge(p.sem("vv"), vv_lg)
        p.i(e.dma_start(out=lg_out[:, :], in_=LG[:, :]), "so", 16)
        e.wait_ge(p.sem("so"), p.c("so"))
        e.wait_ge(p.sem("st0"), p.c("st0"))
        e.wait_ge(p.sem("st1"), p.c("st1"))

    p.finish()
    return p


# version 04 (final)
# speedup vs baseline: 1.0349x; 1.0349x over previous
import math
from collections import defaultdict
from contextlib import ExitStack

import numpy as np
import ml_dtypes
import concourse.bass as bass
import concourse.mybir as mybir
from concourse.bass import IndirectOffsetOnAxis
from concourse.bass_utils import run_bass_kernel_spmd

F32 = mybir.dt.float32
BF16 = mybir.dt.bfloat16
I32 = mybir.dt.int32
AF = mybir.ActivationFunctionType
OP = mybir.AluOpType
BF = ml_dtypes.bfloat16

D = 2048
NCH = D // 128
T = 1024
S = 4096
H = 16
DH = 128
INTER = 8192
EPS_LN = 1e-5
EPS_RMS = 1e-6
ISQ = 1.0 / math.sqrt(DH)
TOPK = 1024
NEG = -30000.0


def to_fm(x_tm):
    t = x_tm.shape[0]
    return np.ascontiguousarray(
        x_tm.reshape(t, NCH, 128).transpose(2, 1, 0).reshape(128, NCH * t))


def from_fm(x_fm, t):
    return np.ascontiguousarray(
        x_fm.reshape(128, NCH, t).transpose(2, 1, 0).reshape(t, D))


def host_consts():
    c = {}
    c["ones_bf"] = np.ones((128, 128), dtype=BF)
    c["ones_f"] = np.ones((128, 128), dtype=np.float32)
    c["eye_bf"] = np.eye(128, dtype=BF)
    c["eye_f"] = np.eye(128, dtype=np.float32)
    p64 = np.zeros((128, 128), dtype=BF)
    for m in range(128):
        p64[(m + 64) % 128, m] = 1
    c["p64_bf"] = p64
    ind = np.zeros((128, 16 * 16), dtype=BF)
    for h in range(16):
        ind[:, h * 16 + h] = 1
    c["zind_bf"] = ind
    bsel = np.zeros((16, 16 * 128), dtype=BF)
    for h in range(16):
        bsel[h, h * 128:(h + 1) * 128] = 1
    c["bsel_bf"] = bsel
    io = np.zeros((128, 8), dtype=np.float32)
    for pp in range(128):
        for jj in range(8):
            io[pp, jj] = jj * 128 + pp
    c["iotasel_f"] = io
    sg = np.ones((128, 1), dtype=np.float32)
    sg[0:64] = -1.0
    c["sgn_f"] = sg
    dm = np.zeros((128, 4 * 512), dtype=BF)
    for tt in range(4):
        for pp in range(128):
            s0 = pp + 128 * tt
            if s0 < 512:
                dm[pp, 512 * tt + s0: 512 * (tt + 1)] = 1
    c["diag_bf"] = dm
    return c


CONSTS = host_consts()


class Prog:
    def __init__(self, name):
        self.name = name
        self.nc = bass.Bass("TRN2", target_bir_lowering=False, debug=False)
        self.st = ExitStack()
        self.semh = {}
        self.cnt = defaultdict(int)
        self.block = None
        self.consts = {}
        self.const_names = []
        self.scopes = []
        self.scope_tag = ""

    def sem(self, name):
        if name not in self.semh:
            self.semh[name] = self.st.enter_context(self.nc.semaphore(name))
        return self.semh[name]

    def i(self, instr, name, n=1):
        self.cnt[name] += n
        instr.then_inc(self.sem(name), n)
        return self.cnt[name]

    def c(self, name):
        return self.cnt[name]

    def sb(self, name, shape, dtype):
        st = self.scopes[-1] if self.scopes else self.st
        nm = name + self.scope_tag
        return st.enter_context(self.nc.sbuf_tensor(nm, list(shape), dtype))

    def push_scope(self, tag):
        self.scopes.append(ExitStack())
        self.scope_tag = "_" + tag

    def pop_scope(self):
        self.scopes.pop().close()
        self.scope_tag = ""

    def psum(self, name, shape, dtype):
        return self.st.enter_context(self.nc.psum_tensor(name, list(shape), dtype))

    def din(self, name, shape, dtype):
        return self.nc.dram_tensor(name, list(shape), dtype, kind="ExternalInput").ap()

    def dout(self, name, shape, dtype):
        return self.nc.dram_tensor(name, list(shape), dtype, kind="ExternalOutput").ap()

    def dint(self, name, shape, dtype):
        return self.nc.dram_tensor(name, list(shape), dtype).ap()

    def dintT(self, name, shape, dtype):
        return self.nc.dram_tensor(name, list(shape), dtype)

    def barrier(self):
        snap = [(k, v) for k, v in sorted(self.cnt.items())
                if v > 0 and k in self.semh]
        for bname in ("sync", "gpsimd", "scalar", "vector", "tensor"):
            @getattr(self.block, bname)
            def _(e, snap=snap):
                for k, v in snap:
                    e.wait_ge(self.sem(k), v)

    def ccwait(self, names):
        for bname in ("sync", "gpsimd", "scalar", "vector", "tensor"):
            @getattr(self.block, bname)
            def _(e):
                for n in names:
                    e.wait_ge(self.sem(n), 1)

    def ccol(self, kind, op, src_t, dst_t, sem_name):
        @self.block.gpsimd
        def _(e):
            e.collective_compute(
                kind, op,
                replica_groups=[[0, 1, 2, 3], [4, 5, 6, 7]],
                ins=[src_t.ap().opt()], outs=[dst_t.ap().opt()],
            ).then_inc(self.sem(sem_name))
        self.cnt[sem_name] += 1

    def begin(self):
        self.block = self.st.enter_context(self.nc.Block())
        self.psf = self.psum("psf", [128, 8 * 512], F32)
        self.bounce = self.sb("bounce", [128, 1024], F32)
        self.wt = self.sb("wt", [128, 2048], BF16)
        return self.block

    def use_consts(self, names):
        for n in names:
            arr = CONSTS[n]
            dt = {np.dtype(np.float32): F32, np.dtype(BF): BF16}[arr.dtype]
            self.consts[n] = (self.din("c_" + n, arr.shape, dt),
                              self.sb("cs_" + n, arr.shape, dt))
        self.const_names = list(names)

    def use_fconsts(self, values):
        self.fconst_vals = list(values)
        arr = np.tile(np.array(values, np.float32), (128, 1))
        CONSTS["fc_" + self.name] = arr
        dt = F32
        self.consts["fc_" + self.name] = (
            self.din("c_fc_" + self.name, arr.shape, dt),
            self.sb("cs_fc_" + self.name, arr.shape, dt))
        self.const_names.append("fc_" + self.name)
        sbuf = self.consts["fc_" + self.name][1]
        for j, v in enumerate(values):
            self.nc.const_aps.register(F32, float(v), sbuf[:, j:j + 1])

    def load_consts(self):
        p = self

        @p.block.sync
        def _(e):
            for n in p.const_names:
                dram, sbuf = p.consts[n]
                p.i(e.dma_start(out=sbuf[:, :], in_=dram[:, :]), "cl", 16)

    def cs(self, name):
        return self.consts[name][1]

    def input_const_maps(self):
        return {("c_" + n): CONSTS[n] for n in self.const_names}

    def finish(self):
        self.st.close()
        return self.nc


def bk(p, b, n=512, off=0):
    return p.psf[:, 512 * b + off: 512 * b + off + n]


def const_maps(names):
    return {("c_" + n): CONSTS[n] for n in names}


def emit_ln_fm(p, x_dram, xb, lnx, sq, stat, eps=EPS_LN, tok=T, stream=True):
    ones = p.cs("ones_bf")
    tc2 = tok // 2
    assert tc2 <= 512
    base_vv = p.c("vv")
    base_sc = p.c("sc")
    base_xl = [p.c("xl0"), p.c("xl1")]
    base_mm = p.c("mm")

    if stream:
        @p.block.sync
        def _(e):
            e.wait_ge(p.sem("cl"), p.c("cl"))
            for k in range(32):
                tc, ch = divmod(k, NCH)
                if k >= 2:
                    e.wait_ge(p.sem("vv"), base_vv + k - 1)
                p.i(e.dma_start(
                    out=p.bounce[:, (k % 2) * 512:(k % 2) * 512 + tc2],
                    in_=x_dram[:, ch * tok + tc * tc2: ch * tok + (tc + 1) * tc2]),
                    "xl%d" % (k % 2), 16)

        @p.block.vector
        def _(e):
            for k in range(32):
                tc, ch = divmod(k, NCH)
                e.wait_ge(p.sem("xl%d" % (k % 2)), base_xl[k % 2] + 16 * (k // 2 + 1))
                p.i(e.tensor_copy(
                    xb[:, ch * tok + tc * tc2: ch * tok + (tc + 1) * tc2],
                    p.bounce[:, (k % 2) * 512:(k % 2) * 512 + tc2]), "vv", 1)
    else:
        @p.block.vector
        def _(e):
            for k in range(32):
                tc, ch = divmod(k, NCH)
                p.i(e.tensor_copy(
                    xb[:, ch * tok + tc * tc2: ch * tok + (tc + 1) * tc2],
                    x_dram[:, ch * tok + tc * tc2: ch * tok + (tc + 1) * tc2]),
                    "vv", 1)
    vv_cast0 = base_vv

    @p.block.scalar
    def _(e):
        for k in range(32):
            tc, ch = divmod(k, NCH)
            if k >= 2:
                e.wait_ge(p.sem("mm"), base_mm + k - 1)
            e.wait_ge(p.sem("vv"), vv_cast0 + k + 1)
            p.i(e.activation(sq[:, (k % 2) * 512:(k % 2) * 512 + tc2],
                             xb[:, ch * tok + tc * tc2: ch * tok + (tc + 1) * tc2],
                             AF.Square), "sc", 1)

    @p.block.tensor
    def _(e):
        for k in range(32):
            tc, ch = divmod(k, NCH)
            e.wait_ge(p.sem("sc"), base_sc + k + 1)
            e.matmul(bk(p, 0 + tc, tc2), ones[:, :],
                     xb[:, ch * tok + tc * tc2: ch * tok + (tc + 1) * tc2],
                     start=(ch == 0), stop=(ch == NCH - 1))
            mm = e.matmul(bk(p, 2 + tc, tc2), ones[:, :],
                          sq[:, (k % 2) * 512:(k % 2) * 512 + tc2],
                          start=(ch == 0), stop=(ch == NCH - 1))
            p.i(mm, "mm", 1)
    mm_done = p.c("mm")

    @p.block.scalar
    def _(e):
        e.wait_ge(p.sem("mm"), mm_done)
        for tc in range(2):
            p.i(e.activation(stat[:, tc * 512:tc * 512 + tc2], bk(p, tc, tc2),
                             AF.Copy, scale=1.0 / D), "sc", 1)
        e.wait_ge(p.sem("sc"), p.c("sc"))
        for tc in range(2):
            p.i(e.activation(p.bounce[:, tc * 512:tc * 512 + tc2],
                             stat[:, tc * 512:tc * 512 + tc2],
                             AF.Square), "sc", 1)
    sc_msq = p.c("sc")

    @p.block.vector
    def _(e):
        e.wait_ge(p.sem("sc"), sc_msq)
        for tc in range(2):
            p.i(e.scalar_tensor_tensor(
                stat[:, 1024 + tc * 512:1024 + tc * 512 + tc2],
                bk(p, 2 + tc, tc2), 1.0 / D, p.bounce[:, tc * 512:tc * 512 + tc2],
                OP.mult, OP.subtract), "vv", 1)
    vv_var = p.c("vv")

    @p.block.scalar
    def _(e):
        e.wait_ge(p.sem("vv"), vv_var)
        for tc in range(2):
            p.i(e.activation(stat[:, 1024 + tc * 512:1024 + tc * 512 + tc2],
                             stat[:, 1024 + tc * 512:1024 + tc * 512 + tc2],
                             AF.Sqrt, bias=eps), "sc", 1)
    sc_sqrt = p.c("sc")

    @p.block.vector
    def _(e):
        e.wait_ge(p.sem("sc"), sc_sqrt)
        for tc in range(2):
            p.i(e.reciprocal(stat[:, 1024 + tc * 512:1024 + tc * 512 + tc2],
                             stat[:, 1024 + tc * 512:1024 + tc * 512 + tc2]),
                "vv", 1)
        e.wait_ge(p.sem("vv"), p.c("vv"))
        for tc in range(2):
            for ch in range(NCH):
                xs = xb[:, ch * tok + tc * tc2: ch * tok + (tc + 1) * tc2]
                dst = lnx[:, ch * tok + tc * tc2: ch * tok + (tc + 1) * tc2]
                p.i(e.tensor_sub(bk(p, 6, tc2), xs,
                                 stat[:, tc * 512:tc * 512 + tc2]), "vv", 1)
                e.wait_ge(p.sem("vv"), p.c("vv"))
                p.i(e.tensor_mul(dst, bk(p, 6, tc2),
                                 stat[:, 1024 + tc * 512:1024 + tc * 512 + tc2]),
                    "vv", 1)
                e.wait_ge(p.sem("vv"), p.c("vv"))
    return p.c("vv")



def emit_proj_chstream(p, n_og, nch, tile_load, tile_mms, evacs, psum_lag=2,
                       rdy=(), queue="sync"):
    base_wl = [p.c("wl0"), p.c("wl1")]
    base_mp = p.c("mp")
    base_ev = p.c("ev")

    def emit_loads(e):
        for og in range(n_og):
            for ch in range(nch):
                k = og * nch + ch
                e.wait_ge(p.sem("mp"), base_mp + max(0, k - 1))
                p.i(tile_load(e, og, ch, k % 2), "wl%d" % (k % 2), 16)

    if queue == "sync":
        @p.block.sync
        def _(e):
            emit_loads(e)
    else:
        @p.block.gpsimd
        def _(e):
            emit_loads(e)

    @p.block.tensor
    def _(e):
        for og in range(n_og):
            for ch in range(nch):
                k = og * nch + ch
                e.wait_ge(p.sem("wl%d" % (k % 2)),
                          base_wl[k % 2] + 16 * (k // 2 + 1))
                if k == 0:
                    for (sn, sv) in rdy:
                        e.wait_ge(p.sem(sn), sv)
                if ch == 0:
                    e.wait_ge(p.sem("ev"), base_ev + max(0, og - psum_lag + 1))
                last = tile_mms(e, og, ch, k % 2, ch == 0, ch == nch - 1)
                p.i(last, "mp", 1)

    for og in range(n_og):
        evacs[og](base_mp + (og + 1) * nch)
    return p.c("ev")



def emit_ffn(p, x_in, x1_out, wgu_bf, wdn_bf, XB, LNB, HH, SQ, STT, GB, XIO,
             vv_lnx):
    def h_ap(mt, lo, ln):
        if mt < 16:
            return XB[:, mt * T + lo: mt * T + lo + ln]
        return HH[:, (mt - 16) * T + lo: (mt - 16) * T + lo + ln]

    def gu_load(e, og, ch, par):
        e.wait_ge(p.sem("wcgu"), 512 * (og // 16 + 1))
        return e.dma_start(
            out=p.wt[:, par * 1024: par * 1024 + 256],
            in_=wgu_bf[og * 256:(og + 1) * 256, ch * 128:(ch + 1) * 128],
            transpose=True)

    def gu_mms(e, og, ch, par, first, last):
        b = (og % 2) * 4
        gt = p.wt[:, par * 1024: par * 1024 + 128]
        ut = p.wt[:, par * 1024 + 128: par * 1024 + 256]
        lnc = LNB[:, ch * T: (ch + 1) * T]
        for tc in range(2):
            e.matmul(bk(p, b + tc), gt, lnc[:, tc * 512:(tc + 1) * 512],
                     start=first, stop=last)
            mm = e.matmul(bk(p, b + 2 + tc), ut, lnc[:, tc * 512:(tc + 1) * 512],
                          start=first, stop=last)
        return mm

    def gu_evac(og):
        def run(mp_target):
            b = (og % 2) * 4

            @p.block.scalar
            def _(e):
                e.wait_ge(p.sem("mp"), mp_target)
                e.wait_ge(p.sem("ev"), p.c("ev"))
                p.i(e.activation(GB[:, :], p.psf[:, b * 512:(b + 2) * 512],
                                 AF.Silu), "sc", 1)
            sc_t = p.c("sc")

            @p.block.vector
            def _(e):
                e.wait_ge(p.sem("sc"), sc_t)
                p.i(e.tensor_mul(h_ap(og, 0, T), GB[:, :],
                                 p.psf[:, (b + 2) * 512:(b + 4) * 512]), "ev", 1)
        return run

    emit_proj_chstream(p, 64, NCH, gu_load, gu_mms, [gu_evac(g) for g in range(64)],
                       psum_lag=2, rdy=[("vv", vv_lnx)])
    ev_gu = p.c("ev")

    def dn_load(e, og, ch, par):
        e.wait_ge(p.sem("wcdn"), 16 * min(2, (og * 256 + 256 + 1023) // 1024))
        return e.dma_start(
            out=p.wt[:, par * 1024: par * 1024 + 256],
            in_=wdn_bf[og * 256:(og + 1) * 256, ch * 128:(ch + 1) * 128],
            transpose=True)

    def dn_mms(e, og, ch, par, first, last):
        b = (og % 2) * 4
        for dt in range(2):
            wti = p.wt[:, par * 1024 + dt * 128: par * 1024 + dt * 128 + 128]
            for tc in range(2):
                mm = e.matmul(bk(p, b + dt * 2 + tc), wti,
                              h_ap(ch, tc * 512, 512),
                              start=first, stop=last)
        return mm

    def dn_evac(og):
        def run(mp_target):
            for dt in range(2):
                d_idx = og * 2 + dt
                b = (og % 2) * 4 + dt * 2
                xl0 = p.c("xl")

                @p.block.gpsimd
                def _(e):
                    e.wait_ge(p.sem("vv"), p.c("vv"))
                    p.i(e.dma_start(
                        out=XIO[:, (d_idx % 2) * 1024:(d_idx % 2) * 1024 + 1024],
                        in_=x_in[:, d_idx * T:(d_idx + 1) * T]), "xg%d" % (d_idx % 2), 16)

                @p.block.vector
                def _(e):
                    e.wait_ge(p.sem("xg%d" % (d_idx % 2)), p.c("xg%d" % (d_idx % 2)))
                    e.wait_ge(p.sem("mp"), mp_target)
                    e.wait_ge(p.sem("st%d" % (d_idx % 2)), p.c("st%d" % (d_idx % 2)))
                    p.i(e.scalar_tensor_tensor(
                        XIO[:, 2048 + (d_idx % 2) * 1024: 2048 + (d_idx % 2) * 1024 + 1024],
                        p.psf[:, b * 512:(b + 2) * 512], 0.5,
                        XIO[:, (d_idx % 2) * 1024:(d_idx % 2) * 1024 + 1024],
                        OP.mult, OP.add), "vv", 1)
                    if dt == 1:
                        p.i(e.engine_nop(), "ev", 1)
                vv_t = p.c("vv")

                @p.block.gpsimd
                def _(e):
                    e.wait_ge(p.sem("vv"), vv_t)
                    p.i(e.dma_start(
                        out=x1_out[:, d_idx * T:(d_idx + 1) * T],
                        in_=XIO[:, 2048 + (d_idx % 2) * 1024: 2048 + (d_idx % 2) * 1024 + 1024]),
                        "st%d" % (d_idx % 2), 16)
        return run

    emit_proj_chstream(p, 8, 64, dn_load, dn_mms, [dn_evac(g) for g in range(8)],
                       psum_lag=2, rdy=[("ev", ev_gu)])
    return (p.c("st0"), p.c("st1"))


def emit_L1(p, io):
    x_in = io["x_fm"]
    wgu = io["wgu"]
    wdn = io["wdn"]
    w_qkv = {n: io[n] for n in ("wk", "wq", "wv")}

    x1_out = io["x1_fm"]
    q_out = io["q_fm"]
    k_out = io["k_fm"]
    v_out = io["v_fm"]
    nrm_out = io["qknorm"]

    wgu_bf = io["wgu_bf"]
    wdn_bf = io["wdn_bf"]
    w_bf = {n: io[n + "_bf"] for n in ("wk", "wq", "wv")}

    XB = p.sb("XB", [128, 16384], BF16)
    LNB = p.sb("LNB", [128, 16384], BF16)
    HH = p.sb("HH", [128, 49152], BF16)
    SQ = p.sb("SQ", [128, 1024], BF16)
    STT = p.sb("STT", [128, 2048], F32)
    GB = p.sb("GB", [128, 1024], BF16)
    XIO = p.sb("XIO", [128, 4096], F32)
    NRM = p.sb("NRM", [128, 16], F32)
    NO = p.sb("NO", [1, 32], F32)

    @p.block.gpsimd
    def _(e):
        for blk in range(4):
            if blk:
                e.wait_ge(p.sem("wcgu"), p.c("wcgu"))
            for g in range(blk * 16, blk * 16 + 16):
                p.i(e.dma_start(out=wgu_bf[g * 256: g * 256 + 128, :],
                                in_=wgu[g * 128: (g + 1) * 128, :]), "wcgu", 16)
                p.i(e.dma_start(out=wgu_bf[g * 256 + 128: g * 256 + 256, :],
                                in_=wgu[INTER + g * 128: INTER + (g + 1) * 128, :]),
                    "wcgu", 16)
        e.wait_ge(p.sem("wcgu"), p.c("wcgu"))
        for b in range(2):
            if b:
                e.wait_ge(p.sem("wcdn"), p.c("wcdn"))
            p.i(e.dma_start(out=wdn_bf[b * 1024:(b + 1) * 1024, :],
                            in_=wdn[b * 1024:(b + 1) * 1024, :]), "wcdn", 16)
        e.wait_ge(p.sem("wcdn"), p.c("wcdn"))
        for n in ("wk", "wq", "wv"):
            p.i(e.dma_start(out=w_bf[n][:, :], in_=w_qkv[n][:, :]), "wc" + n, 16)

    vv_ln = emit_ln_fm(p, x_in, XB, LNB, SQ, STT)

    st_x1 = emit_ffn(p, x_in, x1_out, wgu_bf, wdn_bf, XB, LNB, HH, SQ, STT,
                     GB, XIO, vv_ln)

    @p.block.sync
    def _(e):
        e.wait_ge(p.sem("st0"), st_x1[0])
        e.wait_ge(p.sem("st1"), st_x1[1])
    vv_ln1 = emit_ln_fm(p, x1_out, XB, LNB, SQ, STT)

    KB = HH[:, 0:16384]
    QB = HH[:, 16384:32768]
    VB = HH[:, 32768:49152]

    def qk_spec(wbf, dstbuf):
        def load(e, og, ch, par):
            e.wait_ge(p.sem("wc" + wbf_name[id(wbf)]), 16)
            return e.dma_start(
                out=p.wt[:, par * 1024: par * 1024 + 256],
                in_=wbf[og * 256:(og + 1) * 256, ch * 128:(ch + 1) * 128],
                transpose=True)

        def mms(e, og, ch, par, first, last):
            for hd in range(2):
                wti = p.wt[:, par * 1024 + hd * 128: par * 1024 + hd * 128 + 128]
                for tc in range(2):
                    mm = e.matmul(bk(p, (og % 2) * 4 + hd * 2 + tc), wti,
                                  LNB[:, ch * T + tc * 512: ch * T + tc * 512 + 512],
                                  start=first, stop=last)
            return mm

        def evac(og):
            def run(mp_target):
                @p.block.vector
                def _(e):
                    e.wait_ge(p.sem("mp"), mp_target)
                    b = (og % 2) * 4
                    for hd in range(2):
                        h = og * 2 + hd
                        p.i(e.tensor_copy(
                            dstbuf[:, h * T:(h + 1) * T],
                            p.psf[:, (b + hd * 2) * 512:(b + hd * 2 + 2) * 512]),
                            "ev", 1 if hd == 1 else 0)
            return run
        return load, mms, [evac(g) for g in range(8)]

    wbf_name = {id(w_bf["wk"]): "wk", id(w_bf["wq"]): "wq", id(w_bf["wv"]): "wv"}
    lk, mk, ek = qk_spec(w_bf["wk"], KB)
    emit_proj_chstream(p, 8, NCH, lk, mk, ek, psum_lag=2, rdy=[("vv", vv_ln1)])
    lq, mq, eq = qk_spec(w_bf["wq"], QB)
    emit_proj_chstream(p, 8, NCH, lq, mq, eq, psum_lag=2)

    lv, mv, ev_ = qk_spec(w_bf["wv"], VB)
    emit_proj_chstream(p, 8, NCH, lv, mv, ev_, psum_lag=2)
    ev_qkv = p.c("ev")

    ones = p.cs("ones_bf")
    NRM2 = p.sb("NRM2", [128, 2], F32)
    sc_base = p.c("sc")
    mm_base = p.c("mm")
    vv_base = p.c("vv")

    @p.block.scalar
    def _(e):
        e.wait_ge(p.sem("ev"), ev_qkv)
        k = 0
        for row, buf in ((0, QB), (1, KB)):
            for h in range(H):
                k += 1
                if k >= 2:
                    e.wait_ge(p.sem("mm"), mm_base + k - 1)
                p.i(e.activation(SQ[:, :], buf[:, h * T:(h + 1) * T],
                                 AF.Square), "sc", 1)

    @p.block.tensor
    def _(e):
        k = 0
        for row in range(2):
            for h in range(H):
                k += 1
                e.wait_ge(p.sem("sc"), sc_base + k)
                if k >= 2:
                    e.wait_ge(p.sem("vv"), vv_base + 4 * (k - 1))
                e.matmul(bk(p, 5), ones[:, :], SQ[:, 0:512], start=True, stop=True)
                mm = e.matmul(bk(p, 6), ones[:, :], SQ[:, 512:1024],
                              start=True, stop=True)
                p.i(mm, "mm", 1)

    @p.block.vector
    def _(e):
        k = 0
        for row in range(2):
            for h in range(H):
                k += 1
                e.wait_ge(p.sem("mm"), mm_base + k)
                p.i(e.reduce_max(NRM2[:, 0:1], bk(p, 5),
                                 axis=mybir.AxisListType.X), "vv", 1)
                p.i(e.reduce_max(NRM2[:, 1:2], bk(p, 6),
                                 axis=mybir.AxisListType.X), "vv", 1)
                e.wait_ge(p.sem("vv"), p.c("vv"))
                p.i(e.tensor_max(NRM[:, h:h + 1],
                                 NRM2[:, 0:1], NRM2[:, 1:2]), "vv", 1)
                e.wait_ge(p.sem("vv"), p.c("vv"))
                p.i(e.tensor_copy(NO[0:1, row * 16 + h: row * 16 + h + 1],
                                  NRM[0:1, h:h + 1]), "vv", 1)
    vv_nrm = p.c("vv")

    @p.block.sync
    def _(e):
        e.wait_ge(p.sem("ev"), ev_qkv)
        p.i(e.dma_start(out=k_out[:, :], in_=KB), "st", 16)
        p.i(e.dma_start(out=q_out[:, :], in_=QB), "st", 16)
        p.i(e.dma_start(out=v_out[:, :], in_=VB), "st", 16)
        e.wait_ge(p.sem("vv"), vv_nrm)
        p.i(e.dma_start(out=nrm_out[:, :], in_=NO[:, :]), "st", 16)
        e.wait_ge(p.sem("st"), p.c("st"))


def ref_partial_l1(x, wgu, wdn, wk, wq, wv):
    import numpy as _np
    xx = x.astype(_np.float64)
    m = xx.mean(-1, keepdims=True)
    v = xx.var(-1, keepdims=True)
    ln = (xx - m) / _np.sqrt(v + EPS_LN)
    gu = ln @ wgu.astype(_np.float64).T
    g, u = gu[:, :INTER], gu[:, INTER:]
    hh = g / (1 + _np.exp(-g)) * u
    x1 = xx + 0.5 * (hh @ wdn.astype(_np.float64).T)
    m1 = x1.mean(-1, keepdims=True)
    v1 = x1.var(-1, keepdims=True)
    ln1 = (x1 - m1) / _np.sqrt(v1 + EPS_LN)
    k = ln1 @ wk.astype(_np.float64).T
    q = ln1 @ wq.astype(_np.float64).T
    vv = ln1 @ wv.astype(_np.float64).T
    return x1, ln1, q, k, vv


def host_rotate_k(k_fm_full, r):
    rot = np.roll(k_fm_full, -r, axis=2)
    return np.ascontiguousarray(rot.reshape(128, 16 * S))


def host_rotate_v(v_tm_full, r):
    rot = np.roll(v_tm_full, -r, axis=0)
    return np.ascontiguousarray(
        rot.reshape(S, 4, 512).transpose(1, 0, 2))


def host_cbias(r):
    out = np.zeros((64,), np.float32)
    for kt in range(32):
        for qc in range(2):
            if qc * 4 <= kt < qc * 4 + 4:
                v = 0.0
            else:
                ka = (r + kt * 128) % S
                q0 = r + qc * 512
                if ka + 127 <= q0:
                    v = 0.0
                else:
                    v = NEG
            out[kt * 2 + qc] = v
    return np.tile(out[None, :], (128, 1)).astype(np.float32)


def ref_attention(q, k, v, wo, causal=True):
    nq = q.shape[0]
    nk = k.shape[0]
    qh = q.reshape(nq, H, DH).astype(np.float64)
    kh = k.reshape(nk, H, DH).astype(np.float64)
    vh = v.reshape(nk, H, DH).astype(np.float64)
    out = np.zeros((nq, H, DH))
    return qh, kh, vh, out


def emit_selection(p, lg_full, lg_grid, selrows, LR, LSB, ST):
    ones_f = p.cs("ones_f")

    @p.block.sync
    def _(e):
        p.i(e.dma_start(out=LR[:, :], in_=lg_full), "se", 16)
        p.i(e.dma_start(out=LSB[:, :], in_=lg_grid), "se", 16)
    se0 = p.c("se")

    @p.block.vector
    def _(e):
        e.wait_ge(p.sem("se"), se0)
        e.wait_ge(p.sem("cl"), p.c("cl"))
        p.i(e.reduce_max(ST[0:1, 1:2], LR[:, :], axis=mybir.AxisListType.X),
            "vv", 1)
        p.i(e.tensor_reduce(ST[0:1, 0:1], LR[:, :], mybir.AxisListType.X,
                            OP.min), "vv", 1)
        e.wait_ge(p.sem("vv"), p.c("vv"))
        p.i(e.tensor_copy(ST[0:1, 6:7], ST[0:1, 1:2]), "vv", 1)
        p.i(e.tensor_scalar_add(ST[0:1, 0:1], ST[0:1, 0:1], -1.0), "vv", 1)
        e.wait_ge(p.sem("vv"), p.c("vv"))

    for it in range(50):
        @p.block.vector
        def _(e):
            p.i(e.tensor_add(ST[0:1, 2:3], ST[0:1, 0:1], ST[0:1, 1:2]), "vv", 1)
            e.wait_ge(p.sem("vv"), p.c("vv"))
            p.i(e.tensor_scalar_mul(ST[0:1, 2:3], ST[0:1, 2:3], 0.5), "vv", 1)
        vv_mid = p.c("vv")

        @p.block.tensor
        def _(e):
            e.wait_ge(p.sem("vv"), vv_mid)
            if it > 0:
                pass
            p.i(e.matmul(bk(p, 0, 1), ones_f[0:1, :], ST[0:1, 2:3],
                         start=True, stop=True), "mm", 1)
        mm_mid = p.c("mm")

        @p.block.vector
        def _(e):
            e.wait_ge(p.sem("mm"), mm_mid)
            p.i(e.tensor_scalar(p.bounce[:, 0:32],
                                LSB[:, :], bk(p, 0, 1), None, OP.is_gt,
                                op1=OP.add, accum_out=p.bounce[:, 32:33]),
                "vv", 1)
        vv_cnt = p.c("vv")

        @p.block.tensor
        def _(e):
            e.wait_ge(p.sem("vv"), vv_cnt)
            p.i(e.matmul(p.psf[0:1, 4:5], ones_f[:, 0:1],
                         p.bounce[:, 32:33], start=True, stop=True), "mm", 1)
        mm_tot = p.c("mm")

        @p.block.vector
        def _(e):
            e.wait_ge(p.sem("mm"), mm_tot)
            p.i(e.tensor_scalar(ST[0:1, 3:4], p.psf[0:1, 4:5], 1024.5,
                                None, OP.is_gt), "vv", 1)
            p.i(e.tensor_sub(ST[0:1, 4:5], ST[0:1, 2:3], ST[0:1, 0:1]), "vv", 1)
            e.wait_ge(p.sem("vv"), p.c("vv"))
            p.i(e.tensor_mul(ST[0:1, 4:5], ST[0:1, 4:5], ST[0:1, 3:4]), "vv", 1)
            e.wait_ge(p.sem("vv"), p.c("vv"))
            p.i(e.tensor_add(ST[0:1, 0:1], ST[0:1, 0:1], ST[0:1, 4:5]), "vv", 1)
            e.wait_ge(p.sem("vv"), p.c("vv"))
            p.i(e.tensor_sub(ST[0:1, 4:5], ST[0:1, 2:3], ST[0:1, 1:2]), "vv", 1)
            p.i(e.tensor_scalar(ST[0:1, 5:6], ST[0:1, 3:4], 1.0, -1.0,
                                OP.subtract, OP.mult), "vv", 1)
            e.wait_ge(p.sem("vv"), p.c("vv"))
            p.i(e.tensor_mul(ST[0:1, 4:5], ST[0:1, 4:5], ST[0:1, 5:6]), "vv", 1)
            e.wait_ge(p.sem("vv"), p.c("vv"))
            p.i(e.tensor_add(ST[0:1, 1:2], ST[0:1, 1:2], ST[0:1, 4:5]), "vv", 1)
            e.wait_ge(p.sem("vv"), p.c("vv"))

    @p.block.vector
    def _(e):
        p.i(e.tensor_scalar(selrowbuf(p, 1), LR[:, :], ST[0:1, 1:2], None,
                            OP.is_gt), "vv", 1)
        e.wait_ge(p.sem("vv"), p.c("vv"))
        p.i(e.tensor_tensor_scan(selrowbuf(p, 0), selrowbuf(p, 1),
                                 selrowbuf(p, 1), 0.0, OP.add, OP.bypass),
            "vv", 1)
        p.i(e.tensor_scalar_mul(ST[0:1, 7:8], ST[0:1, 6:7], -1.0), "vv", 1)
        e.wait_ge(p.sem("vv"), p.c("vv"))
    vv_c = p.c("vv")

    @p.block.scalar
    def _(e):
        e.wait_ge(p.sem("vv"), vv_c)
        p.i(e.activation(selrowbuf(p, 2), LR[:, :], AF.Exp,
                         bias=ST[0:1, 7:8]), "sc", 1)
    sc_e = p.c("sc")

    @p.block.vector
    def _(e):
        e.wait_ge(p.sem("sc"), sc_e)
        p.i(e.tensor_mul(selrowbuf(p, 2), selrowbuf(p, 2), selrowbuf(p, 1)),
            "vv", 1)
        e.wait_ge(p.sem("vv"), p.c("vv"))
        p.i(e.reduce_sum(ST[0:1, 8:9], selrowbuf(p, 2),
                         axis=mybir.AxisListType.X), "vv", 1)
        e.wait_ge(p.sem("vv"), p.c("vv"))
        p.i(e.reciprocal(ST[0:1, 9:10], ST[0:1, 8:9]), "vv", 1)
        e.wait_ge(p.sem("vv"), p.c("vv"))
        p.i(e.tensor_scalar(selrowbuf(p, 2), selrowbuf(p, 2), ST[0:1, 9:10],
                            None, OP.mult), "vv", 1)
    vv_sel = p.c("vv")

    @p.block.sync
    def _(e):
        e.wait_ge(p.sem("vv"), vv_sel)
        for rr in range(3):
            p.i(e.dma_start(out=selrows[rr][:, :], in_=selrowbuf(p, rr)),
                "se", 16)
    return p.c("se")


def selrowbuf(p, rr):
    return p.selrow_aps[rr]


def emit_L3(p, io, convert=True):
    lg_full = io["lg_full"]
    lg_grid = io["lg_grid"]
    lnx_tm = io["lnx_tm"]
    g1_in = io["g1_fm"]
    wqs_in = io["wqs"]
    wks_in = io["wks"]
    wvs_in = io["wvs"]
    wos_in = io["wos"]
    maskT_in = io["maskT"]
    cosr_in = io["cosr"]
    sinr_in = io["sinr"]

    part_out = io["partial"]
    filt_out = io["filt_fm"]

    wbf = {n: io[n + "_bf"] for n in ("wqs", "wks", "wvs")}
    wos_bf = io["wos_bf"]
    selrows = io["selrows"]
    cbd = io["cbd"]
    mrow = io["mrow"]

    FB = p.sb("FB", [128, 16384], BF16)
    H1 = p.sb("H1", [128, 16384], BF16)
    QKV3 = p.sb("QKV3", [128, 12288], BF16)
    OO = p.sb("OO", [128, 8192], BF16)
    CBF = p.sb("CBF", [128, 4096], F32)
    RT = p.sb("RT", [128, 2048], F32)
    GT = p.sb("GT", [128, 4096], BF16)
    MTB = p.sb("MTB", [128, 1024], F32)
    TMP = p.sb("TMP", [128, 1024], F32)
    EBB = p.sb("EBB", [128, 2048], BF16)
    LSB = p.sb("LSB", [128, 32], F32)
    ST = p.sb("ST", [1, 16], F32)
    SELF_ = p.sb("SELF", [128, 8], F32)
    SELI = p.sb("SELI", [128, 8], I32)
    SQ = p.sb("SQ", [128, 1024], BF16)
    STT = p.sb("STT", [128, 2048], F32)
    GM = p.sb("GM", [128, 16], F32)
    NB = p.sb("NB", [1, 32], F32)
    CBB = p.sb("CBB", [128, 4], F32)
    MM1 = p.sb("MM1", [128, 1], F32)
    H1F = H1.bitcast(F32)
    p.selrow_aps = [H1F[0:1, 0:4096], H1F[0:1, 4096:8192],
                    QKV3.bitcast(F32)[0:1, 0:4096]]
    LR = CBF[0:1, 0:4096]

    ones_bf = p.cs("ones_bf")
    ones_f = p.cs("ones_f")
    eye = p.cs("eye_bf")
    p64 = p.cs("p64_bf")
    zind = p.cs("zind_bf")
    bsel = p.cs("bsel_bf")

    ZS2 = None
    XIO = p.sb("XIO", [128, 4096], F32)
    ZS2 = XIO[0:4, 0:1024]
    ZSB2 = XIO[0:4, 1024:1536].bitcast(BF16)

    if convert:
        @p.block.gpsimd
        def _(e):
            for n, src_ in (("wqs", wqs_in), ("wks", wks_in), ("wvs", wvs_in)):
                p.i(e.dma_start(out=wbf[n][:, :], in_=src_[:, :]), "wc" + n, 16)
            p.i(e.dma_start(out=wos_bf[:, :], in_=wos_in[:, :]), "wcwos", 16)

    @p.block.sync
    def _(e):
        p.i(e.dma_start(out=RT[:, 0:1024], in_=cosr_in[:, :]), "rt", 16)
        p.i(e.dma_start(out=RT[:, 1024:2048], in_=sinr_in[:, :]), "rt", 16)
        p.i(e.dma_start(out=GM[:, :], in_=g1_in[:, :]), "rt", 16)
    rt0 = p.c("rt")

    @p.block.vector
    def _(e):
        e.wait_ge(p.sem("rt"), rt0)
        e.wait_ge(p.sem("cl"), p.c("cl"))
        p.i(e.tensor_scalar(RT[:, 1024:2048], RT[:, 1024:2048],
                            p.cs("sgn_f")[:, 0:1], None, OP.mult), "vv", 1)
    vv_rt = p.c("vv")

    emit_selection(p, lg_full, lg_grid, selrows, LR, LSB, ST)
    se_done = p.c("se")

    @p.block.sync
    def _(e):
        e.wait_ge(p.sem("se"), se_done)
        p.i(e.dma_start(out=cbd[0:1, :], in_=selrows[0][:, :]), "cb", 16)
        rows = 1
        while rows < 128:
            e.wait_ge(p.sem("cb"), p.c("cb"))
            p.i(e.dma_start(out=cbd[rows:2 * rows, :], in_=cbd[0:rows, :]),
                "cb", 16)
            rows *= 2
        e.wait_ge(p.sem("cb"), p.c("cb"))
        p.i(e.dma_start(out=CBF[:, :], in_=cbd[:, :]), "cb", 16)
    cb_done = p.c("cb")

    @p.block.vector
    def _(e):
        e.wait_ge(p.sem("cb"), cb_done)
        for jj in range(8):
            p.i(e.tensor_scalar(FB.bitcast(F32)[:, 0:4096],
                                CBF[:, :], p.cs("iotasel_f")[:, jj:jj + 1], None,
                                OP.is_le, op1=OP.add,
                                accum_out=SELF_[:, jj:jj + 1]), "vv", 1)
            e.wait_ge(p.sem("vv"), p.c("vv"))
        e.wait_ge(p.sem("vv"), p.c("vv"))
        p.i(e.tensor_copy(SELI[:, :], SELF_[:, :]), "vv", 1)
    vv_seli = p.c("vv")

    mt_base = p.c("mt")
    vv_fb_base = p.c("vv")
    gt_base = [p.c("gt0"), p.c("gt1")]

    for jj in range(8):
        @p.block.gpsimd
        def _(e):
            if jj == 0:
                e.wait_ge(p.sem("vv"), vv_seli)
            if jj >= 2:
                e.wait_ge(p.sem("mt"), mt_base + (jj - 1) * NCH)
            p.i(e.indirect_dma_start(
                GT[:, (jj % 2) * 2048:(jj % 2) * 2048 + 2048],
                None, lnx_tm[:, :], IndirectOffsetOnAxis(SELI[:, jj:jj + 1], 0),
                bounds_check=S - 1, oob_is_err=False), "gt%d" % (jj % 2), 16)

        @p.block.tensor
        def _(e):
            e.wait_ge(p.sem("gt%d" % (jj % 2)),
                      gt_base[jj % 2] + 16 * (jj // 2 + 1))
            for ch in range(NCH):
                tix = jj * NCH + ch
                if tix >= 2:
                    e.wait_ge(p.sem("vv"), vv_fb_base + tix - 1)
                p.i(e.transpose(
                    p.psf[:, (tix % 2) * 512:(tix % 2) * 512 + 64].bitcast(BF16),
                    GT[:, (jj % 2) * 2048 + ch * 128:(jj % 2) * 2048 + (ch + 1) * 128],
                    eye[:, :]), "mt", 1)

        @p.block.vector
        def _(e):
            for ch in range(NCH):
                tix = jj * NCH + ch
                e.wait_ge(p.sem("mt"), mt_base + tix + 1)
                p.i(e.tensor_copy(
                    FB[:, ch * T + jj * 128: ch * T + (jj + 1) * 128],
                    p.psf[:, (tix % 2) * 512:(tix % 2) * 512 + 64].bitcast(BF16)),
                    "vv", 1)
    ev_fb = p.c("vv")

    sq_base = p.c("sc")
    mmr_base = p.c("mm")

    @p.block.scalar
    def _(e):
        e.wait_ge(p.sem("vv"), ev_fb)
        for k in range(32):
            tc, ch = divmod(k, NCH)
            if k >= 2:
                e.wait_ge(p.sem("mm"), mmr_base + k - 1)
            p.i(e.activation(SQ[:, (k % 2) * 512:(k % 2) * 512 + 512],
                             FB[:, ch * T + tc * 512: ch * T + tc * 512 + 512],
                             AF.Square), "sc", 1)

    @p.block.tensor
    def _(e):
        for k in range(32):
            tc, ch = divmod(k, NCH)
            e.wait_ge(p.sem("sc"), sq_base + k + 1)
            p.i(e.matmul(bk(p, tc), ones_bf[:, :],
                         SQ[:, (k % 2) * 512:(k % 2) * 512 + 512],
                         start=(ch == 0), stop=(ch == NCH - 1)), "mm", 1)
    mm_rs = p.c("mm")

    @p.block.scalar
    def _(e):
        e.wait_ge(p.sem("mm"), mm_rs)
        for tc in range(2):
            p.i(e.activation(STT[:, tc * 512:(tc + 1) * 512], bk(p, tc),
                             AF.Sqrt, scale=1.0 / D, bias=EPS_RMS), "sc", 1)
    sc_rs = p.c("sc")

    @p.block.vector
    def _(e):
        e.wait_ge(p.sem("sc"), sc_rs)
        for tc in range(2):
            p.i(e.reciprocal(STT[:, tc * 512:(tc + 1) * 512],
                             STT[:, tc * 512:(tc + 1) * 512]), "vv", 1)
        e.wait_ge(p.sem("vv"), p.c("vv"))
        e.wait_ge(p.sem("rt"), rt0)
        for tc in range(2):
            for ch in range(NCH):
                sl = slice(ch * T + tc * 512, ch * T + tc * 512 + 512)
                p.i(e.scalar_tensor_tensor(
                    H1[:, sl], FB[:, sl], GM[:, ch:ch + 1],
                    STT[:, tc * 512:(tc + 1) * 512], OP.mult, OP.mult), "vv", 1)
    vv_h1 = p.c("vv")

    QH = QKV3[:, 0:4096]
    KH = QKV3[:, 4096:8192]
    VC = QKV3[:, 8192:12288]

    def mk_qk(wname, dst):
        def load(e, og, ch, par):
            e.wait_ge(p.sem("wc" + wname), 16)
            return e.dma_start(
                out=p.wt[:, par * 1024: par * 1024 + 256],
                in_=wbf[wname][ch * 128:(ch + 1) * 128, og * 256:(og + 1) * 256])

        def mms(e, og, ch, par, first, last):
            b = (og % 2) * 4
            for hd in range(2):
                wti = p.wt[:, par * 1024 + hd * 128: par * 1024 + hd * 128 + 128]
                for tc in range(2):
                    mm = e.matmul(bk(p, b + hd * 2 + tc), wti,
                                  H1[:, ch * T + tc * 512: ch * T + tc * 512 + 512],
                                  start=first, stop=last)
            return mm

        def evac(og):
            def run(mp_target):
                @p.block.vector
                def _(e):
                    e.wait_ge(p.sem("mp"), mp_target)
                    b = (og % 2) * 4
                    for hd in range(2):
                        h = og * 2 + hd
                        p.i(e.tensor_copy(
                            dst[:, h * T:(h + 1) * T],
                            p.psf[:, (b + hd * 2) * 512:(b + hd * 2 + 2) * 512]),
                            "ev", 1 if hd == 1 else 0)
            return run
        return load, mms, [evac(g) for g in range(2)]

    lq, mq, eq = mk_qk("wqs", QH)
    emit_proj_chstream(p, 2, NCH, lq, mq, eq, psum_lag=2, rdy=[("vv", vv_h1)])
    lk, mk2, ek = mk_qk("wks", KH)
    emit_proj_chstream(p, 2, NCH, lk, mk2, ek, psum_lag=2)

    def v_load(e, og, ch, par):
        e.wait_ge(p.sem("wcwvs"), 16)
        return e.dma_start(
            out=p.wt[:, par * 1024: par * 1024 + 256],
            in_=wbf["wvs"][ch * 128:(ch + 1) * 128, og * 256:(og + 1) * 256])

    def v_mms(e, og, ch, par, first, last):
        wti = p.wt[:, par * 1024: par * 1024 + 256]
        for tt in range(8):
            mm = e.matmul(p.psf[:, tt * 512: tt * 512 + 256],
                          H1[:, ch * T + tt * 128: ch * T + tt * 128 + 128], wti,
                          start=first, stop=last)
        return mm

    def v_evac(og):
        def run(mp_target):
            @p.block.vector
            def _(e):
                e.wait_ge(p.sem("mp"), mp_target)
                for tt in range(8):
                    p.i(e.tensor_copy(
                        VC[:, tt * 512 + og * 256: tt * 512 + og * 256 + 256],
                        p.psf[:, tt * 512: tt * 512 + 256]),
                        "ev", 1 if tt == 7 else 0)
        return run

    emit_proj_chstream(p, 2, NCH, v_load, v_mms, [v_evac(g) for g in range(2)],
                       psum_lag=1)
    ev_qkv3 = p.c("ev")

    mmp_base = p.c("mm")
    vvr_base = p.c("vv")

    @p.block.tensor
    def _(e):
        e.wait_ge(p.sem("ev"), ev_qkv3)
        for i8 in range(8):
            if i8 >= 2:
                e.wait_ge(p.sem("vv"), vvr_base + (i8 - 1) * 3)
            buf = QH if i8 < 4 else KH
            h = i8 % 4
            for tc in range(2):
                p.i(e.matmul(bk(p, 4 + (i8 % 2) * 2 + tc), p64[:, :],
                             buf[:, h * T + tc * 512: h * T + tc * 512 + 512],
                             start=True, stop=True), "mm", 1)

    @p.block.vector
    def _(e):
        for i8 in range(8):
            buf = QH if i8 < 4 else KH
            h = i8 % 4
            e.wait_ge(p.sem("mm"), mmp_base + (i8 + 1) * 2)
            sl = slice(h * T, (h + 1) * T)
            p.i(e.tensor_mul(MTB[:, 0:1024],
                             p.psf[:, (4 + (i8 % 2) * 2) * 512:
                                   (4 + (i8 % 2) * 2 + 2) * 512],
                             RT[:, 1024:2048]), "vv", 1)
            p.i(e.tensor_mul(GT.bitcast(F32)[:, 0:1024], buf[:, sl],
                             RT[:, 0:1024]), "vv", 1)
            e.wait_ge(p.sem("vv"), p.c("vv"))
            p.i(e.tensor_add(buf[:, sl], GT.bitcast(F32)[:, 0:1024],
                             MTB[:, 0:1024]), "vv", 1)
            e.wait_ge(p.sem("vv"), p.c("vv"))
    vv_rope = p.c("vv")

    nb_base = p.c("mm")
    scn_base = p.c("sc")
    vvn_base = p.c("vv")

    @p.block.scalar
    def _(e):
        e.wait_ge(p.sem("vv"), vv_rope)
        for i8 in range(8):
            if i8 >= 1:
                e.wait_ge(p.sem("mm"), nb_base + i8)
            buf = QH if i8 < 4 else KH
            h = i8 % 4
            p.i(e.activation(SQ[:, :], buf[:, h * T:(h + 1) * T], AF.Square),
                "sc", 1)

    @p.block.tensor
    def _(e):
        for i8 in range(8):
            e.wait_ge(p.sem("sc"), scn_base + i8 + 1)
            if i8 >= 1:
                e.wait_ge(p.sem("vv"), vvn_base + 3 * i8)
            e.matmul(bk(p, 0), ones_bf[:, :], SQ[:, 0:512], start=True, stop=True)
            p.i(e.matmul(bk(p, 1), ones_bf[:, :], SQ[:, 512:1024],
                         start=True, stop=True), "mm", 1)

    @p.block.vector
    def _(e):
        for i8 in range(8):
            e.wait_ge(p.sem("mm"), nb_base + i8 + 1)
            p.i(e.reduce_max(NB[0:1, 16:17], bk(p, 0)[0:1, :],
                             axis=mybir.AxisListType.X), "vv", 1)
            p.i(e.reduce_max(NB[0:1, 17:18], bk(p, 1)[0:1, :],
                             axis=mybir.AxisListType.X), "vv", 1)
            e.wait_ge(p.sem("vv"), p.c("vv"))
            p.i(e.tensor_max(NB[0:1, i8:i8 + 1], NB[0:1, 16:17],
                             NB[0:1, 17:18]), "vv", 1)
            e.wait_ge(p.sem("vv"), p.c("vv"))
    vv_nb = p.c("vv")

    mm_base2 = p.c("mk")

    vvm_base = p.c("vm")

    @p.block.sync
    def _(e):
        for mkt in range(8):
            if mkt == 0:
                e.wait_ge(p.sem("vv"), vv_rope)
            else:
                e.wait_ge(p.sem("vm"), vvm_base + mkt)
            p.i(e.dma_start(out=MTB[:, :],
                            in_=maskT_in[mkt * 128:(mkt + 1) * 128, :]),
                "mk", 16)

    @p.block.vector
    def _(e):
        for mkt in range(8):
            e.wait_ge(p.sem("mk"), mm_base2 + 16 * (mkt + 1))
            p.i(e.reduce_max(MM1[:, 0:1] if mkt == 0 else CBB[:, 0:1],
                             MTB[:, :], axis=mybir.AxisListType.X), "vm", 1)
            e.wait_ge(p.sem("vm"), p.c("vm"))
            if mkt > 0:
                p.i(e.tensor_max(MM1[:, 0:1], MM1[:, 0:1], CBB[:, 0:1]), "vv", 1)
                e.wait_ge(p.sem("vv"), p.c("vv"))
    vv_mm1 = p.c("vv")

    @p.block.sync
    def _(e):
        e.wait_ge(p.sem("vv"), vv_mm1)
        p.i(e.dma_start(out=mrow[:, :], in_=MM1[:, 0:1]), "mk", 16)
        e.wait_ge(p.sem("mk"), p.c("mk"))
        p.i(e.dma_start(out=NB[0:1, 18:19] if False else MM1[0:1, 0:1],
                        in_=mrow[0:1, 0:1]) if False else
            e.dma_start(out=CBF[0:1, 0:128], in_=mrow[:, :]), "mk", 16)
    mk_row = p.c("mk")

    @p.block.vector
    def _(e):
        e.wait_ge(p.sem("mk"), mk_row)
        p.i(e.reduce_max(NB[0:1, 18:19], CBF[0:1, 0:128],
                         axis=mybir.AxisListType.X), "vv", 1)
        e.wait_ge(p.sem("vv"), p.c("vv"))
        p.i(e.tensor_scalar(NB[0:1, 19:20], NB[0:1, 18:19], 0.0, -40.0,
                            OP.max, OP.add), "vv", 1)
        p.i(e.tensor_mul(NB[0:1, 20:24], NB[0:1, 0:4], NB[0:1, 4:8]), "vv", 1)
        e.wait_ge(p.sem("vv"), p.c("vv"))
    vv_qk = p.c("vv")

    @p.block.scalar
    def _(e):
        e.wait_ge(p.sem("vv"), vv_qk)
        p.i(e.activation(NB[0:1, 20:24], NB[0:1, 20:24], AF.Sqrt), "sc", 1)
        e.wait_ge(p.sem("sc"), p.c("sc"))
        p.i(e.activation(NB[0:1, 24:28], NB[0:1, 20:24], AF.Relu, scale=ISQ,
                         bias=NB[0:1, 19:20]), "sc", 1)
    sc_c = p.c("sc")

    @p.block.vector
    def _(e):
        e.wait_ge(p.sem("sc"), sc_c)
        p.i(e.tensor_scalar_mul(NB[0:1, 24:28], NB[0:1, 24:28], -1.0), "vv", 1)
        e.wait_ge(p.sem("vv"), p.c("vv"))
    vv_negc = p.c("vv")

    @p.block.tensor
    def _(e):
        e.wait_ge(p.sem("vv"), vv_negc)
        for h in range(4):
            mm = e.matmul(bk(p, 7, 1, off=h), ones_f[0:1, :],
                          NB[0:1, 24 + h:25 + h], start=True, stop=True)
        p.i(mm, "mm", 1)
    mm_cb = p.c("mm")

    @p.block.vector
    def _(e):
        e.wait_ge(p.sem("mm"), mm_cb)
        p.i(e.tensor_copy(CBB[:, 0:4], bk(p, 7, 4)), "vv", 1)
        e.wait_ge(p.sem("vv"), p.c("vv"))
    vv_cbb = p.c("vv")

    OA2 = OO[:, 0:4096]
    OB2 = OO[:, 4096:8192]
    ms3 = p.c("ms")
    va3 = p.c("va")
    sc3 = p.c("sc")
    ma3 = p.c("ma")
    ev3 = p.c("ev")
    mk3 = [p.c("mk0"), p.c("mk1")]

    @p.block.sync
    def _(e):
        for m in range(16):
            qc, kt = divmod(m, 8)
            if m >= 2:
                e.wait_ge(p.sem("va"), va3 + (m - 1) * 4)
            p.i(e.dma_start(
                out=MTB[:, (m % 2) * 512:(m % 2) * 512 + 512],
                in_=maskT_in[kt * 128:(kt + 1) * 128, qc * 512:(qc + 1) * 512]),
                "mk%d" % (m % 2), 16)

    @p.block.tensor
    def _(e):
        j = 0
        for qc in range(2):
            for kt in range(8):
                m = qc * 8 + kt
                e.wait_ge(p.sem("mk%d" % (m % 2)), mk3[m % 2] + 16 * (m // 2 + 1))
                if m == 0:
                    e.wait_ge(p.sem("vv"), vv_cbb)
                if kt == 0 and qc == 1:
                    e.wait_ge(p.sem("ev"), ev3 + 5)
                for h in range(4):
                    if j >= 2:
                        e.wait_ge(p.sem("va"), va3 + j - 1)
                    p.i(e.matmul(
                        bk(p, 5 + (j % 2)),
                        KH[:, h * T + kt * 128: h * T + kt * 128 + 128],
                        QH[:, h * T + qc * 512: h * T + qc * 512 + 512],
                        start=True, stop=True), "ms", 1)
                    e.wait_ge(p.sem("sc"), sc3 + j + 1)
                    eb = EBB[:, (j % 4) * 512:(j % 4) * 512 + 512]
                    e.matmul(bk(p, h),
                             VC[:, kt * 512 + h * 128: kt * 512 + h * 128 + 128],
                             eb, start=(kt == 0), stop=(kt == 7))
                    p.i(e.matmul(p.psf[0:16, 4 * 512:4 * 512 + 512],
                                 zind[:, h * 16:h * 16 + 16], eb,
                                 start=(j % 32 == 0), stop=(j % 32 == 31)),
                        "ma", 1)
                    j += 1

    @p.block.vector
    def _(e):
        j = 0
        for qc in range(2):
            for kt in range(8):
                for h in range(4):
                    e.wait_ge(p.sem("ms"), ms3 + j + 1)
                    if j >= 2:
                        e.wait_ge(p.sem("sc"), sc3 + j - 1)
                    p.i(e.scalar_tensor_tensor(
                        TMP[:, (j % 2) * 512:(j % 2) * 512 + 512],
                        bk(p, 5 + (j % 2)), ISQ,
                        MTB[:, ((qc * 8 + kt) % 2) * 512:
                            ((qc * 8 + kt) % 2) * 512 + 512],
                        OP.mult, OP.add), "va", 1)
                    j += 1
            e.wait_ge(p.sem("ma"), ma3 + (qc + 1) * 32)
            for h in range(4):
                p.i(e.tensor_copy(
                    OA2[:, h * T + qc * 512: h * T + qc * 512 + 512],
                    bk(p, h)), "ev", 1)
            p.i(e.tensor_copy(ZS2[:, qc * 512:(qc + 1) * 512],
                              p.psf[0:4, 4 * 512:4 * 512 + 512]), "ev", 1)

    @p.block.scalar
    def _(e):
        j = 0
        for qc in range(2):
            for kt in range(8):
                for h in range(4):
                    e.wait_ge(p.sem("va"), va3 + j + 1)
                    if j >= 4:
                        e.wait_ge(p.sem("ma"), ma3 + j - 3)
                    p.i(e.activation(EBB[:, (j % 4) * 512:(j % 4) * 512 + 512],
                                     TMP[:, (j % 2) * 512:(j % 2) * 512 + 512],
                                     AF.Exp, bias=CBB[:, h:h + 1]), "sc", 1)
                    j += 1
    ev_att3 = p.c("ev")

    @p.block.vector
    def _(e):
        e.wait_ge(p.sem("ev"), ev_att3)
        p.i(e.reciprocal(ZS2[:, :], ZS2[:, :]), "vv", 1)
        e.wait_ge(p.sem("vv"), p.c("vv"))
        p.i(e.tensor_copy(ZSB2[:, :], ZS2[:, :]), "vv", 1)
        e.wait_ge(p.sem("vv"), p.c("vv"))
    vv_rz3 = p.c("vv")

    @p.block.tensor
    def _(e):
        e.wait_ge(p.sem("vv"), vv_rz3)
        for h in range(4):
            for qc in range(2):
                idx = h * 2 + qc
                if idx >= 2:
                    e.wait_ge(p.sem("vv"), vv_rz3 + idx - 1)
                p.i(e.matmul(bk(p, 5 + (idx % 2)), bsel[0:4, h * 128:(h + 1) * 128],
                             ZSB2[:, qc * 512:(qc + 1) * 512],
                             start=True, stop=True), "mz", 1)
    mz3 = p.c("mz") - 8

    @p.block.vector
    def _(e):
        for h in range(4):
            for qc in range(2):
                idx = h * 2 + qc
                e.wait_ge(p.sem("mz"), mz3 + idx + 1)
                p.i(e.tensor_mul(
                    OB2[:, h * T + qc * 512: h * T + qc * 512 + 512],
                    OA2[:, h * T + qc * 512: h * T + qc * 512 + 512],
                    bk(p, 5 + (idx % 2))), "vv", 1)
    vv_ob2 = p.c("vv")

    def wos_load(e, og, ch, par):
        e.wait_ge(p.sem("wcwos"), 16)
        return e.dma_start(
            out=p.wt[:, par * 1024: par * 1024 + 256],
            in_=wos_bf[ch * 128:(ch + 1) * 128, og * 256:(og + 1) * 256])

    def wos_mms(e, og, ch, par, first, last):
        b = (og % 2) * 4
        for dt in range(2):
            wti = p.wt[:, par * 1024 + dt * 128: par * 1024 + dt * 128 + 128]
            for tc in range(2):
                mm = e.matmul(bk(p, b + dt * 2 + tc), wti,
                              OB2[:, ch * T + tc * 512: ch * T + tc * 512 + 512],
                              start=first, stop=last)
        return mm

    def wos_evac(og):
        def run(mp_target):
            for dt in range(2):
                d_idx = og * 2 + dt
                b = (og % 2) * 4 + dt * 2

                @p.block.vector
                def _(e):
                    e.wait_ge(p.sem("mp"), mp_target)
                    e.wait_ge(p.sem("st%d" % (d_idx % 2)), p.c("st%d" % (d_idx % 2)))
                    p.i(e.tensor_copy(
                        XIO[:, 2048 + (d_idx % 2) * 1024:
                            2048 + (d_idx % 2) * 1024 + 1024],
                        p.psf[:, b * 512:(b + 2) * 512]), "vv", 1)
                    if dt == 1:
                        p.i(e.engine_nop(), "ev", 1)
                vv_t = p.c("vv")

                @p.block.gpsimd
                def _(e):
                    e.wait_ge(p.sem("vv"), vv_t)
                    p.i(e.dma_start(
                        out=part_out[:, d_idx * T:(d_idx + 1) * T],
                        in_=XIO[:, 2048 + (d_idx % 2) * 1024:
                                2048 + (d_idx % 2) * 1024 + 1024]),
                        "st%d" % (d_idx % 2), 16)
        return run

    emit_proj_chstream(p, 8, 4, wos_load, wos_mms,
                       [wos_evac(g) for g in range(8)], psum_lag=2,
                       rdy=[("vv", vv_ob2)])

    @p.block.sync
    def _(e):
        e.wait_ge(p.sem("vv"), vv_h1)
        p.i(e.dma_start(out=filt_out[:, :], in_=FB[:, :]), "so", 16)
        e.wait_ge(p.sem("so"), p.c("so"))
        e.wait_ge(p.sem("st0"), p.c("st0"))
        e.wait_ge(p.sem("st1"), p.c("st1"))


def emit_L4(p, io, convert=True):
    filt_in = io["filt_fm"]
    pa_in = io["pa"]
    g2_in = io["g2_fm"]
    w1s_in = io["w1s"]
    w3s_in = io["w3s"]
    w2s_in = io["w2s"]
    pffn_out = io["pffn"]
    xaa_out = io["xaa"]

    wbf = {n: io[n + "_bf"] for n in ("w1s", "w3s", "w2s")}

    FB = p.sb("FB", [128, 16384], BF16)
    XAA = p.sb("XAA", [128, 16384], F32)
    H2 = p.sb("H2", [128, 16384], BF16)
    HC = p.sb("HC", [128, 16384], BF16)
    XIO = p.sb("XIO", [128, 4096], F32)
    SQ = p.sb("SQ", [128, 1024], BF16)
    STT = p.sb("STT", [128, 2048], F32)
    GB = p.sb("GB", [128, 1024], BF16)
    GM = p.sb("GM", [128, 16], F32)
    PIO = HC.bitcast(F32)

    ones_bf = p.cs("ones_bf")

    if convert:
        @p.block.gpsimd
        def _(e):
            for n, src_ in (("w1s", w1s_in), ("w3s", w3s_in), ("w2s", w2s_in)):
                p.i(e.dma_start(out=wbf[n][:, :], in_=src_[:, :]), "wc" + n, 16)

    @p.block.sync
    def _(e):
        p.i(e.dma_start(out=FB[:, :], in_=filt_in[:, :]), "cf", 16)
        p.i(e.dma_start(out=GM[:, :], in_=g2_in[:, :]), "cf", 16)
    cf0 = p.c("cf")

    vva_base = p.c("vv")
    pl_base = [p.c("pl0"), p.c("pl1")]

    @p.block.sync
    def _(e):
        for ci in range(NCH):
            par = ci % 2
            if ci >= 2:
                e.wait_ge(p.sem("vv"), vva_base + (ci - 1))
            p.i(e.dma_start(
                out=PIO[:, par * 4096: par * 4096 + 1024],
                in_=pa_in[:, ci * T:(ci + 1) * T]), "pl%d" % par, 16)

    @p.block.vector
    def _(e):
        for ci in range(NCH):
            par = ci % 2
            e.wait_ge(p.sem("pl%d" % par), pl_base[par] + 16 * (ci // 2 + 1))
            if ci == 0:
                e.wait_ge(p.sem("cf"), cf0)
            dst = XAA[:, ci * T:(ci + 1) * T]
            p.i(e.tensor_add(dst, FB[:, ci * T:(ci + 1) * T],
                             PIO[:, par * 4096: par * 4096 + 1024]), "vv", 1)
            e.wait_ge(p.sem("vv"), p.c("vv"))
    vv_xaa = p.c("vv")

    @p.block.sync
    def _(e):
        e.wait_ge(p.sem("vv"), vv_xaa)
        p.i(e.dma_start(out=xaa_out[:, :], in_=XAA[:, :]), "so", 16)

    @p.block.vector
    def _(e):
        e.wait_ge(p.sem("vv"), vv_xaa)
        p.i(e.tensor_copy(FB[:, :], XAA[:, :]), "vv", 1)
    vv_xb2 = p.c("vv")

    sq_base = p.c("sc")
    mmr_base = p.c("mm")

    @p.block.scalar
    def _(e):
        e.wait_ge(p.sem("vv"), vv_xb2)
        for k in range(32):
            tc, ch = divmod(k, NCH)
            if k >= 2:
                e.wait_ge(p.sem("mm"), mmr_base + k - 1)
            p.i(e.activation(SQ[:, (k % 2) * 512:(k % 2) * 512 + 512],
                             FB[:, ch * T + tc * 512: ch * T + tc * 512 + 512],
                             AF.Square), "sc", 1)

    @p.block.tensor
    def _(e):
        e.wait_ge(p.sem("cl"), p.c("cl"))
        for k in range(32):
            tc, ch = divmod(k, NCH)
            e.wait_ge(p.sem("sc"), sq_base + k + 1)
            p.i(e.matmul(bk(p, tc), ones_bf[:, :],
                         SQ[:, (k % 2) * 512:(k % 2) * 512 + 512],
                         start=(ch == 0), stop=(ch == NCH - 1)), "mm", 1)
    mm_rs = p.c("mm")

    @p.block.scalar
    def _(e):
        e.wait_ge(p.sem("mm"), mm_rs)
        for tc in range(2):
            p.i(e.activation(STT[:, tc * 512:(tc + 1) * 512], bk(p, tc),
                             AF.Sqrt, scale=1.0 / D, bias=EPS_RMS), "sc", 1)
    sc_rs = p.c("sc")

    @p.block.vector
    def _(e):
        e.wait_ge(p.sem("sc"), sc_rs)
        for tc in range(2):
            p.i(e.reciprocal(STT[:, tc * 512:(tc + 1) * 512],
                             STT[:, tc * 512:(tc + 1) * 512]), "vv", 1)
        e.wait_ge(p.sem("vv"), p.c("vv"))
        for tc in range(2):
            for ch in range(NCH):
                sl = slice(ch * T + tc * 512, ch * T + tc * 512 + 512)
                p.i(e.scalar_tensor_tensor(
                    H2[:, sl], FB[:, sl], GM[:, ch:ch + 1],
                    STT[:, tc * 512:(tc + 1) * 512], OP.mult, OP.mult), "vv", 1)
    vv_h2 = p.c("vv")

    def gu_load(e, og, ch, par):
        e.wait_ge(p.sem("wcw1s"), 16)
        e.wait_ge(p.sem("wcw3s"), 16)
        e.dma_start(out=p.wt[:, par * 1024: par * 1024 + 128],
                    in_=wbf["w1s"][ch * 128:(ch + 1) * 128,
                                   og * 128:(og + 1) * 128]).then_inc(
                                       p.sem("wl%d" % par), 16)
        p.cnt["wl%d" % par] += 16
        return e.dma_start(out=p.wt[:, par * 1024 + 128: par * 1024 + 256],
                           in_=wbf["w3s"][ch * 128:(ch + 1) * 128,
                                          og * 128:(og + 1) * 128])

    def gu_mms(e, og, ch, par, first, last):
        b = (og % 2) * 4
        gt = p.wt[:, par * 1024: par * 1024 + 128]
        ut = p.wt[:, par * 1024 + 128: par * 1024 + 256]
        for tc in range(2):
            e.matmul(bk(p, b + tc), gt,
                     H2[:, ch * T + tc * 512: ch * T + tc * 512 + 512],
                     start=first, stop=last)
            mm = e.matmul(bk(p, b + 2 + tc), ut,
                          H2[:, ch * T + tc * 512: ch * T + tc * 512 + 512],
                          start=first, stop=last)
        return mm

    def gu_evac(og):
        def run(mp_target):
            b = (og % 2) * 4

            @p.block.scalar
            def _(e):
                e.wait_ge(p.sem("mp"), mp_target)
                e.wait_ge(p.sem("ev"), p.c("ev"))
                p.i(e.activation(GB[:, :], p.psf[:, b * 512:(b + 2) * 512],
                                 AF.Silu), "sc", 1)
            sc_t = p.c("sc")

            @p.block.vector
            def _(e):
                e.wait_ge(p.sem("sc"), sc_t)
                p.i(e.tensor_mul(HC[:, og * T:(og + 1) * T], GB[:, :],
                                 p.psf[:, (b + 2) * 512:(b + 4) * 512]), "ev", 1)
        return run

    base_wl = [p.c("wl0"), p.c("wl1")]
    base_mp = p.c("mp")
    base_ev = p.c("ev")
    n_og, nch = 16, NCH
    evacs = [gu_evac(g) for g in range(n_og)]

    @p.block.sync
    def _(e):
        for og in range(n_og):
            for ch in range(nch):
                k = og * nch + ch
                e.wait_ge(p.sem("mp"), base_mp + max(0, k - 1))
                p.i(gu_load(e, og, ch, k % 2), "wl%d" % (k % 2), 16)

    @p.block.tensor
    def _(e):
        for og in range(n_og):
            for ch in range(nch):
                k = og * nch + ch
                e.wait_ge(p.sem("wl%d" % (k % 2)),
                          base_wl[k % 2] + 32 * (k // 2 + 1))
                if k == 0:
                    e.wait_ge(p.sem("vv"), vv_h2)
                if ch == 0:
                    e.wait_ge(p.sem("ev"), base_ev + max(0, og - 1))
                last = gu_mms(e, og, ch, k % 2, ch == 0, ch == nch - 1)
                p.i(last, "mp", 1)

    for og in range(n_og):
        evacs[og](base_mp + (og + 1) * nch)
    ev_hc = p.c("ev")

    def dn_load(e, og, ch, par):
        e.wait_ge(p.sem("wcw2s"), 16)
        return e.dma_start(
            out=p.wt[:, par * 1024: par * 1024 + 256],
            in_=wbf["w2s"][ch * 128:(ch + 1) * 128, og * 256:(og + 1) * 256])

    def dn_mms(e, og, ch, par, first, last):
        b = (og % 2) * 4
        for dt in range(2):
            wti = p.wt[:, par * 1024 + dt * 128: par * 1024 + dt * 128 + 128]
            for tc in range(2):
                mm = e.matmul(bk(p, b + dt * 2 + tc), wti,
                              HC[:, ch * T + tc * 512: ch * T + tc * 512 + 512],
                              start=first, stop=last)
        return mm

    def dn_evac(og):
        def run(mp_target):
            for dt in range(2):
                d_idx = og * 2 + dt
                b = (og % 2) * 4 + dt * 2

                @p.block.vector
                def _(e):
                    e.wait_ge(p.sem("mp"), mp_target)
                    e.wait_ge(p.sem("st%d" % (d_idx % 2)), p.c("st%d" % (d_idx % 2)))
                    p.i(e.tensor_copy(
                        XIO[:, 2048 + (d_idx % 2) * 1024:
                            2048 + (d_idx % 2) * 1024 + 1024],
                        p.psf[:, b * 512:(b + 2) * 512]), "vv", 1)
                    if dt == 1:
                        p.i(e.engine_nop(), "ev", 1)
                vv_t = p.c("vv")

                @p.block.gpsimd
                def _(e):
                    e.wait_ge(p.sem("vv"), vv_t)
                    p.i(e.dma_start(
                        out=pffn_out[:, d_idx * T:(d_idx + 1) * T],
                        in_=XIO[:, 2048 + (d_idx % 2) * 1024:
                                2048 + (d_idx % 2) * 1024 + 1024]),
                        "st%d" % (d_idx % 2), 16)
        return run

    emit_proj_chstream(p, 8, NCH, dn_load, dn_mms, [dn_evac(g) for g in range(8)],
                       psum_lag=2, rdy=[("ev", ev_hc)])

    @p.block.sync
    def _(e):
        e.wait_ge(p.sem("so"), p.c("so"))
        e.wait_ge(p.sem("st0"), p.c("st0"))
        e.wait_ge(p.sem("st1"), p.c("st1"))


def emit_L5(p, io, qkv=True):
    x3_in = io["x3_fm"]
    lnx3_in = io["lnx3_fm"]
    lg_full = io["lg_full"]
    lg_grid = io["lg_grid"]
    xaa_in = io["xaa"]
    pf_in = io["pf"]
    selcomb = io["selcomb"]
    sel8_in = io["sel8"]
    if qkv:
        wqkv_in = io["wqkv"]
        gq_in = io["gq"]
        gqr_in = io["gqr"]
        gk_in = io["gk"]
        gkr_in = io["gkr"]
        cosa_in = io["cosa"]
        sina_in = io["sina"]

    x4_out = io["x4_fm"]
    if qkv:
        q_out = io["q_fm"]
        k_out = io["k_fm"]
        v_out = io["v_fm"]
        nrm_out = io["qknorm"]
        wqkv_bf = io["wqkv_bf"]
        qraw = io["qraw"]
        kraw = io["kraw"]

    selrows = io["selrows"]
    xout_tm = io["xout_tm"]

    X4 = p.sb("X4", [128, 16384], F32)
    BIGB = p.sb("BIGB", [128, 16384], BF16)
    BIGC = p.sb("BIGC", [128, 16384], BF16)
    LNB = p.sb("LNB", [128, 16384], BF16)
    SQ = p.sb("SQ", [128, 1024], BF16)
    STT = p.sb("STT", [128, 2048], F32)
    LSB = p.sb("LSB", [128, 32], F32)
    ST = p.sb("ST", [1, 16], F32)
    ROFS = p.sb("ROFS", [128, 16], F32)
    ROFI = p.sb("ROFI", [128, 8], I32)
    HT = p.sb("HT", [128, 2048], BF16)
    HT2 = p.sb("HT2", [128, 2048], BF16)
    RSD = p.sb("RSD", [128, 1024], F32)
    RT2 = p.sb("RT2", [128, 2048], F32)
    GMS = p.sb("GMS", [128, 4], F32)
    NB = p.sb("NB", [1, 36], F32)
    BF32 = BIGB.bitcast(F32)
    PIO = BF32[:, 0:8192]
    GX = BF32[:, 0:2048]
    TMPA = BF32[:, 2048:3072]
    TMPB = BF32[:, 3072:4096]
    TMB = [BF32[:, 4096:6144], BF32[:, 6144:8192]]

    ones_bf = p.cs("ones_bf")
    ones_f = p.cs("ones_f")
    eye_f = p.cs("eye_f")
    p64 = p.cs("p64_bf")
    H1F = LNB.bitcast(F32)
    p.selrow_aps = [H1F[0:1, 0:4096], H1F[0:1, 4096:8192],
                    BIGC.bitcast(F32)[0:1, 0:4096]]
    LR = BF32[0:1, 0:4096]

    emit_selection(p, lg_full, lg_grid, selrows, LR, LSB, ST)
    se_done = p.c("se")

    SR = p.sb("SR", [32, 384], F32)
    SEL8S = p.sb("SEL8S", [32, 8], F32)

    @p.block.sync
    def _(e):
        e.wait_ge(p.sem("se"), se_done)
        p.i(e.dma_start(out=SR[:, :], in_=selcomb[:, :]), "my", 16)
        p.i(e.dma_start(out=SEL8S[:, :], in_=sel8_in[:, :]), "my", 16)
    my_done = p.c("my")

    @p.block.tensor
    def _(e):
        e.wait_ge(p.sem("my"), my_done)
        for rr in range(2):
            p.i(e.matmul(bk(p, 5 + rr, 8), SR[0:32, rr * 128:(rr + 1) * 128],
                         SEL8S[0:32, :], start=True, stop=True), "mt", 1)
    mt_cmr = p.c("mt")

    @p.block.vector
    def _(e):
        e.wait_ge(p.sem("mt"), mt_cmr)
        p.i(e.tensor_scalar_add(ROFS[:, 0:8], bk(p, 5, 8), -1.0), "vv", 1)
        p.i(e.tensor_scalar(ROFS[:, 8:16], bk(p, 6, 8), -1.0, -1000000.0,
                            OP.add, OP.mult), "vv", 1)
        e.wait_ge(p.sem("vv"), p.c("vv"))
        p.i(e.tensor_add(ROFS[:, 0:8], ROFS[:, 0:8], ROFS[:, 8:16]), "vv", 1)
        e.wait_ge(p.sem("vv"), p.c("vv"))
        p.i(e.tensor_copy(ROFI[:, :], ROFS[:, 0:8]), "vv", 1)
    vv_rofi = p.c("vv")

    @p.block.tensor
    def _(e):
        e.wait_ge(p.sem("vv"), vv_rofi)
        p.i(e.matmul(bk(p, 5, 8), SR[0:32, 2 * 128:3 * 128], SEL8S[0:32, :],
                     start=True, stop=True), "mt", 1)
    mt_rw = p.c("mt")

    @p.block.vector
    def _(e):
        e.wait_ge(p.sem("mt"), mt_rw)
        p.i(e.tensor_scalar_mul(ROFS[:, 8:16], bk(p, 5, 8), 0.5), "vv", 1)
    vv_rwh = p.c("vv")

    vva_base = p.c("vv")
    pl_base = [p.c("pl0"), p.c("pl1")]

    @p.block.sync
    def _(e):
        for ci in range(NCH):
            par = ci % 2
            if ci >= 2:
                e.wait_ge(p.sem("vv"), vva_base + (ci - 1))
            p.i(e.dma_start(out=PIO[:, par * 4096 + 3072: par * 4096 + 4096],
                            in_=xaa_in[:, ci * T:(ci + 1) * T]), "pl%d" % par, 16)
            p.i(e.dma_start(
                out=PIO[:, par * 4096: par * 4096 + 1024],
                in_=pf_in[:, ci * T:(ci + 1) * T]), "pl%d" % par, 16)

    @p.block.vector
    def _(e):
        for ci in range(NCH):
            par = ci % 2
            e.wait_ge(p.sem("pl%d" % par), pl_base[par] + 32 * (ci // 2 + 1))
            if ci == 0:
                e.wait_ge(p.sem("vv"), vv_rwh)
            dst = X4[:, ci * T:(ci + 1) * T]
            p.i(e.tensor_add(dst, PIO[:, par * 4096 + 3072: par * 4096 + 4096],
                             PIO[:, par * 4096: par * 4096 + 1024]), "vv", 1)
            e.wait_ge(p.sem("vv"), p.c("vv"))
    vv_xout = p.c("vv")

    mtx_base = p.c("mt")
    vvx_base = p.c("vv")
    xs_base = [p.c("xs0"), p.c("xs1")]
    xt_base = p.c("xt")

    @p.block.tensor
    def _(e):
        e.wait_ge(p.sem("vv"), vv_xout)
        for tt in range(8):
            for ci in range(NCH):
                tix = tt * NCH + ci
                if tix >= 2:
                    e.wait_ge(p.sem("vv"), vvx_base + tix - 1)
                p.i(e.transpose(bk(p, tix % 2, 128),
                                X4[:, ci * T + tt * 128: ci * T + (tt + 1) * 128],
                                eye_f[:, :]), "mt", 1)

    @p.block.vector
    def _(e):
        for tt in range(8):
            for ci in range(NCH):
                tix = tt * NCH + ci
                e.wait_ge(p.sem("mt"), mtx_base + tix + 1)
                if ci == 0 and tt >= 2:
                    e.wait_ge(p.sem("xs%d" % (tt % 2)),
                              xs_base[tt % 2] + 16 * (tt // 2))
                p.i(e.tensor_copy(TMB[tt % 2][:, ci * 128:(ci + 1) * 128],
                                  bk(p, tix % 2, 128)), "vv", 1)
            if True:
                e.wait_ge(p.sem("vv"), p.c("vv"))
                p.i(e.engine_nop(), "xt", 1)

    @p.block.gpsimd
    def _(e):
        for tt in range(8):
            e.wait_ge(p.sem("xt"), xt_base + tt + 1)
            p.i(e.dma_start(out=xout_tm[tt * 128:(tt + 1) * 128, :],
                            in_=TMB[tt % 2][:, :]), "xs%d" % (tt % 2), 16)
    xs_done = (p.c("xs0"), p.c("xs1"))

    vvb_base = p.c("vv")
    xb_base = [p.c("xb0"), p.c("xb1")]

    @p.block.sync
    def _(e):
        e.wait_ge(p.sem("xs0"), xs_done[0])
        e.wait_ge(p.sem("xs1"), xs_done[1])
        for ci in range(NCH):
            par = ci % 2
            if ci >= 2:
                e.wait_ge(p.sem("vv"), vvb_base + ci - 1)
            p.i(e.dma_start(out=PIO[:, par * 4096: par * 4096 + 1024],
                            in_=x3_in[:, ci * T:(ci + 1) * T]), "xb%d" % par, 16)
            p.i(e.dma_start(
                out=PIO[:, par * 4096 + 1024: par * 4096 + 2048].bitcast(BF16)
                [:, 0:1024],
                in_=lnx3_in[:, ci * T:(ci + 1) * T]), "xb%d" % par, 16)

    @p.block.vector
    def _(e):
        for ci in range(NCH):
            par = ci % 2
            e.wait_ge(p.sem("xb%d" % par), xb_base[par] + 32 * (ci // 2 + 1))
            p.i(e.scalar_tensor_tensor(
                X4[:, ci * T:(ci + 1) * T],
                PIO[:, par * 4096 + 1024: par * 4096 + 2048].bitcast(BF16)[:, 0:1024],
                0.5, PIO[:, par * 4096: par * 4096 + 1024],
                OP.mult, OP.add), "vv", 1)
            e.wait_ge(p.sem("vv"), p.c("vv"))
    vv_x4b = p.c("vv")

    vvS_base = p.c("vv")
    for jj in range(8):
        @p.block.gpsimd
        def _(e):
            if jj == 0:
                e.wait_ge(p.sem("vv"), vv_x4b)
            else:
                e.wait_ge(p.sem("vv"), p.c("vv"))
            p.i(e.memset(GX[:, :], 0.0), "gm", 1)
            e.wait_ge(p.sem("gm"), p.c("gm"))
            p.i(e.indirect_dma_start(
                GX[:, :], None, xout_tm[:, :],
                IndirectOffsetOnAxis(ROFI[:, jj:jj + 1], 0),
                bounds_check=TOPK - 1, oob_is_err=False,
                compute_op=OP.add), "gx", 16)
        gx_done = p.c("gx")

        @p.block.vector
        def _(e):
            e.wait_ge(p.sem("gx"), gx_done)
            p.i(e.tensor_scalar(GX[:, :], GX[:, :], ROFS[:, 8 + jj:9 + jj], None,
                                OP.mult), "vv", 1)
        vv_gx = p.c("vv")

        @p.block.tensor
        def _(e):
            e.wait_ge(p.sem("vv"), vv_gx)
            for ci in range(NCH):
                tix = jj * NCH + ci
                if tix >= 2:
                    k = tix - 2
                    e.wait_ge(p.sem("vv"), vvS_base + k + k // NCH + 2)
                p.i(e.transpose(bk(p, 2 + tix % 2, 128), GX[:, ci * 128:(ci + 1) * 128],
                                eye_f[:, :]), "mg", 1)

        mg_c = p.c("mg")

        @p.block.vector
        def _(e):
            for ci in range(NCH):
                tix = jj * NCH + ci
                e.wait_ge(p.sem("mg"), mg_c - NCH + ci + 1)
                p.i(e.tensor_add(
                    X4[:, ci * T + jj * 128: ci * T + (jj + 1) * 128],
                    X4[:, ci * T + jj * 128: ci * T + (jj + 1) * 128],
                    bk(p, 2 + tix % 2, 128)), "vv", 1)
    vv_x4 = p.c("vv")

    @p.block.sync
    def _(e):
        e.wait_ge(p.sem("vv"), vv_x4)
        p.i(e.dma_start(out=x4_out[:, :], in_=X4[:, :]), "so", 16)
    so_x4 = p.c("so")

    if not qkv:
        @p.block.sync
        def _(e):
            e.wait_ge(p.sem("so"), p.c("so"))
        return

    @p.block.sync
    def _(e):
        e.wait_ge(p.sem("so"), so_x4)
    vv_ln4 = emit_ln_fm(p, x4_out, BIGC, LNB, SQ, STT)

    @p.block.gpsimd
    def _(e):
        for b_ in range(3):
            if b_:
                e.wait_ge(p.sem("wcq"), p.c("wcq"))
            p.i(e.dma_start(out=wqkv_bf[:, b_ * D:(b_ + 1) * D],
                            in_=wqkv_in[:, b_ * D:(b_ + 1) * D]), "wcq", 16)
        p.i(e.dma_start(out=RT2[:, 0:1024], in_=cosa_in[:, :]), "rt", 16)
        p.i(e.dma_start(out=RT2[:, 1024:2048], in_=sina_in[:, :]), "rt", 16)
        p.i(e.dma_start(out=GMS[:, 0:1], in_=gq_in[:, :]), "rt", 16)
        p.i(e.dma_start(out=GMS[:, 1:2], in_=gqr_in[:, :]), "rt", 16)
        p.i(e.dma_start(out=GMS[:, 2:3], in_=gk_in[:, :]), "rt", 16)
        p.i(e.dma_start(out=GMS[:, 3:4], in_=gkr_in[:, :]), "rt", 16)
    rt_done = p.c("rt")

    def mk_qkraw(col0, dst_dram):
        def load(e, og, ch, par):
            e.wait_ge(p.sem("wcq"), 16 * ((col0 + (og + 1) * 256 + D - 1) // D))
            return e.dma_start(
                out=p.wt[:, par * 1024: par * 1024 + 256],
                in_=wqkv_bf[ch * 128:(ch + 1) * 128,
                            col0 + og * 256: col0 + (og + 1) * 256])

        def mms(e, og, ch, par, first, last):
            b = (og % 2) * 4
            for hd in range(2):
                wti = p.wt[:, par * 1024 + hd * 128: par * 1024 + hd * 128 + 128]
                for tc in range(2):
                    mm = e.matmul(bk(p, b + hd * 2 + tc), wti,
                                  LNB[:, ch * T + tc * 512: ch * T + tc * 512 + 512],
                                  start=first, stop=last)
            return mm

        def evac(og):
            def run(mp_target):
                @p.block.vector
                def _(e):
                    e.wait_ge(p.sem("mp"), mp_target)
                    e.wait_ge(p.sem("qs"), p.c("qs"))
                    b = (og % 2) * 4
                    for hd in range(2):
                        p.i(e.tensor_copy(
                            HT[:, hd * 1024:(hd + 1) * 1024],
                            p.psf[:, (b + hd * 2) * 512:(b + hd * 2 + 2) * 512]),
                            "ev", 1 if hd == 1 else 0)
                vv_t = p.c("ev")

                @p.block.gpsimd
                def _(e):
                    e.wait_ge(p.sem("ev"), vv_t)
                    p.i(e.dma_start(out=dst_dram[:, og * 2048:(og + 1) * 2048],
                                    in_=HT[:, :]), "qs", 16)
            return run
        return load, mms, [evac(g) for g in range(8)]

    lq, mq, eq = mk_qkraw(0, qraw)
    emit_proj_chstream(p, 8, NCH, lq, mq, eq, psum_lag=2, rdy=[("vv", vv_ln4)])
    lk, mk2, ek = mk_qkraw(D, kraw)
    emit_proj_chstream(p, 8, NCH, lk, mk2, ek, psum_lag=2)

    lv5, mv5, ev5 = mk_qkraw(2 * D, v_out)
    emit_proj_chstream(p, 8, NCH, lv5, mv5, ev5, psum_lag=2)

    nbase = dict(mm=p.c("mm"), sc=p.c("sc"), vv=p.c("vv"), hl0=p.c("hl0"),
                 hl1=p.c("hl1"), qf0=p.c("qf0"), qf1=p.c("qf1"))
    qs_all = p.c("qs")

    @p.block.sync
    def _(e):
        e.wait_ge(p.sem("qs"), qs_all)
        for i32 in range(32):
            par = i32 % 2
            if i32 >= 2:
                e.wait_ge(p.sem("vv"), nbase["vv"] + (i32 - 1) * 8)
            src_ = qraw if i32 < 16 else kraw
            h = i32 % 16
            p.i(e.dma_start(out=HT[:, par * 1024:(par + 1) * 1024],
                            in_=src_[:, h * 1024:(h + 1) * 1024]),
                "hl%d" % par, 16)

    for i32 in range(32):
        par = i32 % 2
        h = i32 % 16
        isq = i32 < 16
        gcol = 0 if isq else 2
        dst_dram = q_out if isq else k_out
        nbcol = i32

        @p.block.scalar
        def _(e):
            e.wait_ge(p.sem("hl%d" % par), nbase["hl%d" % par] + 16 * (i32 // 2 + 1))
            if i32 >= 1:
                e.wait_ge(p.sem("mm"), nbase["mm"] + 6 * i32)
            p.i(e.activation(SQ[:, :], HT[:, par * 1024:(par + 1) * 1024],
                             AF.Square), "sc", 1)
        sc1 = p.c("sc")

        @p.block.tensor
        def _(e):
            e.wait_ge(p.sem("sc"), sc1)
            if i32 >= 1:
                e.wait_ge(p.sem("vv"), nbase["vv"] + (i32 - 1) * 8 + 7)
            for tc in range(2):
                p.i(e.matmul(bk(p, 4 + tc), ones_bf[:, :],
                             SQ[:, tc * 512:(tc + 1) * 512],
                             start=True, stop=True), "mm", 1)
        mm1 = p.c("mm")

        @p.block.scalar
        def _(e):
            e.wait_ge(p.sem("mm"), mm1)
            p.i(e.activation(RSD[:, :], p.psf[:, 4 * 512:4 * 512 + 1024],
                             AF.Sqrt, scale=1.0 / DH, bias=EPS_RMS), "sc", 1)
        sc2 = p.c("sc")

        @p.block.tensor
        def _(e):
            e.wait_ge(p.sem("sc"), sc1)
            if i32 >= 1:
                e.wait_ge(p.sem("vv"), nbase["vv"] + (i32 - 1) * 8 + 3)
            for tc in range(2):
                p.i(e.matmul(bk(p, 6 + tc), p64[:, :],
                             HT[:, par * 1024 + tc * 512:
                                par * 1024 + (tc + 1) * 512],
                             start=True, stop=True), "mm", 1)
        mm2 = p.c("mm")

        @p.block.vector
        def _(e):
            e.wait_ge(p.sem("sc"), sc2)
            p.i(e.reciprocal(RSD[:, :], RSD[:, :]), "vv", 1)
            e.wait_ge(p.sem("vv"), p.c("vv"))
            e.wait_ge(p.sem("rt"), rt_done)
            p.i(e.scalar_tensor_tensor(TMPA[:, :],
                                       HT[:, par * 1024:(par + 1) * 1024],
                                       GMS[:, gcol:gcol + 1], RT2[:, 0:1024],
                                       OP.mult, OP.mult), "vv", 1)
            e.wait_ge(p.sem("mm"), mm2)
            p.i(e.scalar_tensor_tensor(TMPB[:, :],
                                       p.psf[:, 6 * 512:6 * 512 + 1024],
                                       GMS[:, gcol + 1:gcol + 2], RT2[:, 1024:2048],
                                       OP.mult, OP.mult), "vv", 1)
            e.wait_ge(p.sem("vv"), p.c("vv"))
            p.i(e.tensor_add(TMPA[:, :], TMPA[:, :], TMPB[:, :]), "vv", 1)
            e.wait_ge(p.sem("vv"), p.c("vv"))
            if i32 >= 2:
                e.wait_ge(p.sem("qf%d" % par),
                          nbase["qf%d" % par] + 16 * (i32 // 2))
            p.i(e.tensor_mul(HT2[:, par * 1024:(par + 1) * 1024], TMPA[:, :],
                             RSD[:, :]), "vv", 1)
            e.wait_ge(p.sem("vv"), p.c("vv"))
        sc3_b = p.c("sc")

        @p.block.scalar
        def _(e):
            e.wait_ge(p.sem("vv"), p.c("vv"))
            p.i(e.activation(SQ[:, :], HT2[:, par * 1024:(par + 1) * 1024],
                             AF.Square), "sc", 1)
        sc3 = p.c("sc")

        @p.block.tensor
        def _(e):
            e.wait_ge(p.sem("sc"), sc3)
            for tc in range(2):
                p.i(e.matmul(bk(p, 4 + tc), ones_bf[:, :],
                             SQ[:, tc * 512:(tc + 1) * 512],
                             start=True, stop=True), "mm", 1)
        mm3 = p.c("mm")

        @p.block.vector
        def _(e):
            e.wait_ge(p.sem("mm"), mm3)
            p.i(e.reduce_max(NB[0:1, 32:33], bk(p, 4)[0:1, :],
                             axis=mybir.AxisListType.X), "vv", 1)
            p.i(e.reduce_max(NB[0:1, 33:34], bk(p, 5)[0:1, :],
                             axis=mybir.AxisListType.X), "vv", 1)
            e.wait_ge(p.sem("vv"), p.c("vv"))
            p.i(e.tensor_max(NB[0:1, nbcol:nbcol + 1], NB[0:1, 32:33],
                             NB[0:1, 33:34]), "vv", 1)
            e.wait_ge(p.sem("vv"), p.c("vv"))
        vv_h = p.c("vv")

        @p.block.gpsimd
        def _(e):
            e.wait_ge(p.sem("vv"), vv_h)
            p.i(e.dma_start(out=dst_dram[:, h * 1024:(h + 1) * 1024],
                            in_=HT2[:, par * 1024:(par + 1) * 1024]),
                "qf%d" % par, 16)
    vv_nb_all = p.c("vv")

    @p.block.sync
    def _(e):
        e.wait_ge(p.sem("vv"), vv_nb_all)
        p.i(e.dma_start(out=nrm_out[:, :], in_=NB[0:1, 0:32]), "so", 16)
        e.wait_ge(p.sem("so"), p.c("so"))
        e.wait_ge(p.sem("qf0"), p.c("qf0"))
        e.wait_ge(p.sem("qf1"), p.c("qf1"))


_PROG_CACHE = {}


def get_prog(name, builder, *a, **k):
    if name not in _PROG_CACHE:
        _PROG_CACHE[name] = builder(*a, **k)
    return _PROG_CACHE[name]


def _launch(p, in_maps):
    res = run_bass_kernel_spmd(p.nc, in_maps, core_ids=list(range(8)))
    return res.results


def _rope_tables(r):
    pos = np.arange(r, r + T, dtype=np.float64)
    inv = 1.0 / (10000.0 ** (np.arange(0, DH, 2, dtype=np.float64) / DH))
    emb = pos[:, None] * inv[None, :]
    cosa = np.empty((128, T), np.float32)
    sina = np.empty((128, T), np.float32)
    for pp in range(128):
        cosa[pp] = np.cos(emb[:, pp % 64])
        sina[pp] = np.sin(emb[:, pp % 64]) * (-1.0 if pp < 64 else 1.0)
    return cosa, sina


def host_rotq(qi):
    out = np.zeros((128, 4), np.int32)
    pp = np.arange(128, dtype=np.int32)
    for a in range(4):
        out[:, a] = 128 * ((qi + a) % 4) + pp
    return out


def host_sel8(qi):
    out = np.zeros((32, 8), np.float32)
    for i in range(8):
        out[qi * 8 + i, i] = 1.0
    return out


def build_mega(upto="I"):
    p = Prog("MEGA_" + upto)
    cut_b = (upto == "B")

    x_fm = p.din("x_fm", [128, NCH * T], F32)
    wgu = p.din("wgu", [2 * INTER, D], F32)
    wdn = p.din("wdn", [D, INTER], F32)
    wk = p.din("wk", [D, D], F32)
    wq = p.din("wq", [D, D], F32)
    wv = p.din("wv", [D, D], F32)
    wo0 = p.din("wo0", [D, D], F32)
    wr = p.din("wr_fm", [128, 16], F32)
    cbias = p.din("cbias", [128, 64], F32)
    rotq = p.din("rotq", [128, 4], I32)
    diag = p.din("c_diag_bf", [128, 2048], BF16)
    if not cut_b:
        g1 = p.din("g1_fm", [128, 16], F32)
        g2 = p.din("g2_fm", [128, 16], F32)
        wqs = p.din("wqs", [D, 512], F32)
        wks = p.din("wks", [D, 512], F32)
        wvs = p.din("wvs", [D, 512], F32)
        wos = p.din("wos", [512, D], F32)
        maskT = p.din("maskT", [TOPK, TOPK], F32)
        cosr = p.din("cosr", [128, TOPK], F32)
        sinr = p.din("sinr", [128, TOPK], F32)
        w1s = p.din("w1s", [D, D], F32)
        w3s = p.din("w3s", [D, D], F32)
        w2s = p.din("w2s", [D, D], F32)
        sel8 = p.din("sel8", [32, 8], F32)
        wqkv = p.din("wqkv", [D, 3 * D], F32)
        gq = p.din("gq", [128, 1], F32)
        gqr = p.din("gqr", [128, 1], F32)
        gk_ = p.din("gk", [128, 1], F32)
        gkr = p.din("gkr", [128, 1], F32)
        cosa = p.din("cosa", [128, T], F32)
        sina = p.din("sina", [128, T], F32)
        wo1 = p.din("wo1", [D, D], F32)

    wgu_bf = p.dint("wgu_bf", [2 * INTER, D], BF16)
    wdn_bf = p.dint("wdn_bf", [D, INTER], BF16)
    wk_bf = p.dint("wk_bf", [D, D], BF16)
    wq_bf = p.dint("wq_bf", [D, D], BF16)
    wv_bf = p.dint("wv_bf", [D, D], BF16)
    wo0_bf = p.dint("wo0_bf", [D, D], BF16)
    if not cut_b:
        wqs_bf = p.dint("wqs_bf", [D, 512], BF16)
        wks_bf = p.dint("wks_bf", [D, 512], BF16)
        wvs_bf = p.dint("wvs_bf", [D, 512], BF16)
        wos_bf = p.dint("wos_bf", [512, D], BF16)
        w1s_bf = p.dint("w1s_bf", [D, D], BF16)
        w3s_bf = p.dint("w3s_bf", [D, D], BF16)
        w2s_bf = p.dint("w2s_bf", [D, D], BF16)
        wqkv_bf = p.dint("wqkv_bf", [D, 3 * D], BF16)
        wo1_bf = p.dint("wo1_bf", [D, D], BF16)

    x1_fm = p.dint("x1_fm", [128, NCH * T], F32)
    q0 = p.dint("q0_fm", [128, H * T], BF16)
    k0T = p.dintT("k0", [128, H * T], BF16)
    v0T = p.dintT("v0", [128, H * T], BF16)
    qkn0T = p.dintT("qkn0", [1, 32], F32)
    gk0T = p.dintT("gk0", [512, H * T], BF16)
    gv0T = p.dintT("gv0", [512, H * T], BF16)
    gqkn0T = p.dintT("gqkn0", [4, 32], F32)
    kin0 = p.dint("kin0", [128, H * S], BF16)
    vrot0 = p.dint("vrot0", [D, S], BF16)
    x2_fm = p.dint("x2_fm", [128, NCH * T], F32)
    if cut_b:
        x3_0 = p.dout("x3_fm", [128, NCH * T], F32)
        lnx3_0 = p.dout("lnx3_fm", [128, NCH * T], BF16)
        lg0 = p.dout("logits", [1, T], F32)
        lnxtm0 = p.dout("lnx_tm", [T, D], BF16)
    else:
        x3_0 = p.dint("x3_fm0", [128, NCH * T], F32)
        lnx3_0 = p.dint("lnx3_fm0", [128, NCH * T], BF16)
        lg0T = p.dintT("lg0", [1, T], F32)
        lg0 = lg0T.ap()
        lnxtm0T = p.dintT("lnxtm0", [T, D], BF16)
        lnxtm0 = lnxtm0T.ap()
        glg0T = p.dintT("glg0", [4, T], F32)
        glnx0T = p.dintT("glnx0", [S, D], BF16)
        pa0T = p.dintT("pa0", [128, NCH * T], F32)
        pas0T = p.dintT("pas0", [128, NCH * T], F32)
        filt0 = p.dint("filt0", [128, NCH * T], BF16)
        selcomb0 = p.dint("selcomb0", [32, 384], F32)
        selrows0 = [selcomb0[:, 0:128], selcomb0[:, 128:256],
                    selcomb0[:, 256:384]]
        cbd0 = p.dint("cbd0", [128, S], F32)
        mrow0 = p.dint("mrow0", [1, 128], F32)
        pf0T = p.dintT("pf0", [128, NCH * T], F32)
        pfs0T = p.dintT("pfs0", [128, NCH * T], F32)
        xaa0 = p.dint("xaa0", [128, NCH * T], F32)
        selcomb1 = p.dint("selcomb1", [32, 384], F32)
        selrows1 = [selcomb1[:, 0:128], selcomb1[:, 128:256],
                    selcomb1[:, 256:384]]
        xout0 = p.dint("xout0", [S // 4, D], F32)
        x4_0 = p.dint("x4_fm0", [128, NCH * T], F32)
        q1 = p.dint("q1_fm", [128, H * T], BF16)
        k1T = p.dintT("k1", [128, H * T], BF16)
        v1T = p.dintT("v1", [128, H * T], BF16)
        qkn1T = p.dintT("qkn1", [1, 32], F32)
        gk1T = p.dintT("gk1", [512, H * T], BF16)
        gv1T = p.dintT("gv1", [512, H * T], BF16)
        gqkn1T = p.dintT("gqkn1", [4, 32], F32)
        kin1 = p.dint("kin1", [128, H * S], BF16)
        vrot1 = p.dint("vrot1", [D, S], BF16)
        qraw1 = p.dint("qraw1", [128, H * T], BF16)
        kraw1 = p.dint("kraw1", [128, H * T], BF16)
        x5_0 = p.dint("x5_fm", [128, NCH * T], F32)
        lnx5_0 = p.dint("lnx5_fm", [128, NCH * T], BF16)
        lg1T = p.dintT("lg1", [1, T], F32)
        lnxtm1T = p.dintT("lnxtm1", [T, D], BF16)
        glg1T = p.dintT("glg1", [4, T], F32)
        glnx1T = p.dintT("glnx1", [S, D], BF16)
        pa1T = p.dintT("pa1", [128, NCH * T], F32)
        pas1T = p.dintT("pas1", [128, NCH * T], F32)
        filt1 = p.dint("filt1", [128, NCH * T], BF16)
        selcomb2 = p.dint("selcomb2", [32, 384], F32)
        selrows2 = [selcomb2[:, 0:128], selcomb2[:, 128:256],
                    selcomb2[:, 256:384]]
        cbd1 = p.dint("cbd1", [128, S], F32)
        mrow1 = p.dint("mrow1", [1, 128], F32)
        pf1T = p.dintT("pf1", [128, NCH * T], F32)
        pfs1T = p.dintT("pfs1", [128, NCH * T], F32)
        xaa1 = p.dint("xaa1", [128, NCH * T], F32)
        selcomb3 = p.dint("selcomb3", [32, 384], F32)
        selrows3 = [selcomb3[:, 0:128], selcomb3[:, 128:256],
                    selcomb3[:, 256:384]]
        xout1 = p.dint("xout1", [S // 4, D], F32)
        x_final = p.dout("x4_fm", [128, NCH * T], F32)

    p.use_consts(["ones_bf", "ones_f", "eye_bf", "eye_f", "p64_bf", "zind_bf",
                  "bsel_bf", "sgn_f", "iotasel_f"])
    p.use_fconsts([EPS_LN, EPS_RMS, -40.0])
    p.begin()
    p.load_consts()

    def rg2(t):
        return t.ap().rearrange("(x a) b -> x (a b)", x=1)

    def rgrid(t):
        return t.ap().rearrange("a (r j) -> (a r) j", j=32)

    ioA = dict(x_fm=x_fm, wgu=wgu, wdn=wdn, wk=wk, wq=wq, wv=wv,
               wgu_bf=wgu_bf, wdn_bf=wdn_bf, wk_bf=wk_bf, wq_bf=wq_bf,
               wv_bf=wv_bf, x1_fm=x1_fm, q_fm=q0, k_fm=k0T.ap(),
               v_fm=v0T.ap(), qknorm=qkn0T.ap())
    p.push_scope("A")
    emit_L1(p, ioA)
    p.pop_scope()
    p.barrier()
    p.ccol("AllGather", OP.bypass, k0T, gk0T, "cck0")
    p.ccol("AllGather", OP.bypass, v0T, gv0T, "ccv0")
    p.ccol("AllGather", OP.bypass, qkn0T, gqkn0T, "ccq0")
    p.ccwait(["cck0", "ccv0", "ccq0"])

    ioB = dict(x1_fm=x1_fm, q_fm=q0, cbias=cbias,
               qkn=rg2(gqkn0T), wo=wo0, wr_fm=wr, wgu=wgu, wdn=wdn,
               wgu_bf=wgu_bf, wdn_bf=wdn_bf, x3_fm=x3_0, lnx3_fm=lnx3_0,
               logits=lg0, wo_bf=wo0_bf, x2_fm=x2_fm, c_diag_bf=diag)
    rotB = dict(gk=gk0T.ap()[:, :], gv=gv0T.ap()[:, :], k_loc=k0T.ap(),
                v_loc=v0T.ap(), rotq=rotq, kin=kin0, vrot=vrot0)
    p.push_scope("B")
    emit_L2(p, ioB, with_ffn=True, rot=rotB, convert_ffn=False,
            lnxtm_out=lnxtm0)
    p.pop_scope()
    if cut_b:
        p.barrier()
        p.finish()
        return p
    p.barrier()
    p.ccol("AllGather", OP.bypass, lg0T, glg0T, "cclg0")
    p.ccol("AllGather", OP.bypass, lnxtm0T, glnx0T, "cclx0")
    p.ccwait(["cclg0", "cclx0"])

    ioC = dict(lg_full=rg2(glg0T), lg_grid=rgrid(glg0T),
               lnx_tm=glnx0T.ap(), g1_fm=g1, wqs=wqs, wks=wks, wvs=wvs,
               wos=wos, maskT=maskT, cosr=cosr, sinr=sinr,
               partial=pa0T.ap(), filt_fm=filt0, wqs_bf=wqs_bf,
               wks_bf=wks_bf, wvs_bf=wvs_bf, wos_bf=wos_bf,
               selrows=selrows0, cbd=cbd0, mrow=mrow0)
    p.push_scope("C")
    emit_L3(p, ioC, convert=True)
    p.pop_scope()
    p.barrier()
    p.ccol("AllReduce", OP.add, pa0T, pas0T, "ccpa0")
    p.ccwait(["ccpa0"])

    ioD = dict(filt_fm=filt0, pa=pas0T.ap(), g2_fm=g2, w1s=w1s, w3s=w3s,
               w2s=w2s, w1s_bf=w1s_bf, w3s_bf=w3s_bf, w2s_bf=w2s_bf,
               pffn=pf0T.ap(), xaa=xaa0)
    p.push_scope("D")
    emit_L4(p, ioD, convert=True)
    p.pop_scope()
    p.barrier()
    p.ccol("AllReduce", OP.add, pf0T, pfs0T, "ccpf0")
    p.ccwait(["ccpf0"])

    ioE = dict(x3_fm=x3_0, lnx3_fm=lnx3_0, lg_full=rg2(glg0T),
               lg_grid=rgrid(glg0T), xaa=xaa0, pf=pfs0T.ap(),
               selcomb=selcomb1, sel8=sel8,
               wqkv=wqkv, gq=gq, gqr=gqr, gk=gk_, gkr=gkr, cosa=cosa,
               sina=sina, x4_fm=x4_0, q_fm=q1, k_fm=k1T.ap(), v_fm=v1T.ap(),
               qknorm=qkn1T.ap(), wqkv_bf=wqkv_bf, qraw=qraw1, kraw=kraw1,
               selrows=selrows1, xout_tm=xout0)
    p.push_scope("E")
    emit_L5(p, ioE, qkv=True)
    p.pop_scope()
    p.barrier()
    p.ccol("AllGather", OP.bypass, k1T, gk1T, "cck1")
    p.ccol("AllGather", OP.bypass, v1T, gv1T, "ccv1")
    p.ccol("AllGather", OP.bypass, qkn1T, gqkn1T, "ccq1")
    p.ccwait(["cck1", "ccv1", "ccq1"])

    ioF = dict(x1_fm=x4_0, q_fm=q1, cbias=cbias, qkn=rg2(gqkn1T), wo=wo1,
               wr_fm=wr, x3_fm=x5_0, lnx3_fm=lnx5_0, logits=lg1T.ap(),
               wo_bf=wo1_bf, c_diag_bf=diag)
    rotF = dict(gk=gk1T.ap()[:, :], gv=gv1T.ap()[:, :], k_loc=k1T.ap(),
                v_loc=v1T.ap(), rotq=rotq, kin=kin1, vrot=vrot1)
    p.push_scope("F")
    emit_L2(p, ioF, with_ffn=False, rot=rotF, lnxtm_out=lnxtm1T.ap())
    p.pop_scope()
    p.barrier()
    p.ccol("AllGather", OP.bypass, lg1T, glg1T, "cclg1")
    p.ccol("AllGather", OP.bypass, lnxtm1T, glnx1T, "cclx1")
    p.ccwait(["cclg1", "cclx1"])

    ioG = dict(lg_full=rg2(glg1T), lg_grid=rgrid(glg1T),
               lnx_tm=glnx1T.ap(), g1_fm=g1, wqs=wqs, wks=wks, wvs=wvs,
               wos=wos, maskT=maskT, cosr=cosr, sinr=sinr,
               partial=pa1T.ap(), filt_fm=filt1, wqs_bf=wqs_bf,
               wks_bf=wks_bf, wvs_bf=wvs_bf, wos_bf=wos_bf,
               selrows=selrows2, cbd=cbd1, mrow=mrow1)
    p.push_scope("G")
    emit_L3(p, ioG, convert=False)
    p.pop_scope()
    p.barrier()
    p.ccol("AllReduce", OP.add, pa1T, pas1T, "ccpa1")
    p.ccwait(["ccpa1"])

    ioH = dict(filt_fm=filt1, pa=pas1T.ap(), g2_fm=g2, w1s=w1s, w3s=w3s,
               w2s=w2s, w1s_bf=w1s_bf, w3s_bf=w3s_bf, w2s_bf=w2s_bf,
               pffn=pf1T.ap(), xaa=xaa1)
    p.push_scope("H")
    emit_L4(p, ioH, convert=False)
    p.pop_scope()
    p.barrier()
    p.ccol("AllReduce", OP.add, pf1T, pfs1T, "ccpf1")
    p.ccwait(["ccpf1"])

    ioI = dict(x3_fm=x5_0, lnx3_fm=lnx5_0, lg_full=rg2(glg1T),
               lg_grid=rgrid(glg1T), xaa=xaa1, pf=pfs1T.ap(),
               selcomb=selcomb3, sel8=sel8,
               x4_fm=x_final, selrows=selrows3, xout_tm=xout1)
    p.push_scope("I")
    emit_L5(p, ioI, qkv=False)
    p.pop_scope()
    p.barrier()

    p.finish()
    return p


def qi_of(c):
    return c % 4


def b_of(c):
    return c // 4


def mega_in_maps(upto, x, wk0, wq0, wv0, wo0, wgu, wdn, wqkv1, wo1, gq, gk,
                 maskT, cosr, sinr, wr_fm, g1_fm, g2_fm, wq1, wk1, wv1, wo1f,
                 w1, w3, w2, consts):
    ims = []
    for c in range(8):
        b, qi = b_of(c), qi_of(c)
        r = 1024 * qi
        im = dict(consts)
        im["x_fm"] = to_fm(x[b, qi * T:(qi + 1) * T])
        im["wgu"] = wgu
        im["wdn"] = wdn
        im["wk"] = wk0
        im["wq"] = wq0
        im["wv"] = wv0
        im["wo0"] = wo0
        im["wr_fm"] = wr_fm
        im["cbias"] = host_cbias(r)
        im["rotq"] = host_rotq(qi)
        im["c_diag_bf"] = CONSTS["diag_bf"]
        if upto != "B":
            cosa, sina = _rope_tables(r)
            im["g1_fm"] = g1_fm
            im["g2_fm"] = g2_fm
            im["wqs"] = np.ascontiguousarray(wq1[:, qi * 512:(qi + 1) * 512])
            im["wks"] = np.ascontiguousarray(wk1[:, qi * 512:(qi + 1) * 512])
            im["wvs"] = np.ascontiguousarray(wv1[:, qi * 512:(qi + 1) * 512])
            im["wos"] = np.ascontiguousarray(wo1f[qi * 512:(qi + 1) * 512, :])
            im["maskT"] = maskT
            im["cosr"] = cosr
            im["sinr"] = sinr
            im["w1s"] = np.ascontiguousarray(w1[:, qi * D:(qi + 1) * D])
            im["w3s"] = np.ascontiguousarray(w3[:, qi * D:(qi + 1) * D])
            im["w2s"] = np.ascontiguousarray(w2[qi * D:(qi + 1) * D, :])
            im["sel8"] = host_sel8(qi)
            im["wqkv"] = wqkv1
            im["gq"] = gq[:, None]
            im["gqr"] = np.ascontiguousarray(np.roll(gq, -64)[:, None])
            im["gk"] = gk[:, None]
            im["gkr"] = np.ascontiguousarray(np.roll(gk, -64)[:, None])
            im["cosa"] = cosa
            im["sina"] = sina
            im["wo1"] = wo1
        ims.append(im)
    return ims


def kernel(L0_attn_x, L0_attn_W_K, L0_attn_W_Q, L0_attn_W_V, L0_attn_W_O,
           L0_ffn_w_gate_up, L0_ffn_w_down, L1_attn_w_qkv, L1_attn_w_out,
           L1_attn_q_gamma, L1_attn_k_gamma, L1_ffn_mask, L1_ffn_freqs_cis,
           L1_ffn_w_router, L1_ffn_g1, L1_ffn_wq, L1_ffn_wk, L1_ffn_wv,
           L1_ffn_wo, L1_ffn_g2, L1_ffn_w1, L1_ffn_w3, L1_ffn_w2):
    x = np.ascontiguousarray(np.asarray(L0_attn_x, dtype=np.float32))
    wk0 = np.ascontiguousarray(np.asarray(L0_attn_W_K).reshape(D, D))
    wq0 = np.ascontiguousarray(np.asarray(L0_attn_W_Q).reshape(D, D))
    wv0 = np.ascontiguousarray(np.asarray(L0_attn_W_V).reshape(D, D))
    wo0 = np.ascontiguousarray(np.asarray(L0_attn_W_O))
    wgu = np.ascontiguousarray(np.asarray(L0_ffn_w_gate_up))
    wdn = np.ascontiguousarray(np.asarray(L0_ffn_w_down))
    wqkv1 = np.ascontiguousarray(np.asarray(L1_attn_w_qkv))
    wo1 = np.ascontiguousarray(np.asarray(L1_attn_w_out).T)
    gq = np.asarray(L1_attn_q_gamma, dtype=np.float32)
    gk = np.asarray(L1_attn_k_gamma, dtype=np.float32)
    maskT = np.ascontiguousarray(np.asarray(L1_ffn_mask).T)
    freqs = np.asarray(L1_ffn_freqs_cis, dtype=np.float32)
    wr_fm = np.ascontiguousarray(
        to_fm(np.asarray(L1_ffn_w_router, np.float32)[None, :]).reshape(128, 16))
    g1_fm = np.ascontiguousarray(
        to_fm(np.asarray(L1_ffn_g1, np.float32)[None, :]).reshape(128, 16))
    g2_fm = np.ascontiguousarray(
        to_fm(np.asarray(L1_ffn_g2, np.float32)[None, :]).reshape(128, 16))
    wq1 = np.ascontiguousarray(np.asarray(L1_ffn_wq))
    wk1 = np.ascontiguousarray(np.asarray(L1_ffn_wk))
    wv1 = np.ascontiguousarray(np.asarray(L1_ffn_wv))
    wo1f = np.ascontiguousarray(np.asarray(L1_ffn_wo))
    w1 = np.ascontiguousarray(np.asarray(L1_ffn_w1))
    w3 = np.ascontiguousarray(np.asarray(L1_ffn_w3))
    w2 = np.ascontiguousarray(np.asarray(L1_ffn_w2))

    cosr = np.empty((128, TOPK), np.float32)
    sinr = np.empty((128, TOPK), np.float32)
    for pp in range(128):
        cosr[pp] = freqs[:, pp % 64, 0]
        sinr[pp] = freqs[:, pp % 64, 1]

    pM = get_prog("MEGA", build_mega, "I")
    consts = pM.input_const_maps()
    ims = mega_in_maps("I", x, wk0, wq0, wv0, wo0, wgu, wdn, wqkv1, wo1, gq,
                       gk, maskT, cosr, sinr, wr_fm, g1_fm, g2_fm, wq1, wk1,
                       wv1, wo1f, w1, w3, w2, consts)
    res = _launch(pM, ims)

    out = np.empty((2, S, D), dtype=np.float32)
    for c in range(8):
        b, qi = b_of(c), qi_of(c)
        out[b, qi * T:(qi + 1) * T] = from_fm(res[c]["x4_fm"], T)
    return out


def emit_attention(p, QB, kin, vin, cbias_in, qkn_in, OB, KS, VS, EB, CBH, CBC,
                   ZS, ZSB, OA, diag, diag_qc, vtr=False, kv_rdy=None):
    ones_f = p.cs("ones_f")
    zind = p.cs("zind_bf")
    bsel = p.cs("bsel_bf")
    uq = str(p.cnt["uniq"]); p.cnt["uniq"] += 1
    QKN = p.sb("QKN" + uq, [1, 128], F32)
    CBIAS = p.sb("CBIAS" + uq, [128, 64], F32)

    @p.block.sync
    def _(e):
        p.i(e.dma_start(out=QKN[:, :], in_=qkn_in), "ca", 16)
        p.i(e.dma_start(out=CBIAS[:, :], in_=cbias_in[:, :]), "ca", 16)
    cl_att = p.c("ca")

    @p.block.vector
    def _(e):
        e.wait_ge(p.sem("ca"), cl_att)
        for q in range(1, 4):
            p.i(e.tensor_max(QKN[0:1, 0:32], QKN[0:1, 0:32],
                             QKN[0:1, q * 32:(q + 1) * 32]), "vv", 1)
            e.wait_ge(p.sem("vv"), p.c("vv"))
        p.i(e.tensor_mul(QKN[0:1, 32:48], QKN[0:1, 0:16], QKN[0:1, 16:32]),
            "vv", 1)
    vv_b = p.c("vv")

    @p.block.scalar
    def _(e):
        e.wait_ge(p.sem("vv"), vv_b)
        e.wait_ge(p.sem("cl"), p.c("cl"))
        p.i(e.activation(QKN[0:1, 32:48], QKN[0:1, 32:48], AF.Sqrt), "sc", 1)
        e.wait_ge(p.sem("sc"), p.c("sc"))
        p.i(e.activation(QKN[0:1, 48:64], QKN[0:1, 32:48], AF.Relu,
                         scale=ISQ, bias=-40.0), "sc", 1)
    sc_b = p.c("sc")

    @p.block.tensor
    def _(e):
        e.wait_ge(p.sem("sc"), sc_b)
        for h in range(16):
            mm = e.matmul(bk(p, 7, 1, off=h), ones_f[0:1, :],
                          QKN[0:1, 48 + h:49 + h], start=True, stop=True)
        p.i(mm, "mm", 1)
    mm_b = p.c("mm")

    @p.block.vector
    def _(e):
        e.wait_ge(p.sem("mm"), mm_b)
        p.i(e.tensor_copy(CBC[:, :], bk(p, 7, 16)), "vv", 1)
        e.wait_ge(p.sem("vv"), p.c("vv"))
        for h in range(16):
            p.i(e.tensor_scalar(CBH[:, h * 64:(h + 1) * 64], CBIAS[:, :],
                                CBC[:, h:h + 1], None, OP.subtract), "vv", 1)
    vv_cbh = p.c("vv")

    steps = []
    groups = []
    j = 0
    for qc in range(2):
        for hg in range(4):
            for ktg in range(8):
                groups.append(dict(qc=qc, hg=hg, ktg=ktg, first_j=j))
                for kt in range(4):
                    ktglob = ktg * 4 + kt
                    for hh in range(4):
                        steps.append(dict(
                            j=j, qc=qc, hg=hg, ktg=ktg, kt=kt, hh=hh,
                            h=hg * 4 + hh, ktglob=ktglob, g=len(groups) - 1,
                            di=diag_qc(qc, ktglob)))
                        j += 1
    nj = len(steps)
    ndiag_upto = [0] * (nj + 1)
    for s in steps:
        ndiag_upto[s["j"] + 1] = ndiag_upto[s["j"]] + (s["di"] is not None)

    def ev_after_grp(grp):
        base = (grp + 1) * 4
        if grp >= 3:
            base += 1
        if grp >= 7:
            base += 1
        return base

    base_ka = [p.c("ka0"), p.c("ka1")]
    base_ms = p.c("ms")
    base_ma = p.c("ma")
    base_sc2 = p.c("sc")
    base_ev2 = p.c("ev")
    base_vd = p.c("vd")

    @p.block.sync
    def _(e):
        for g, gr in enumerate(groups):
            par = g % 2
            if g == 0 and kv_rdy:
                for (sn, sv) in kv_rdy:
                    e.wait_ge(p.sem(sn), sv)
            if g >= 2:
                e.wait_ge(p.sem("ma"), base_ma + (g - 1) * 16)
            qc, hg, ktg = gr["qc"], gr["hg"], gr["ktg"]
            for hh in range(4):
                h = hg * 4 + hh
                p.i(e.dma_start(
                    out=KS[:, par * 2048 + hh * 512: par * 2048 + (hh + 1) * 512],
                    in_=kin[:, h * 4096 + ktg * 512: h * 4096 + (ktg + 1) * 512]),
                    "ka%d" % par, 16)
            for kt in range(4):
                dst = VS[:, par * 2048 + kt * 512: par * 2048 + (kt + 1) * 512]
                j0 = ktg * 512 + kt * 128
                if vtr:
                    p.i(e.dma_start(
                        out=dst,
                        in_=vin[hg * 512:(hg + 1) * 512, j0: j0 + 128],
                        transpose=True),
                        "ka%d" % par, 16)
                else:
                    p.i(e.dma_start(
                        out=dst,
                        in_=vin[hg, j0: j0 + 128, :]),
                        "ka%d" % par, 16)

    @p.block.tensor
    def _(e):
        for s in steps:
            j_, g = s["j"], s["g"]
            par = g % 2
            if s["kt"] == 0 and s["hh"] == 0:
                e.wait_ge(p.sem("ka%d" % par), base_ka[par] + (g // 2 + 1) * 128)
                if j_ == 0:
                    e.wait_ge(p.sem("vv"), vv_cbh)
                if s["ktg"] == 0:
                    grp = s["qc"] * 4 + s["hg"]
                    if grp >= 1:
                        e.wait_ge(p.sem("ev"), base_ev2 + ev_after_grp(grp - 1))
            sb_ = 5 + (j_ % 2)
            if j_ >= 2:
                e.wait_ge(p.sem("sc"), base_sc2 + j_ - 1)
            p.i(e.matmul(
                bk(p, sb_),
                KS[:, par * 2048 + s["hh"] * 512 + s["kt"] * 128:
                   par * 2048 + s["hh"] * 512 + s["kt"] * 128 + 128],
                QB[:, s["h"] * 1024 + s["qc"] * 512: s["h"] * 1024 + s["qc"] * 512 + 512],
                start=True, stop=True), "ms", 1)
            e.wait_ge(p.sem("sc"), base_sc2 + j_ + 1)
            if s["di"] is not None:
                e.wait_ge(p.sem("vd"), base_vd + ndiag_upto[j_ + 1])
            eb = EB[:, (j_ % 4) * 512:(j_ % 4) * 512 + 512]
            e.matmul(bk(p, s["hh"]),
                     VS[:, par * 2048 + s["kt"] * 512 + s["hh"] * 128:
                        par * 2048 + s["kt"] * 512 + s["hh"] * 128 + 128],
                     eb, start=(s["ktg"] == 0 and s["kt"] == 0),
                     stop=(s["ktg"] == 7 and s["kt"] == 3))
            p.i(e.matmul(p.psf[0:16, 4 * 512: 4 * 512 + 512],
                         zind[:, s["h"] * 16: s["h"] * 16 + 16], eb,
                         start=(j_ % 512 == 0), stop=(j_ % 512 == 511)),
                "ma", 1)

    @p.block.scalar
    def _(e):
        for s in steps:
            j_ = s["j"]
            e.wait_ge(p.sem("ms"), base_ms + j_ + 1)
            if j_ >= 4:
                e.wait_ge(p.sem("ma"), base_ma + j_ - 3)
                if ndiag_upto[j_ - 3] > 0:
                    e.wait_ge(p.sem("vd"), base_vd + ndiag_upto[j_ - 3])
            if j_ == 0:
                e.wait_ge(p.sem("vv"), vv_cbh)
            bias = CBH[:, s["h"] * 64 + s["ktglob"] * 2 + s["qc"]:
                       s["h"] * 64 + s["ktglob"] * 2 + s["qc"] + 1]
            p.i(e.activation(EB[:, (j_ % 4) * 512:(j_ % 4) * 512 + 512],
                             bk(p, 5 + (j_ % 2)), AF.Exp, bias=bias, scale=ISQ),
                "sc", 1)

    @p.block.vector
    def _(e):
        for s in steps:
            j_ = s["j"]
            if s["di"] is not None:
                e.wait_ge(p.sem("sc"), base_sc2 + j_ + 1)
                p.i(e.tensor_mul(EB[:, (j_ % 4) * 512:(j_ % 4) * 512 + 512],
                                 EB[:, (j_ % 4) * 512:(j_ % 4) * 512 + 512],
                                 diag[:, s["di"] * 512:(s["di"] + 1) * 512]),
                    "vd", 1)
            if s["ktg"] == 7 and s["kt"] == 3 and s["hh"] == 3:
                e.wait_ge(p.sem("ma"), base_ma + j_ + 1)
                for hh in range(4):
                    h = s["hg"] * 4 + hh
                    p.i(e.tensor_copy(
                        OA[:, h * 1024 + s["qc"] * 512: h * 1024 + s["qc"] * 512 + 512],
                        bk(p, hh)), "ev", 1)
                if s["hg"] == 3:
                    p.i(e.tensor_copy(ZS[:, s["qc"] * 512:(s["qc"] + 1) * 512],
                                      p.psf[0:16, 4 * 512: 4 * 512 + 512]),
                        "ev", 1)
    ev_att = p.c("ev")

    @p.block.vector
    def _(e):
        e.wait_ge(p.sem("ev"), ev_att)
        p.i(e.reciprocal(ZS[:, :], ZS[:, :]), "vv", 1)
        e.wait_ge(p.sem("vv"), p.c("vv"))
        p.i(e.tensor_copy(ZSB[:, :], ZS[:, :]), "vv", 1)
        e.wait_ge(p.sem("vv"), p.c("vv"))
    vv_rz = p.c("vv")

    @p.block.tensor
    def _(e):
        e.wait_ge(p.sem("vv"), vv_rz)
        for h in range(16):
            for qc in range(2):
                idx = h * 2 + qc
                if idx >= 2:
                    e.wait_ge(p.sem("vv"), vv_rz + idx - 1)
                p.i(e.matmul(bk(p, 5 + (idx % 2)), bsel[:, h * 128:(h + 1) * 128],
                             ZSB[:, qc * 512:(qc + 1) * 512],
                             start=True, stop=True), "mz", 1)
    base_mz = p.c("mz") - 32

    @p.block.vector
    def _(e):
        for h in range(16):
            for qc in range(2):
                idx = h * 2 + qc
                e.wait_ge(p.sem("mz"), base_mz + idx + 1)
                p.i(e.tensor_mul(
                    OB[:, h * 1024 + qc * 512: h * 1024 + qc * 512 + 512],
                    OA[:, h * 1024 + qc * 512: h * 1024 + qc * 512 + 512],
                    bk(p, 5 + (idx % 2))), "vv", 1)
    return p.c("vv")


def emit_resid_evacs(p, n_og, x_src, x_dst, XIO, scale):
    def mk(og):
        def run(mp_target):
            for dt in range(2):
                d_idx = og * 2 + dt
                b = (og % 2) * 4 + dt * 2

                @p.block.gpsimd
                def _(e):
                    e.wait_ge(p.sem("vv"), p.c("vv"))
                    p.i(e.dma_start(
                        out=XIO[:, (d_idx % 2) * 1024:(d_idx % 2) * 1024 + 1024],
                        in_=x_src[:, d_idx * T:(d_idx + 1) * T]),
                        "xg%d" % (d_idx % 2), 16)

                @p.block.vector
                def _(e):
                    e.wait_ge(p.sem("xg%d" % (d_idx % 2)), p.c("xg%d" % (d_idx % 2)))
                    e.wait_ge(p.sem("mp"), mp_target)
                    e.wait_ge(p.sem("st%d" % (d_idx % 2)), p.c("st%d" % (d_idx % 2)))
                    p.i(e.scalar_tensor_tensor(
                        XIO[:, 2048 + (d_idx % 2) * 1024: 2048 + (d_idx % 2) * 1024 + 1024],
                        p.psf[:, b * 512:(b + 2) * 512], scale,
                        XIO[:, (d_idx % 2) * 1024:(d_idx % 2) * 1024 + 1024],
                        OP.mult, OP.add), "vv", 1)
                    if dt == 1:
                        p.i(e.engine_nop(), "ev", 1)
                vv_t = p.c("vv")

                @p.block.gpsimd
                def _(e):
                    e.wait_ge(p.sem("vv"), vv_t)
                    p.i(e.dma_start(
                        out=x_dst[:, d_idx * T:(d_idx + 1) * T],
                        in_=XIO[:, 2048 + (d_idx % 2) * 1024: 2048 + (d_idx % 2) * 1024 + 1024]),
                        "st%d" % (d_idx % 2), 16)
        return run
    return [mk(g) for g in range(n_og)]


def emit_rotate_kv(p, rot, STG):
    base_rk = p.c("rk")
    ROTQ = p.sb("ROTQ", [128, 4], I32)
    kin, vrot = rot["kin"], rot["vrot"]

    @p.block.gpsimd
    def _(e):
        p.i(e.dma_start(out=ROTQ[:, :], in_=rot["rotq"][:, :]), "rk", 16)
        for h in range(16):
            p.i(e.dma_start(out=kin[:, h * 4096: h * 4096 + 1024],
                            in_=rot["k_loc"][:, h * 1024:(h + 1) * 1024]),
                "rs", 16)
        for ch in range(16):
            p.i(e.dma_start(out=vrot[ch * 128:(ch + 1) * 128, 0:1024],
                            in_=rot["v_loc"][:, ch * 1024:(ch + 1) * 1024]),
                "rs", 16)
        e.wait_ge(p.sem("rk"), base_rk + 16)
        for a in (1, 2, 3):
            for which in ("k", "v"):
                src = rot["gk"] if which == "k" else rot["gv"]
                e.wait_ge(p.sem("rs"), p.c("rs"))
                p.i(e.indirect_dma_start(
                    STG[:, :], None, src,
                    IndirectOffsetOnAxis(ROTQ[:, a:a + 1], 0),
                    bounds_check=511, oob_is_err=False), "rk", 16)
                e.wait_ge(p.sem("rk"), p.c("rk"))
                if which == "k":
                    for h in range(16):
                        p.i(e.dma_start(
                            out=kin[:, h * 4096 + a * 1024:
                                    h * 4096 + a * 1024 + 1024],
                            in_=STG[:, h * 1024:(h + 1) * 1024]), "rs", 16)
                else:
                    for ch in range(16):
                        p.i(e.dma_start(
                            out=vrot[ch * 128:(ch + 1) * 128,
                                     a * 1024:(a + 1) * 1024],
                            in_=STG[:, ch * 1024:(ch + 1) * 1024]), "rs", 16)
    return [("rs", p.c("rs"))]


def emit_L2(p, io, with_ffn=True, rot=None, convert_ffn=True, lnxtm_out=None):
    x1_in = io["x1_fm"]
    q_in = io["q_fm"]
    kin = io.get("kin")
    vin = io.get("vin")
    cbias_in = io["cbias"]
    qkn_in = io["qkn"]
    wo_in = io["wo"]
    wr_in = io["wr_fm"]
    if with_ffn:
        wgu = io["wgu"]
        wdn = io["wdn"]
        wgu_bf = io["wgu_bf"]
        wdn_bf = io["wdn_bf"]

    x3_out = io["x3_fm"]
    lnx3_out = io["lnx3_fm"]
    lg_out = io["logits"]

    wo_bf = io["wo_bf"]
    x2_dram = x3_out if not with_ffn else io["x2_fm"]

    XB = p.sb("XB", [128, 16384], BF16)
    LNB = p.sb("LNB", [128, 16384], BF16)
    HH = p.sb("HH", [128, 49152], BF16)
    SQ = p.sb("SQ", [128, 1024], BF16)
    STT = p.sb("STT", [128, 2048], F32)
    GB = p.sb("GB", [128, 1024], BF16)
    XIO = p.sb("XIO", [128, 4096], F32)
    WR = p.sb("WR", [128, 16], BF16)

    diag_in = io["c_diag_bf"]
    wcwo_base = p.c("wcwo")

    QB = HH[:, 0:16384]
    OA = HH[:, 16384:32768]
    OB = HH[:, 32768:49152]
    KS = XB[:, 0:4096]
    VS = XB[:, 4096:8192]
    EB = XB[:, 8192:10240]
    CBH = STT[:, 0:1024]
    CBC = STT[:, 1024:1040]
    DIAG = XB[:, 10240:12288]
    ZS = XIO[0:16, 0:1024]
    ZSB = XIO[0:16, 1024:1536].bitcast(BF16)
    LG = STT[0:1, 0:1024]

    @p.block.gpsimd
    def _(e):
        p.i(e.dma_start(out=wo_bf[:, :], in_=wo_in[:, :]), "wcwo", 16)
        p.i(e.dma_start(out=WR[:, :], in_=wr_in[:, :]), "wcwr", 16)
        if with_ffn and convert_ffn:
            e.wait_ge(p.sem("wcwo"), wcwo_base + 16)
            for blk in range(4):
                if blk:
                    e.wait_ge(p.sem("wcgu"), p.c("wcgu"))
                for g in range(blk * 16, blk * 16 + 16):
                    p.i(e.dma_start(out=wgu_bf[g * 256: g * 256 + 128, :],
                                    in_=wgu[g * 128: (g + 1) * 128, :]), "wcgu", 16)
                    p.i(e.dma_start(out=wgu_bf[g * 256 + 128: g * 256 + 256, :],
                                    in_=wgu[INTER + g * 128: INTER + (g + 1) * 128, :]),
                        "wcgu", 16)
            e.wait_ge(p.sem("wcgu"), p.c("wcgu"))
            for b in range(2):
                if b:
                    e.wait_ge(p.sem("wcdn"), p.c("wcdn"))
                p.i(e.dma_start(out=wdn_bf[b * 1024:(b + 1) * 1024, :],
                                in_=wdn[b * 1024:(b + 1) * 1024, :]), "wcdn", 16)

    @p.block.sync
    def _(e):
        p.i(e.dma_start(out=QB, in_=q_in[:, :]), "cq", 16)
        p.i(e.dma_start(out=DIAG, in_=diag_in[:, :]), "cq", 16)
    cl_q = p.c("cq")

    @p.block.tensor
    def _(e):
        e.wait_ge(p.sem("cq"), cl_q)

    @p.block.vector
    def _(e):
        e.wait_ge(p.sem("cq"), cl_q)

    def diag_qc(qc, ktglob):
        lo = qc * 4
        return ktglob - lo if lo <= ktglob < lo + 4 else None

    if rot is not None:
        kv_rdy = emit_rotate_kv(p, rot, OA)
        kin = rot["kin"]
        vin = rot["vrot"]
        vtr = True
    else:
        kv_rdy = None
        vtr = False
    emit_attention(p, QB, kin, vin, cbias_in, qkn_in, OB, KS, VS, EB, CBH, CBC,
                   ZS, ZSB, OA, DIAG, diag_qc, vtr=vtr, kv_rdy=kv_rdy)
    vv_att = p.c("vv")

    def wo_load(e, og, ch, par):
        e.wait_ge(p.sem("wcwo"), wcwo_base + 16)
        return e.dma_start(
            out=p.wt[:, par * 1024: par * 1024 + 256],
            in_=wo_bf[og * 256:(og + 1) * 256, ch * 128:(ch + 1) * 128],
            transpose=True)

    def wo_mms(e, og, ch, par, first, last):
        b = (og % 2) * 4
        for dt in range(2):
            wti = p.wt[:, par * 1024 + dt * 128: par * 1024 + dt * 128 + 128]
            for tc in range(2):
                mm = e.matmul(bk(p, b + dt * 2 + tc), wti,
                              OB[:, ch * T + tc * 512: ch * T + tc * 512 + 512],
                              start=first, stop=last)
        return mm

    emit_proj_chstream(p, 8, NCH, wo_load, wo_mms,
                       emit_resid_evacs(p, 8, x1_in, x2_dram, XIO, 1.0),
                       psum_lag=2, rdy=[("vv", vv_att)])
    st_x2 = (p.c("st0"), p.c("st1"))

    @p.block.sync
    def _(e):
        e.wait_ge(p.sem("st0"), st_x2[0])
        e.wait_ge(p.sem("st1"), st_x2[1])

    if with_ffn:
        vv_ln2 = emit_ln_fm(p, x2_dram, XB, LNB, SQ, STT)
        st_x3 = emit_ffn(p, x2_dram, x3_out, wgu_bf, wdn_bf, XB, LNB, HH, SQ,
                         STT, GB, XIO, vv_ln2)

        @p.block.sync
        def _(e):
            e.wait_ge(p.sem("st0"), st_x3[0])
            e.wait_ge(p.sem("st1"), st_x3[1])
        vv_ln3 = emit_ln_fm(p, x3_out, XB, LNB, SQ, STT)
    else:
        vv_ln3 = emit_ln_fm(p, x2_dram, XB, LNB, SQ, STT)

    cl_wr = p.c("wcwr")
    mr_base = p.c("mr")

    @p.block.tensor
    def _(e):
        e.wait_ge(p.sem("wcwr"), cl_wr)
        e.wait_ge(p.sem("vv"), vv_ln3)
        e.wait_ge(p.sem("ev"), p.c("ev"))
        for tc in range(2):
            for ch in range(NCH):
                mm = e.matmul(p.psf[0:1, tc * 512:(tc + 1) * 512],
                              WR[:, ch:ch + 1],
                              LNB[:, ch * T + tc * 512: ch * T + tc * 512 + 512],
                              start=(ch == 0), stop=(ch == NCH - 1))
            p.i(mm, "mr", 1)

    @p.block.vector
    def _(e):
        e.wait_ge(p.sem("mr"), mr_base + 2)
        p.i(e.tensor_copy(LG[:, :], p.psf[0:1, 0:1024]), "vv", 1)
    vv_lg = p.c("vv")

    @p.block.sync
    def _(e):
        e.wait_ge(p.sem("vv"), vv_ln3)
        p.i(e.dma_start(out=lnx3_out[:, :], in_=LNB[:, :]), "so", 16)
        e.wait_ge(p.sem("vv"), vv_lg)
        p.i(e.dma_start(out=lg_out[:, :], in_=LG[:, :]), "so", 16)
        e.wait_ge(p.sem("so"), p.c("so"))
        e.wait_ge(p.sem("st0"), p.c("st0"))
        e.wait_ge(p.sem("st1"), p.c("st1"))

    if lnxtm_out is not None:
        TMB2 = [XB[:, 0:2048], XB[:, 2048:4096]]
        eye_b = p.cs("eye_bf")
        mtx_base = p.c("mt")
        vvx_base = p.c("vv")
        xt_base = p.c("xt")
        xs_base = [p.c("xs0"), p.c("xs1")]

        @p.block.tensor
        def _(e):
            e.wait_ge(p.sem("vv"), vv_lg)
            for tt in range(8):
                for ci in range(NCH):
                    tix = tt * NCH + ci
                    if tix >= 2:
                        e.wait_ge(p.sem("vv"), vvx_base + tix - 1)
                    p.i(e.transpose(
                        bk(p, tix % 2, 64).bitcast(BF16),
                        LNB[:, ci * T + tt * 128: ci * T + (tt + 1) * 128],
                        eye_b[:, :]), "mt", 1)

        @p.block.vector
        def _(e):
            for tt in range(8):
                for ci in range(NCH):
                    tix = tt * NCH + ci
                    e.wait_ge(p.sem("mt"), mtx_base + tix + 1)
                    if ci == 0 and tt >= 2:
                        e.wait_ge(p.sem("xs%d" % (tt % 2)),
                                  xs_base[tt % 2] + 16 * (tt // 2))
                    p.i(e.tensor_copy(TMB2[tt % 2][:, ci * 128:(ci + 1) * 128],
                                      bk(p, tix % 2, 64).bitcast(BF16)),
                        "vv", 1)
                e.wait_ge(p.sem("vv"), p.c("vv"))
                p.i(e.engine_nop(), "xt", 1)

        @p.block.gpsimd
        def _(e):
            for tt in range(8):
                e.wait_ge(p.sem("xt"), xt_base + tt + 1)
                p.i(e.dma_start(out=lnxtm_out[tt * 128:(tt + 1) * 128, :],
                                in_=TMB2[tt % 2][:, :]), "xs%d" % (tt % 2), 16)

        @p.block.sync
        def _(e):
            e.wait_ge(p.sem("xs0"), p.c("xs0"))
            e.wait_ge(p.sem("xs1"), p.c("xs1"))
